# version 03
# baseline (speedup 1.0000x reference)
import numpy as np
import ml_dtypes
import concourse.bass as bass
import concourse.mybir as mybir
from concourse.bass_core_rust import IndirectOffsetOnAxis
from concourse.bass_utils import run_bass_kernel_spmd

F32 = mybir.dt.float32
BF16 = mybir.dt.bfloat16
U32 = mybir.dt.uint32
U8 = mybir.dt.uint8
AF = mybir.ActivationFunctionType
ALU = mybir.AluOpType
AX = mybir.AxisListType

NB = 2
T = 4096
D = 1024
NT = 32
K15 = 15
E = 8
HID = 4096
CAP = 1280
NCAP = CAP // 128
TRASH = E * CAP
LROWS = 128 * ((TRASH + 128 + 127) // 128)
ZPI = 2 * T
ZX1 = NB * T
GC = 0.044715
SQGC = GC ** 0.5
GS = 1.5957691216057308
MASKV = -1e30
NWE = D * HID + HID * D
WS1B = E * NWE
WS2B = WS1B + D * HID


class Sem:
    def __init__(self, nc, name):
        self.h = nc.alloc_semaphore(name)
        self.v = 0


class Prog:
    def __init__(self, nc):
        self.nc = nc
        self.q = {k: [] for k in ("sync", "vector", "scalar", "tensor", "gpsimd")}

    def op(self, engq, fn, wait=None, inc=None, incn=1):
        waits = []
        if wait:
            for w in wait:
                waits.append((w[0].h, w[1]) if isinstance(w, tuple) else (w.h, w.v))
        incs = None
        if inc is not None:
            inc.v += incn
            incs = (inc.h, incn)

        def emit(eng, fn=fn, waits=waits, incs=incs):
            for (h, v) in waits:
                if v > 0:
                    eng.wait_ge(h, v)
            ins = fn(eng)
            if incs is not None:
                ins.then_inc(incs[0], incs[1])
        self.q[engq].append(emit)

    def run(self, block):
        for name in ("sync", "vector", "scalar", "tensor", "gpsimd"):
            ops = self.q[name]

            def body(eng, ops=ops):
                for f in ops:
                    f(eng)
            getattr(block, name)(body)


def host_constants():
    c = {}
    half = D // 2
    inv_freq = 1.0 / 10000.0 ** (np.arange(half, dtype=np.float64) / half)
    ang = np.arange(T, dtype=np.float64)[:, None] * inv_freq[None, :]
    c["rope_cos"] = np.ascontiguousarray(np.cos(ang).T.astype(np.float32))
    c["rope_sin"] = np.ascontiguousarray(np.sin(ang).T.astype(np.float32))
    m = np.zeros((128, 4, 512), dtype=np.uint8)
    for r in range(4):
        for p in range(128):
            m[p, r, 128 * r + p + 1:] = 1
    c["cmask_u8"] = np.ascontiguousarray(m.reshape(128, 4 * 512))
    c["ident_bf"] = np.eye(128, dtype=ml_dtypes.bfloat16)
    c["ident_f"] = np.eye(128, dtype=np.float32)
    c["ones_row_f"] = np.ones((1, 128), dtype=np.float32)
    c["ones_col_f"] = np.ones((128, 1), dtype=np.float32)
    c["ones_row_bf"] = np.ones((1, 128), dtype=ml_dtypes.bfloat16)
    c["trilS_bf"] = (np.arange(128)[:, None] < np.arange(128)[None, :]).astype(ml_dtypes.bfloat16)
    c["ones128_bf"] = np.ones((128, 128), dtype=ml_dtypes.bfloat16)
    p = np.arange(128)
    c["e16_f"] = ((p[:, None] // 2) % 8 == np.arange(8)[None, :]).astype(np.float32)
    c["e16T_f"] = np.ascontiguousarray(c["e16_f"].T)
    c["iota16"] = np.tile(np.arange(16, dtype=np.float32), (128, K15))
    c["pcol_f"] = p.astype(np.float32).reshape(128, 1)
    c["gcol_u"] = (T + np.arange(NT)[None, :] * 128 + p[:, None]).astype(np.uint32)
    c["tokid_u"] = (np.arange(NB * NT)[None, :] * 128 + p[:, None]).astype(np.uint32)
    c["trash_u"] = (TRASH + p).astype(np.uint32).reshape(128, 1)
    c["ecap_f"] = np.tile((np.arange(8) * CAP).astype(np.float32), (128, 1))
    return c


def build_program(dbg=()):
    import os as _os
    nc = bass.Bass("TRN2", target_bir_lowering=False,
                   detect_race_conditions=_os.environ.get("SIM_RELAX") != "1")
    P = 128
    dbg = dict(dbg)
    dbg_outs = {}

    def din(name, shape, dt=F32):
        return nc.dram_tensor(name, list(shape), dt, kind="ExternalInput")

    x_in = din("x", [NB, T, D])
    wi_in = din("Wi", [D, D]); wp_in = din("Wp", [D, D]); wo_in = din("Wo2", [D, D])
    palT_in = din("palT", [256, D])
    w1e_in = din("W1full", [17, 64])
    b1a_in = din("b1a", [64, 1]); w2a_in = din("W2a", [64, 64]); b2a_in = din("b2a", [64, 1])
    wc_in = din("Wc", [64, 2]); bc_in = din("bc_row", [1, 2])
    wm_in = din("Wm", [64, 1]); bm_in = din("bm_row", [1, 1])
    gate_in = din("gate", [D, E])
    w1f_in = din("W1f", [E, D, HID]); w2f_in = din("W2f", [E, HID, D])
    b1f_in = din("b1f_cols", [128, E * 32])
    b2f_in = din("b2f_row", [1, E * D])
    ws1_in = din("Ws1", [D, HID]); ws2_in = din("Ws2", [HID, D])
    bs1_in = din("bs1_cols", [128, 32])
    bs2_in = din("bs2_row", [1, D])
    cos_in = din("rope_cos", [D // 2, T]); sin_in = din("rope_sin", [D // 2, T])
    cmask_in = din("cmask_u8", [128, 4 * 512], U8)
    identb_in = din("ident_bf", [128, 128], BF16)
    identf_in = din("ident_f", [128, 128])
    onesrf_in = din("ones_row_f", [1, 128]); onescf_in = din("ones_col_f", [128, 1])
    onesrb_in = din("ones_row_bf", [1, 128], BF16)
    ones128_in = din("ones128_bf", [128, 128], BF16)
    tril_in = din("trilS_bf", [128, 128], BF16)
    e16_in = din("e16_f", [128, 8]); e16T_in = din("e16T_f", [8, 128])
    iota16_in = din("iota16", [128, K15 * 16])
    pcol_in = din("pcol_f", [128, 1])
    gcol_in = din("gcol_u", [128, NT], U32)
    tokid_in = din("tokid_u", [128, NB * NT], U32)
    trash_in = din("trash_u", [128, 1], U32)
    ecap_in = din("ecap_f", [128, 8])

    y_out = nc.dram_tensor("out", [NB, T, D], F32, kind="ExternalOutput")

    PI = nc.dram_tensor("PI", [2 * T + 1, D], BF16)
    x1f_d = nc.dram_tensor("x1f", [NB * T, D], F32)
    x1b_d = nc.dram_tensor("x1b", [NB * T + 1, D], BF16)
    delta_d = nc.dram_tensor("deltab", [NB * NT, 128 * K15], F32)
    probsT_d = nc.dram_tensor("probsT", [E, NB * T], F32)
    ag_d = nc.dram_tensor("agout", [8 * E, NB * T], F32)
    lists_d = nc.dram_tensor("lists", [LROWS, 2], U32)
    routed_d = nc.dram_tensor("routed", [NB * T + 1, D], F32)
    wffn_d = nc.dram_tensor("wffnb", [E * NWE + 2 * D * HID], BF16)

    for name, (shape, dt) in dbg.items():
        dbg_outs[name] = nc.dram_tensor("dbg_" + name, list(shape), dt, kind="ExternalOutput")

    from contextlib import ExitStack
    with ExitStack() as ctx:
        en = ctx.enter_context

        def sbt(name, shape, dt):
            return en(nc.sbuf_tensor(name, list(shape), dt))

        wi_b = sbt("wi_b", [P, 8 * D], BF16)
        wp_b = sbt("wp_b", [P, 8 * D], BF16)
        wo_b = sbt("wo_b", [P, 8 * D], BF16)
        palT_b = sbt("palT_b", [P, 2 * D], BF16)
        gate_b = sbt("gate_b", [P, 8 * E], BF16)
        w1e_b = sbt("w1e_b", [17, 64], BF16)
        w2a_b = sbt("w2a_b", [64, 64], BF16)
        wc_b = sbt("wc_b", [64, 2], BF16)
        wm_b = sbt("wm_b", [64, 1], BF16)
        b1a_s = sbt("b1a_s", [64, 1], F32)
        b2a_s = sbt("b2a_s", [64, 1], F32)
        bc_r = sbt("bc_r", [1, 2], F32)
        bm_r = sbt("bm_r", [1, 1], F32)
        b2e_b = sbt("b2e_b", [1, D], BF16)
        bs2_b = sbt("bs2_b", [1, D], BF16)
        b1f_s = sbt("b1f_s", [P, E * 32], F32)
        bs1_s = sbt("bs1_s", [P, 32], F32)
        identb_s = sbt("identb_s", [P, 128], BF16)
        identf_s = sbt("identf_s", [P, 128], F32)
        onesrf_s = sbt("onesrf_s", [1, 128], F32)
        onescf_s = sbt("onescf_s", [P, 1], F32)
        onesrb_s = sbt("onesrb_s", [1, 128], BF16)
        ones128_s = sbt("ones128_s", [P, 128], BF16)
        tril_s = sbt("tril_s", [P, 128], BF16)
        e16_s = sbt("e16_s", [P, 8], F32)
        e16T_s = sbt("e16T_s", [8, 128], F32)
        iota16_s = sbt("iota16_s", [P, K15 * 16], F32)
        pcol_s = sbt("pcol_s", [P, 1], F32)
        gcol_s = sbt("gcol_s", [P, NT], U32)
        tokid_s = sbt("tokid_s", [P, NB * NT], U32)
        trash_s = sbt("trash_s", [P, 1], U32)
        ecap_s = sbt("ecap_s", [P, 8], F32)
        cmask_s = sbt("cmask_s", [P, 4 * 512], U8)
        zrow_u = sbt("zrow_u", [P, K15], U32)
        neg_c = sbt("neg_c", [P, 1], F32)
        cst_s = sbt("cst_s", [P, 4], F32)
        bcm_b = sbt("bcm_b", [1, 48], BF16)

        idxbuf = sbt("idxbuf", [P, NT * 16], U32)
        keepf = sbt("keepf", [P, NT * K15], F32)
        probs_all = sbt("probs_all", [P, NB * NT * 8], F32)
        relb = sbt("relb", [33, 128 * K15], BF16)
        h1b = sbt("h1b", [64, 128 * K15], BF16)
        h2b = sbt("h2b", [64, 128 * K15], BF16)
        lo8 = sbt("lo8", [8, 1], F32)
        hi8 = sbt("hi8", [8, 1], F32)
        mid8 = sbt("mid8", [8, 1], F32)
        mid8t = sbt("mid8t", [8, 1], F32)
        cnt8 = sbt("cnt8", [8, 1], F32)
        cge8 = sbt("cge8", [8, 1], U8)
        cgeN = sbt("cgeN", [8, 1], U8)
        loT_s = sbt("loT_s", [1, 8], F32)
        loB_s = sbt("loB_s", [P, 8], F32)
        baseB = sbt("baseB", [P, 8], F32)
        small = sbt("small", [P, 1280], F32)
        smallb = sbt("smallb", [P, 512], BF16)
        smallu = sbt("smallu", [P, 64], U32)
        smallu8 = sbt("smallu8", [P, 32], U8)

        arena = sbt("arena", [P, 122880], U8)
        A0 = nc.lookup_mloc(arena).addr

        def at(name, shape, dt, off):
            return nc.alloc_sbuf_tensor_at(name, list(shape), dt, offset=A0 + off)

        phat_T = at("phat_T", [P, 8 * T], BF16, 0)
        packT = at("packT", [P, 8 * T], BF16, 0)
        x1T = at("x1T", [P, 8 * T], BF16, 0)
        tokrows = at("tokrows", [P, D], BF16, 0)
        tokT = at("tokT", [P, 8 * CAP], BF16, 2048)
        ws1sl = at("ws1sl", [P, 8 * 512], BF16, 22528)
        w2sl = at("w2sl", [P, 4 * D], BF16, 30720)
        hsbE = at("hsbE", [P, 4 * 256], BF16, 38912)
        grows = at("grows", [P, D], F32, 40960)
        grows2 = at("grows2", [P, D], F32, 46080)
        gscrE = at("gscrE", [P, 256], F32, 45056)
        logits = at("logits", [P, T], F32, 65536)
        normP_bc = at("normP_bc", [P, T], F32, 81920)
        pack = at("pack", [P, 16 * D], BF16, 65536)
        AG = at("AG", [P, 4096], F32, 65536)
        AGs = at("AGs", [P, 4096], F32, 81920)
        ST = 98304
        xstage = at("xstage", [P, D], F32, ST)
        rows_f = at("rows_f", [P, D], F32, ST)
        sqs = at("sqs", [P, D], F32, ST)
        xb_st = at("xb_st", [P, D], BF16, ST + 4096)
        xT_t = at("xT_t", [P, 8 * 128], BF16, ST + 6144)
        riT = at("riT", [P, 8 * 128], F32, ST + 8192)
        riTb = at("riTb", [P, 8 * 128], BF16, ST + 12288)
        cs_st = at("cs_st", [P, 8 * 128], F32, ST + 14336)
        rows_b = at("rows_b", [P, D], BF16, ST + 18432)
        sqs2 = at("sqs2", [P, D], F32, ST + 20480)
        x0st = at("x0st", [P, D], F32, ST)
        ust = at("ust", [P, D], F32, ST + 4096)
        scrap = at("scrap", [P, D], F32, ST + 8192)
        x1bst = at("x1bst", [P, D], BF16, ST + 12288)
        vt_b = at("vt_b", [P, 8 * 128], BF16, ST + 14336)
        ct_b = at("ct_b", [P, 256], BF16, ST + 16384)
        pwk = at("pwk", [P, 512], F32, ST + 16896)
        hxw = at("hxw", [P, 3 * 240], F32, ST + 18944)
        c256 = at("c256", [P, 256], F32, ST + 21824)
        t256 = at("t256", [P, 256], F32, ST + 22848)
        bx1row = at("bx1row", [P, D], BF16, ST)
        bws1 = at("bws1", [P, 8 * 512], BF16, ST + 2048)
        bw2 = at("bw2", [P, 4 * D], BF16, ST + 10240)
        shhs = at("shhs", [P, 4 * 256], BF16, ST + 18432)
        gscrS = at("gscrS", [P, 256], F32, ST + 20480)

        psF = en(nc.psum_tensor("psF", [P, 3072], F32))
        psB = en(nc.psum_tensor("psB", [P, 1024], BF16))
        psB2 = en(nc.psum_tensor("psB2", [P, 1024], BF16))

        sLD = Sem(nc, "sLD")
        sGP = Sem(nc, "sGP")
        sTE = Sem(nc, "sTE")
        sVE = Sem(nc, "sVE")
        sSC = Sem(nc, "sSC")
        sCC = Sem(nc, "sCC")
        sST = Sem(nc, "sST")
        ALL = [sLD, sGP, sTE, sVE, sSC, sST]

        pr = Prog(nc)
        OP = pr.op

        def VGAP():
            OP("vector", lambda e: e.tensor_copy(out=small[:, 519:520], in_=cst_s[:, 0:1]))

        def BAR():
            return [(s, s.v) for s in ALL]

        def dump(name, src_ap, wait=None):
            if name in dbg_outs:
                o = dbg_outs[name]
                OP("sync", lambda e, o=o, s=src_ap: e.dma_start(out=o.ap(), in_=s),
                   wait=wait if wait is not None else BAR(), inc=sLD, incn=16)

        def ld(dst, src):
            OP("sync", lambda e, d=dst, s=src: e.dma_start(out=d, in_=s), inc=sLD, incn=16)

        def ldc(dst, src):
            OP("gpsimd", lambda e, d=dst, s=src: e.dma_start(out=d, in_=s), inc=sGP, incn=16)

        for c in range(8):
            ldc(wi_b[:, c * D:(c + 1) * D], wi_in.ap()[c * 128:(c + 1) * 128, :])
            ldc(wp_b[:, c * D:(c + 1) * D], wp_in.ap()[c * 128:(c + 1) * 128, :])
            ldc(wo_b[:, c * D:(c + 1) * D], wo_in.ap()[c * 128:(c + 1) * 128, :])
        ldc(palT_b[:, 0:D], palT_in.ap()[0:128, :])
        ldc(palT_b[:, D:2 * D], palT_in.ap()[128:256, :])
        for c in range(8):
            ldc(gate_b[:, c * E:(c + 1) * E], gate_in.ap()[c * 128:(c + 1) * 128, :])
        ldc(w1e_b[:, :], w1e_in.ap())
        ldc(w2a_b[:, :], w2a_in.ap()); ldc(wc_b[:, :], wc_in.ap()); ldc(wm_b[:, :], wm_in.ap())
        ldc(bs2_b[:, :], bs2_in.ap())
        ld(b1a_s[:, :], b1a_in.ap()); ld(b2a_s[:, :], b2a_in.ap())
        ld(bc_r[:, :], bc_in.ap()); ld(bm_r[:, :], bm_in.ap())
        ld(b1f_s[:, :], b1f_in.ap()); ld(bs1_s[:, :], bs1_in.ap())
        ld(identb_s[:, :], identb_in.ap()); ld(identf_s[:, :], identf_in.ap())
        ld(onesrf_s[:, :], onesrf_in.ap()); ld(onescf_s[:, :], onescf_in.ap())
        ld(onesrb_s[:, :], onesrb_in.ap()); ld(ones128_s[:, :], ones128_in.ap())
        ld(tril_s[:, :], tril_in.ap())
        ld(e16_s[:, :], e16_in.ap()); ld(e16T_s[:, :], e16T_in.ap())
        ld(iota16_s[:, :], iota16_in.ap()); ld(pcol_s[:, :], pcol_in.ap())
        ld(gcol_s[:, :], gcol_in.ap()); ld(tokid_s[:, :], tokid_in.ap())
        ld(trash_s[:, :], trash_in.ap()); ld(ecap_s[:, :], ecap_in.ap())
        ld(cmask_s[:, :], cmask_in.ap())
        ld0, gp0 = sLD.v, sGP.v

        OP("vector", lambda e: e.memset(zrow_u[:, :], ZPI), wait=[(sGP, gp0), (sLD, ld0)], inc=sVE)
        OP("vector", lambda e: e.memset(neg_c[:, :], MASKV), inc=sVE)
        OP("vector", lambda e: e.memset(cst_s[:, 0:1], 1e-24), inc=sVE)
        OP("vector", lambda e: e.memset(cst_s[:, 1:2], 1e-5), inc=sVE)
        OP("vector", lambda e: e.memset(cst_s[:, 2:3], SQGC), inc=sVE)
        OP("vector", lambda e: e.memset(cst_s[:, 3:4], GS), inc=sVE)
        OP("vector", lambda e: e.memset(baseB[:, :], 0.0), inc=sVE)
        OP("vector", lambda e: e.tensor_copy(out=bcm_b[0:1, 0:15], in_=bc_r[0:1, 0:1].to_broadcast([1, 15])), inc=sVE)
        OP("vector", lambda e: e.tensor_copy(out=bcm_b[0:1, 15:30], in_=bc_r[0:1, 1:2].to_broadcast([1, 15])), inc=sVE)
        OP("vector", lambda e: e.tensor_copy(out=bcm_b[0:1, 30:45], in_=bm_r[0:1, 0:1].to_broadcast([1, 15])), inc=sVE)
        OP("vector", lambda e: e.memset(small[0:1, 0:1024], 0.0), inc=sVE)
        OP("gpsimd", lambda e: e.dma_start(out=PI.ap()[ZPI:ZPI + 1, :], in_=small[0:1, 0:1024]), wait=BAR(), inc=sGP, incn=16)
        OP("gpsimd", lambda e: e.dma_start(out=x1b_d.ap()[ZX1:ZX1 + 1, :], in_=small[0:1, 0:1024]), inc=sGP, incn=16)
        LR128 = LROWS // 128
        OP("vector", lambda e: e.memset(smallu[:, 1:2], 0), inc=sVE)
        OP("vector", lambda e: e.memset(smallu[:, 0:1], ZX1), inc=sVE)
        OP("sync", lambda e: e.dma_start(
            out=lists_d.ap().rearrange("(p r) c -> p (r c)", p=128),
            in_=smallu[:, 0:2].unsqueeze(1).to_broadcast([P, LR128, 2])),
            wait=BAR(), inc=sLD, incn=16)

        cv = [at("cv0", [P, HID], BF16, 65536), at("cv1", [P, HID], BF16, 65536 + 8192)]
        conv_jobs = []
        for e_ in range(E):
            conv_jobs.append((w1f_in.ap()[e_], e_ * NWE, D, HID))
            conv_jobs.append((w2f_in.ap()[e_], e_ * NWE + D * HID, HID, D))
        conv_jobs.append((ws1_in.ap(), WS1B, D, HID))
        conv_jobs.append((ws2_in.ap(), WS2B, HID, D))
        wff = wffn_d.ap()
        bi = 0
        for (src, base, nr, ncol) in conv_jobs:
            for r0 in range(0, nr, 128):
                buf = cv[bi % 2]
                prev = sGP.v
                OP("gpsimd", lambda e, b_=buf, s=src, r0=r0, ncl=ncol: e.dma_start(
                    out=b_[0:128, 0:ncl], in_=s[r0:r0 + 128, :]),
                    wait=[(sGP, max(0, prev - 16))], inc=sGP, incn=16)
                ldone = sGP.v
                OP("gpsimd", lambda e, b_=buf, base=base, r0=r0, ncl=ncol: e.dma_start(
                    out=wff[base + r0 * ncl: base + (r0 + 128) * ncl].rearrange("(p n) -> p n", p=128),
                    in_=b_[0:128, 0:ncl]),
                    wait=[(sGP, ldone)], inc=sGP, incn=16)
                bi += 1
        gp_conv = sGP.v

        IDB = identb_s[:, :]
        IDF = identf_s[:, :]

        def proj_tile(b, t0, w_b):
            OP("sync", lambda e: e.dma_start(out=xstage[:, :], in_=x_in.ap()[b, t0:t0 + 128, :]),
               wait=BAR(), inc=sLD, incn=16)
            OP("sync", lambda e: e.dma_start(out=cs_st[:, 0:512].rearrange("p (m t) -> p m t", m=4), in_=cos_in.ap()[:, t0:t0 + 128].rearrange("(m p) t -> p m t", p=128)), inc=sLD, incn=16)
            OP("sync", lambda e: e.dma_start(out=cs_st[:, 512:1024].rearrange("p (m t) -> p m t", m=4), in_=sin_in.ap()[:, t0:t0 + 128].rearrange("(m p) t -> p m t", p=128)), inc=sLD, incn=16)
            OP("vector", lambda e: e.tensor_copy(out=xb_st[:, :], in_=xstage[:, :]), wait=[(sLD, sLD.v)], inc=sVE)
            for c in range(8):
                OP("tensor", lambda e, c=c: e.transpose(psB[:, c * 128:(c + 1) * 128], xb_st[:, c * 128:(c + 1) * 128], IDB),
                   wait=[(sVE, sVE.v)] if c == 0 else None, inc=sTE if c == 7 else None)
            for c in range(8):
                OP("vector", lambda e, c=c: e.tensor_copy(out=xT_t[:, c * 128:(c + 1) * 128], in_=psB[:, c * 128:(c + 1) * 128]),
                   wait=[(sTE, sTE.v)] if c == 0 else None, inc=sVE if c == 7 else None)
            for m in range(8):
                for c in range(8):
                    OP("tensor", lambda e, m=m, c=c, w_b=w_b: e.matmul(
                        psF[:, m * 128:(m + 1) * 128],
                        w_b[:, c * D + m * 128: c * D + (m + 1) * 128],
                        xT_t[:, c * 128:(c + 1) * 128],
                        start=(c == 0), stop=(c == 7)),
                       wait=[(sVE, sVE.v)] if (m == 0 and c == 0) else None,
                       inc=sTE if (m == 7 and c == 7) else None)
            teq = sTE.v
            for m in range(4):
                cosm = cs_st[:, m * 128:(m + 1) * 128]
                sinm = cs_st[:, 512 + m * 128: 512 + (m + 1) * 128]
                pm = psF[:, m * 128:(m + 1) * 128]
                pm4 = psF[:, (m + 4) * 128:(m + 5) * 128]
                t1 = small[:, 0:128]
                t2 = small[:, 128:256]
                OP("vector", lambda e, pm=pm, cosm=cosm, t1=t1: e.tensor_mul(t1, pm, cosm),
                   wait=[(sTE, teq)] if m == 0 else None)
                OP("vector", lambda e, pm4=pm4, sinm=sinm, t2=t2: e.tensor_mul(t2, pm4, sinm))
                OP("vector", lambda e, m=m, t1=t1, t2=t2: e.tensor_sub(riT[:, m * 128:(m + 1) * 128], t1, t2))
                OP("vector", lambda e, pm=pm, sinm=sinm, t1=t1: e.tensor_mul(t1, pm, sinm))
                OP("vector", lambda e, pm4=pm4, cosm=cosm, t2=t2: e.tensor_mul(t2, pm4, cosm))
                OP("vector", lambda e, m=m, t1=t1, t2=t2: e.tensor_add(riT[:, (m + 4) * 128:(m + 5) * 128], t1, t2),
                   inc=sVE if m == 3 else None)

        def phaseA(b):
            import os as _os
            KSUB = int(_os.environ.get("KSUB", "9"))
            for tl in range(NT):
                t0 = tl * 128
                proj_tile(b, t0, wp_b)
                for c in range(8):
                    OP("scalar", lambda e, c=c: e.activation(out=sqs[:, c * 128:(c + 1) * 128], in_=riT[:, c * 128:(c + 1) * 128], func=AF.Square),
                       wait=[(sVE, sVE.v)] if c == 0 else None, inc=sSC if c == 7 else None)
                for c in range(8):
                    OP("tensor", lambda e, c=c: e.matmul(psF[0:1, 1536:1664], onescf_s[:, :], sqs[:, c * 128:(c + 1) * 128],
                                                         start=(c == 0), stop=(c == 7)),
                       wait=[(sSC, sSC.v)] if c == 0 else None, inc=sTE if c == 7 else None)
                OP("scalar", lambda e: e.activation(out=small[0:1, 256:384], in_=psF[0:1, 1536:1664], func=AF.Sqrt, bias=cst_s[0:1, 0:1]),
                   wait=[(sTE, sTE.v)], inc=sSC)
                OP("tensor", lambda e: e.matmul(psF[:, 2560:2688], onesrf_s[0:1, :], small[0:1, 256:384], start=True, stop=True),
                   wait=[(sSC, sSC.v)], inc=sTE)
                OP("vector", lambda e, t0=t0: e.tensor_copy(out=normP_bc[:, t0:t0 + 128], in_=psF[:, 2560:2688]),
                   wait=[(sTE, sTE.v), (sGP, gp_conv)], inc=sVE)
                OP("vector", lambda e: e.reciprocal(out=small[:, 384:512], in_=psF[:, 2560:2688]), inc=sVE)
                VGAP()
                for c in range(8):
                    OP("vector", lambda e, c=c, t0=t0: e.tensor_mul(
                        phat_T[:, c * T + t0: c * T + t0 + 128], riT[:, c * 128:(c + 1) * 128], small[:, 384:512]),
                       inc=sVE if c == 7 else None)
                for c in range(8):
                    OP("tensor", lambda e, c=c, t0=t0: e.transpose(psB[:, c * 128:(c + 1) * 128],
                                                                   phat_T[:, c * T + t0: c * T + t0 + 128], IDB),
                       wait=[(sVE, sVE.v)] if c == 0 else None, inc=sTE if c == 7 else None)
                for c in range(8):
                    OP("vector", lambda e, c=c: e.tensor_copy(out=rows_b[:, c * 128:(c + 1) * 128], in_=psB[:, c * 128:(c + 1) * 128]),
                       wait=[(sTE, sTE.v)] if c == 0 else None, inc=sVE if c == 7 else None)
                OP("sync", lambda e, t0=t0: e.dma_start(out=PI.ap()[t0:t0 + 128, :], in_=rows_b[:, :]),
                   wait=[(sVE, sVE.v)], inc=sLD, incn=16)
            dump("phatT", phat_T[:, :])
            dump("normPbc", normP_bc[:, :])
            if KSUB < 2:
                return
            for qt in range(NT):
                t0 = qt * 128
                nch = qt // 4 + 1
                L = nch * 512
                proj_tile(b, t0, wi_b)
                OP("vector", lambda e: e.tensor_copy(out=riTb[:, :], in_=riT[:, :]), inc=sVE)
                for c in range(8):
                    slot = 2048 if (c % 2 == 0) else 2560
                    OP("tensor", lambda e, c=c, slot=slot: e.transpose(psF[:, slot:slot + 128], riT[:, c * 128:(c + 1) * 128], IDF),
                       wait=[(sSC, sSC.v)] if c >= 2 else ([(sVE, sVE.v)] if c == 0 else None), inc=sTE)
                    OP("scalar", lambda e, c=c, slot=slot: e.copy(out=rows_f[:, c * 128:(c + 1) * 128], in_=psF[:, slot:slot + 128]),
                       wait=[(sTE, sTE.v)], inc=sSC)
                OP("scalar", lambda e: e.activation(out=sqs2[:, :], in_=rows_f[:, :], func=AF.Square, accum_out=small[:, 512:513]),
                   inc=sSC)
                OP("scalar", lambda e: e.copy(out=small[:, 516:517], in_=cst_s[:, 0:1]), inc=sSC)
                OP("scalar", lambda e: e.activation(out=small[:, 513:514], in_=small[:, 512:513], func=AF.Sqrt, bias=cst_s[:, 0:1]), inc=sSC)
                OP("vector", lambda e: e.reciprocal(out=small[:, 514:515], in_=small[:, 513:514]), wait=[(sSC, sSC.v)], inc=sVE)
                VGAP()
                OP("vector", lambda e: e.tensor_scalar(rows_b[:, :], rows_f[:, :], small[:, 514:515], None, ALU.mult), inc=sVE)
                OP("sync", lambda e, t0=t0: e.dma_start(out=PI.ap()[T + t0: T + t0 + 128, :], in_=rows_b[:, :]),
                   wait=[(sVE, sVE.v)], inc=sLD, incn=16)
                for ch in range(nch):
                    for c in range(8):
                        OP("tensor", lambda e, ch=ch, c=c: e.matmul(
                            psF[:, 1024:1536],
                            riTb[:, c * 128:(c + 1) * 128],
                            phat_T[:, c * T + ch * 512: c * T + (ch + 1) * 512],
                            start=(c == 0), stop=(c == 7)),
                           wait=[(sVE, sVE.v)] if c == 0 else None,
                           inc=sTE if c == 7 else None)
                    OP("vector", lambda e, ch=ch: e.tensor_mul(logits[:, ch * 512:(ch + 1) * 512], psF[:, 1024:1536], normP_bc[:, ch * 512:(ch + 1) * 512]),
                       wait=[(sTE, sTE.v)], inc=sVE)
                OP("vector", lambda e, qt=qt, ch=nch - 1: e.copy_predicated(
                    logits[:, ch * 512:(ch + 1) * 512], cmask_s[:, (qt % 4) * 512:(qt % 4 + 1) * 512],
                    neg_c[:, 0:1].to_broadcast([P, 512])), inc=sVE)
                if qt == 0:
                    dump("logits0", logits[:, 0:512])
                mx1 = small[:, 520:528]; mx2 = small[:, 528:536]
                mi1 = smallu[:, 0:8]; mi2 = smallu[:, 8:16]
                micat = smallu[:, 16:31]
                OP("vector", lambda e, L=L: e.max(out=mx1, in_=logits[:, 0:L]))
                OP("vector", lambda e, t0=t0: e.tensor_scalar(small[:, 536:537], pcol_s[:, :], float(t0), None, ALU.add))
                OP("vector", lambda e, L=L, mx1=mx1, mi1=mi1: e.max_index(out=mi1, in_max=mx1, in_values=logits[:, 0:L]))
                OP("vector", lambda e, L=L, mx1=mx1: e.match_replace(out=logits[:, 0:L], in_to_replace=mx1, in_values=logits[:, 0:L], imm_value=MASKV))
                OP("vector", lambda e, L=L, mx2=mx2: e.max(out=mx2, in_=logits[:, 0:L]))
                OP("vector", lambda e, qt=qt, mx1=mx1: e.tensor_scalar(keepf[:, qt * K15: qt * K15 + 8], mx1, -1e29, None, ALU.is_gt))
                OP("vector", lambda e, L=L, mx2=mx2, mi2=mi2: e.max_index(out=mi2, in_max=mx2, in_values=logits[:, 0:L]), inc=sVE)
                OP("vector", lambda e, qt=qt, mx2=mx2: e.tensor_scalar(keepf[:, qt * K15 + 8: qt * K15 + 15], mx2[:, 0:7], -1e29, None, ALU.is_gt))
                OP("vector", lambda e, mi1=mi1, micat=micat: e.tensor_copy(out=micat[:, 0:8], in_=mi1))
                OP("vector", lambda e, mi2=mi2, micat=micat: e.tensor_copy(out=micat[:, 8:15], in_=mi2[:, 0:7]), inc=sVE)
                OP("vector", lambda e, qt=qt: e.tensor_copy(out=smallu8[:, 0:15], in_=keepf[:, qt * K15: qt * K15 + 15]), inc=sVE)
                VGAP()
                OP("vector", lambda e, qt=qt, micat=micat: e.select(idxbuf[:, qt * 16: qt * 16 + 15], smallu8[:, 0:15], micat, zrow_u[:, :]), inc=sVE)
                OP("vector", lambda e, qt=qt: e.tensor_copy(out=idxbuf[:, qt * 16 + 15: qt * 16 + 16], in_=gcol_s[:, qt:qt + 1]), inc=sVE)
                OP("vector", lambda e, micat=micat: e.tensor_copy(out=small[:, 544:559], in_=micat))
                OP("vector", lambda e: e.tensor_scalar(small[:, 560:575], small[:, 544:559], small[:, 536:537], None, ALU.subtract))
                OP("vector", lambda e: e.tensor_copy(out=small[:, 518:519], in_=cst_s[:, 0:1]))
                OP("vector", lambda e: e.tensor_scalar(small[:, 592:607], small[:, 560:575], -1.0 / T, 0.0, ALU.mult, ALU.max))
                VGAP()
                OP("vector", lambda e, qt=qt: e.tensor_mul(small[:, 576:591], small[:, 592:607], keepf[:, qt * K15: qt * K15 + K15]), inc=sVE)
                OP("sync", lambda e, b=b, qt=qt: e.dma_start(
                    out=delta_d.ap()[b * NT + qt].rearrange("(p k) -> p k", p=128), in_=small[:, 576:591]),
                    wait=[(sVE, sVE.v)], inc=sLD, incn=16)
                if qt == 0 and b == 0:
                    dump("dbg_tcol", small[:, 536:537])
                    dump("dbg_idxf", small[:, 544:559])
                    dump("dbg_d1", small[:, 560:575])
                    dump("dbg_d2", small[:, 592:607])
                    dump("dbg_d3", small[:, 576:591])
            dump("idxbuf", idxbuf[:, :])
            dump("keepf", keepf[:, :])
            if KSUB < 3:
                return
            KTILES = int(_os.environ.get("KTILES", str(NT)))
            KA = int(_os.environ.get("KASTEP", "9"))
            packT_4d = packT.ap().rearrange("p (c t w) -> p c t w", c=8, t=128, w=32)
            for qt in range(KTILES):
                t0 = qt * 128
                bar = BAR()
                for g in range(16):
                    OP("gpsimd", lambda e, g=g, qt=qt: e.indirect_dma_start(
                        out=pack[:, g * D:(g + 1) * D], out_offset=None,
                        in_=PI.ap(), in_offset=IndirectOffsetOnAxis(idxbuf[:, qt * 16 + g: qt * 16 + g + 1], 0)),
                        wait=bar if g == 0 else None, inc=sGP, incn=16)
                gath = sGP.v
                if qt == 0:
                    dump("pack0", pack[:, :], wait=[(sGP, gath)])
                if KA < 2:
                    continue
                if _os.environ.get("KB_DELTA") != "0":
                    OP("gpsimd", lambda e, b=b, qt=qt: e.dma_start(out=relb[16:17, :], in_=delta_d.ap()[b * NT + qt: b * NT + qt + 1, :]),
                       inc=sGP, incn=16)
                dl = sGP.v
                evac_done = {}
                for g in range(16):
                    pb = psB if g % 2 == 0 else psB2
                    for c in range(8):
                        w = None
                        if c == 0:
                            w = [(sGP, gath)] if g == 0 else []
                            if g >= 2:
                                w.append((sVE, evac_done[g - 2]))
                        OP("tensor", lambda e, g=g, c=c, pb=pb: e.transpose(
                            pb[:, c * 128:(c + 1) * 128], pack[:, g * D + c * 128: g * D + (c + 1) * 128], IDB),
                           wait=w if w else None, inc=sTE if c == 7 else None)
                    for c in range(8):
                        OP("vector", lambda e, g=g, c=c, pb=pb: e.tensor_copy(
                            out=packT_4d[:, c, :, g], in_=pb[:, c * 128:(c + 1) * 128]),
                           wait=[(sTE, sTE.v)] if c == 0 else None, inc=sVE)
                    evac_done[g] = sVE.v
                if KA < 3:
                    continue
                for grp in range(32):
                    slot = (grp % 8) * 64
                    for c in range(8):
                        OP("tensor", lambda e, grp=grp, c=c, slot=slot: e.matmul(
                            psF[:, slot:slot + 64],
                            packT[:, c * T + grp * 128: c * T + (grp + 1) * 128],
                            packT_4d[:, c, 4 * grp:4 * grp + 4, 0:16],
                            start=(c == 0), stop=(c == 7)),
                           wait=[(sVE, sVE.v)] if c == 0 else None,
                           inc=sTE if c == 7 else None)
                    for tau in range(4):
                        t_ = grp * 4 + tau
                        OP("vector", lambda e, tau=tau, t_=t_, slot=slot: e.tensor_scalar(
                            relb[0:16, t_ * K15:(t_ + 1) * K15],
                            psF[32 * tau:32 * tau + 16, slot + tau * 16: slot + tau * 16 + 15],
                            1.0, -1.0, ALU.min, ALU.max),
                           wait=[(sTE, sTE.v)] if tau == 0 else None,
                           inc=sVE if tau == 3 else None)
                if qt == 0:
                    dump("relb0", relb[0:33, :])
                    dump("deltad", delta_d.ap()[b * NT: b * NT + 1, :])
                if KA < 4:
                    continue
                for (n0, n1) in ((0, 512), (512, 1024), (1024, 1536), (1536, 1920)):
                    OP("tensor", lambda e, n0=n0, n1=n1: e.matmul(psF[0:64, 512 + n0:512 + n1], w1e_b[:, :], relb[0:17, n0:n1], start=True, stop=True),
                       wait=[(sVE, sVE.v), (sGP, dl)] if n0 == 0 else None,
                       inc=sTE if n1 == 1920 else None)
                OP("scalar", lambda e: e.activation(out=h1b[:, :], in_=psF[0:64, 512:2432], func=AF.Gelu, bias=b1a_s[:, :]),
                   wait=[(sTE, sTE.v)], inc=sSC)
                for (n0, n1) in ((0, 512), (512, 1024), (1024, 1536), (1536, 1920)):
                    OP("tensor", lambda e, n0=n0, n1=n1: e.matmul(psF[0:64, 512 + n0:512 + n1], w2a_b[:, :], h1b[:, n0:n1], start=True, stop=True),
                       wait=[(sSC, sSC.v)] if n0 == 0 else None,
                       inc=sTE if n1 == 1920 else None)
                OP("scalar", lambda e: e.activation(out=h2b[:, :], in_=psF[0:64, 512:2432], func=AF.Gelu, bias=b2a_s[:, :]),
                   wait=[(sTE, sTE.v)], inc=sSC)
                if qt == 0:
                    dump("h1b0", h1b[:, :])
                    dump("h2b0", h2b[:, :])
                if KA < 5:
                    continue
                OP("tensor", lambda e: e.matmul(psF[:, 2560:2605], onesrb_s[0:1, :], bcm_b[0:1, 0:45], start=True, stop=False, skip_group_check=True),
                   wait=[(sSC, sSC.v)])
                hp = h2b.ap()
                pitch_h = hp.ap[0][0]
                for k in range(K15):
                    lA = bass.AP(h2b, k, [[pitch_h, 64], [K15, 128]])
                    OP("tensor", lambda e, k=k, lA=lA: e.matmul(psF[:, 2560 + k:2561 + k], lA, wc_b[0:64, 0:1], start=False, stop=True, skip_group_check=True))
                    OP("tensor", lambda e, k=k, lA=lA: e.matmul(psF[:, 2575 + k:2576 + k], lA, wc_b[0:64, 1:2], start=False, stop=True, skip_group_check=True))
                    OP("tensor", lambda e, k=k, lA=lA: e.matmul(psF[:, 2590 + k:2591 + k], lA, wm_b[0:64, 0:1], start=False, stop=True, skip_group_check=True),
                       inc=sTE if k == K15 - 1 else None)
                OP("scalar", lambda e: e.activation(out=pwk[:, 0:30], in_=psF[:, 2560:2590], func=AF.Tanh),
                   wait=[(sTE, sTE.v)], inc=sSC)
                OP("vector", lambda e: e.tensor_copy(out=pwk[:, 32:47], in_=psF[:, 2590:2605]), wait=[(sTE, sTE.v)], inc=sVE)
                VGAP()
                OP("vector", lambda e: e.tensor_reduce(pwk[:, 48:49], pwk[:, 32:47], AX.X, ALU.max), inc=sVE)
                VGAP()
                OP("vector", lambda e: e.tensor_scalar(pwk[:, 64:79], pwk[:, 32:47], pwk[:, 48:49], None, ALU.subtract), inc=sVE)
                OP("scalar", lambda e: e.activation(out=pwk[:, 80:95], in_=pwk[:, 64:79], func=AF.Exp),
                   wait=[(sVE, sVE.v)], inc=sSC)
                OP("vector", lambda e, qt=qt: e.tensor_mul(pwk[:, 96:111], pwk[:, 80:95], keepf[:, qt * K15:(qt + 1) * K15]),
                   wait=[(sSC, sSC.v)], inc=sVE)
                VGAP()
                OP("vector", lambda e: e.tensor_reduce(pwk[:, 112:113], pwk[:, 96:111], AX.X, ALU.add), inc=sVE)
                VGAP()
                OP("vector", lambda e: e.reciprocal(out=pwk[:, 113:114], in_=pwk[:, 112:113]), inc=sVE)
                VGAP()
                OP("vector", lambda e: e.tensor_scalar(pwk[:, 128:143], pwk[:, 96:111], pwk[:, 113:114], None, ALU.mult), inc=sVE)
                if qt == 0:
                    dump("pwk0", pwk[:, 0:352])
                if KA < 6:
                    continue
                OP("vector", lambda e: e.tensor_scalar(pwk[:, 144:174], pwk[:, 0:30], 7.5, 7.5, ALU.mult, ALU.add), wait=[(sSC, sSC.v)], inc=sVE)
                VGAP()
                OP("vector", lambda e: e.tensor_scalar(pwk[:, 304:334], pwk[:, 144:174], 15.0, 0.0, ALU.min, ALU.max), inc=sVE)
                VGAP()
                OP("vector", lambda e: e.tensor_scalar(pwk[:, 176:206], pwk[:, 304:334], -0.5, None, ALU.add), inc=sVE)
                VGAP()
                OP("vector", lambda e: e.tensor_copy(out=smallu[:, 32:62], in_=pwk[:, 176:206]), inc=sVE)
                VGAP()
                OP("vector", lambda e: e.tensor_copy(out=pwk[:, 176:206], in_=smallu[:, 32:62]), inc=sVE)
                VGAP()
                OP("vector", lambda e: e.tensor_sub(pwk[:, 208:238], pwk[:, 304:334], pwk[:, 176:206]), inc=sVE)
                OP("vector", lambda e: e.tensor_scalar(pwk[:, 240:270], pwk[:, 176:206], 1.0, 15.0, ALU.add, ALU.min), inc=sVE)
                VGAP()
                OP("vector", lambda e: e.tensor_scalar(pwk[:, 272:302], pwk[:, 208:238], -1.0, 1.0, ALU.mult, ALU.add), inc=sVE)
                VGAP()
                def bc15(ap_):
                    return ap_.rearrange("p k -> p k").unsqueeze(2).to_broadcast([P, 15, 16])
                OP("vector", lambda e: e.tensor_tensor(hxw[:, 0:240].rearrange("p (k i) -> p k i", k=15), iota16_s[:, :].rearrange("p (k i) -> p k i", k=15), bc15(pwk[:, 176:191]), ALU.is_equal), inc=sVE)
                OP("vector", lambda e: e.tensor_tensor(hxw[:, 0:240].rearrange("p (k i) -> p k i", k=15), hxw[:, 0:240].rearrange("p (k i) -> p k i", k=15), bc15(pwk[:, 272:287]), ALU.mult), inc=sVE)
                OP("vector", lambda e: e.tensor_tensor(t256[:, 0:240].rearrange("p (k i) -> p k i", k=15), iota16_s[:, :].rearrange("p (k i) -> p k i", k=15), bc15(pwk[:, 240:255]), ALU.is_equal), inc=sVE)
                OP("vector", lambda e: e.tensor_tensor(t256[:, 0:240].rearrange("p (k i) -> p k i", k=15), t256[:, 0:240].rearrange("p (k i) -> p k i", k=15), bc15(pwk[:, 208:223]), ALU.mult), inc=sVE)
                VGAP()
                OP("vector", lambda e: e.tensor_add(hxw[:, 0:240], hxw[:, 0:240], t256[:, 0:240]), inc=sVE)
                OP("vector", lambda e: e.tensor_tensor(hxw[:, 240:480].rearrange("p (k i) -> p k i", k=15), iota16_s[:, :].rearrange("p (k i) -> p k i", k=15), bc15(pwk[:, 191:206]), ALU.is_equal), inc=sVE)
                OP("vector", lambda e: e.tensor_tensor(hxw[:, 240:480].rearrange("p (k i) -> p k i", k=15), hxw[:, 240:480].rearrange("p (k i) -> p k i", k=15), bc15(pwk[:, 287:302]), ALU.mult), inc=sVE)
                OP("vector", lambda e: e.tensor_tensor(t256[:, 0:240].rearrange("p (k i) -> p k i", k=15), iota16_s[:, :].rearrange("p (k i) -> p k i", k=15), bc15(pwk[:, 255:270]), ALU.is_equal), inc=sVE)
                OP("vector", lambda e: e.tensor_tensor(t256[:, 0:240].rearrange("p (k i) -> p k i", k=15), t256[:, 0:240].rearrange("p (k i) -> p k i", k=15), bc15(pwk[:, 223:238]), ALU.mult), inc=sVE)
                VGAP()
                OP("vector", lambda e: e.tensor_add(hxw[:, 240:480], hxw[:, 240:480], t256[:, 0:240]), inc=sVE)
                VGAP()
                OP("vector", lambda e: e.tensor_tensor(hxw[:, 480:720].rearrange("p (k i) -> p k i", k=15), hxw[:, 240:480].rearrange("p (k i) -> p k i", k=15), bc15(pwk[:, 128:143]), ALU.mult), inc=sVE)
                bufs = [t256[:, :], small[:, 640:896]]
                for k in range(K15):
                    ay = hxw[:, 480 + k * 16:480 + (k + 1) * 16].unsqueeze(2).to_broadcast([P, 16, 16])
                    hx = hxw[:, k * 16:(k + 1) * 16].unsqueeze(1).to_broadcast([P, 16, 16])
                    dst = c256[:, :] if k == 0 else bufs[k % 2]
                    OP("vector", lambda e, ay=ay, hx=hx, dst=dst: e.tensor_tensor(dst.rearrange("p (y x) -> p y x", y=16), ay, hx, ALU.mult))
                    if k >= 2:
                        pb2 = bufs[(k - 1) % 2]
                        OP("vector", lambda e, pb2=pb2: e.tensor_add(c256[:, :], c256[:, :], pb2))
                VGAP()
                OP("vector", lambda e: e.tensor_add(c256[:, :], c256[:, :], bufs[(K15 - 1) % 2]), inc=sVE)
                if qt == 0:
                    dump("c256_0", c256[:, :])
                if KA < 7:
                    continue
                for c2 in range(2):
                    cts = 2048 if c2 == 0 else 2688
                    OP("tensor", lambda e, c2=c2, cts=cts: e.transpose(psF[:, cts:cts + 128], c256[:, c2 * 128:(c2 + 1) * 128], IDF),
                       wait=[(sVE, sVE.v)] if c2 == 0 else None, inc=sTE)
                    OP("vector", lambda e, c2=c2, cts=cts: e.tensor_copy(out=ct_b[:, c2 * 128:(c2 + 1) * 128], in_=psF[:, cts:cts + 128]),
                       wait=[(sTE, sTE.v)], inc=sVE)
                for m in range(8):
                    for c2 in range(2):
                        OP("tensor", lambda e, m=m, c2=c2: e.matmul(
                            psF[:, m * 128:(m + 1) * 128],
                            palT_b[:, c2 * D + m * 128: c2 * D + (m + 1) * 128],
                            ct_b[:, c2 * 128:(c2 + 1) * 128],
                            start=(c2 == 0), stop=(c2 == 1)),
                           wait=[(sVE, sVE.v)] if (m == 0 and c2 == 0) else None,
                           inc=sTE if (m == 7 and c2 == 1) else None)
                for m in range(8):
                    OP("vector", lambda e, m=m: e.tensor_copy(out=vt_b[:, m * 128:(m + 1) * 128], in_=psF[:, m * 128:(m + 1) * 128]),
                       wait=[(sTE, sTE.v)] if m == 0 else None, inc=sVE if m == 7 else None)
                if qt == 0:
                    dump("vtb0", vt_b[:, :])
                for n2 in range(2):
                    for c in range(8):
                        OP("tensor", lambda e, n2=n2, c=c: e.matmul(
                            psF[:, 1024 + n2 * 512:1024 + (n2 + 1) * 512],
                            vt_b[:, c * 128:(c + 1) * 128],
                            wo_b[:, c * D + n2 * 512: c * D + (n2 + 1) * 512],
                            start=(c == 0), stop=(c == 7)),
                           wait=[(sVE, sVE.v)] if (n2 == 0 and c == 0) else None,
                           inc=sTE if (n2 == 1 and c == 7) else None)
                if KA < 8:
                    continue
                OP("sync", lambda e, b=b, t0=t0: e.dma_start(out=x0st[:, :], in_=x_in.ap()[b, t0:t0 + 128, :]),
                   wait=[(sVE, sVE.v), (sSC, sSC.v)], inc=sLD, incn=16)
                OP("vector", lambda e: e.tensor_add(ust[:, :], psF[:, 1024:2048], x0st[:, :]),
                   wait=[(sTE, sTE.v), (sLD, sLD.v)], inc=sVE)
                VGAP()
                OP("vector", lambda e: e.tensor_reduce(small[:, 600:601], ust[:, :], AX.X, ALU.add), inc=sVE)
                VGAP()
                OP("vector", lambda e: e.tensor_scalar(small[:, 601:602], small[:, 600:601], 1.0 / D, None, ALU.mult), inc=sVE)
                VGAP()
                OP("vector", lambda e: e.tensor_scalar(x0st[:, :], ust[:, :], small[:, 601:602], None, ALU.subtract), inc=sVE)
                OP("scalar", lambda e: e.activation(out=scrap[:, :], in_=x0st[:, :], func=AF.Square, accum_out=small[:, 602:603]),
                   wait=[(sVE, sVE.v)], inc=sSC)
                OP("vector", lambda e: e.tensor_scalar(small[:, 603:604], small[:, 602:603], 1.0 / D, None, ALU.mult),
                   wait=[(sSC, sSC.v)], inc=sVE)
                OP("scalar", lambda e: e.activation(out=small[:, 604:605], in_=small[:, 603:604], func=AF.Sqrt, bias=cst_s[:, 1:2]),
                   wait=[(sVE, sVE.v)], inc=sSC)
                OP("vector", lambda e: e.reciprocal(out=small[:, 605:606], in_=small[:, 604:605]), wait=[(sSC, sSC.v)], inc=sVE)
                VGAP()
                OP("vector", lambda e: e.tensor_scalar(scrap[:, :], x0st[:, :], small[:, 605:606], None, ALU.mult), inc=sVE)
                VGAP()
                OP("vector", lambda e: e.tensor_copy(out=x1bst[:, :], in_=scrap[:, :]), inc=sVE)
                if qt == 0:
                    dump("x1st0", scrap[:, :])
                OP("sync", lambda e, b=b, t0=t0: e.dma_start(out=x1f_d.ap()[b * T + t0: b * T + t0 + 128, :], in_=scrap[:, :]),
                   wait=[(sVE, sVE.v)], inc=sLD, incn=16)
                OP("sync", lambda e, b=b, t0=t0: e.dma_start(out=x1b_d.ap()[b * T + t0: b * T + t0 + 128, :], in_=x1bst[:, :]),
                   inc=sLD, incn=16)
            dump("x1_b%d" % b, None)

        def gate_batch(b):
            for tl in range(NT):
                t0 = tl * 128
                OP("sync", lambda e, b=b, t0=t0: e.dma_start(out=bx1row[:, :], in_=x1b_d.ap()[b * T + t0: b * T + t0 + 128, :]),
                   wait=BAR(), inc=sLD, incn=16)
                for c in range(8):
                    OP("tensor", lambda e, c=c: e.transpose(psB[:, c * 128:(c + 1) * 128], bx1row[:, c * 128:(c + 1) * 128], IDB),
                       wait=[(sLD, sLD.v), (sVE, sVE.v)] if c == 0 else None, inc=sTE if c == 7 else None)
                for c in range(8):
                    OP("vector", lambda e, c=c, t0=t0: e.tensor_copy(out=x1T[:, c * T + t0: c * T + t0 + 128], in_=psB[:, c * 128:(c + 1) * 128]),
                       wait=[(sTE, sTE.v)] if c == 0 else None, inc=sVE if c == 7 else None)
                for c in range(8):
                    OP("tensor", lambda e, c=c, t0=t0: e.matmul(
                        psF[:, 0:8], x1T[:, c * T + t0: c * T + t0 + 128], gate_b[:, c * E:(c + 1) * E],
                        start=(c == 0), stop=(c == 7)),
                       wait=[(sVE, sVE.v)] if c == 0 else None, inc=sTE if c == 7 else None)
                pa = (b * NT + tl) * 8
                OP("vector", lambda e: e.tensor_reduce(small[:, 0:1], psF[:, 0:8], AX.X, ALU.max), wait=[(sTE, sTE.v)], inc=sVE)
                VGAP()
                OP("vector", lambda e: e.tensor_scalar(small[:, 8:16], psF[:, 0:8], small[:, 0:1], None, ALU.subtract), inc=sVE)
                OP("scalar", lambda e: e.activation(out=small[:, 16:24], in_=small[:, 8:16], func=AF.Exp),
                   wait=[(sVE, sVE.v)], inc=sSC)
                OP("vector", lambda e: e.tensor_reduce(small[:, 1:2], small[:, 16:24], AX.X, ALU.add), wait=[(sSC, sSC.v)], inc=sVE)
                VGAP()
                OP("vector", lambda e: e.reciprocal(out=small[:, 2:3], in_=small[:, 1:2]), inc=sVE)
                VGAP()
                OP("vector", lambda e, pa=pa: e.tensor_scalar(probs_all[:, pa:pa + 8], small[:, 16:24], small[:, 2:3], None, ALU.mult), inc=sVE)
                OP("tensor", lambda e, pa=pa: e.transpose(psF[0:8, 128:256], probs_all[:, pa:pa + 8], IDF),
                   wait=[(sVE, sVE.v)], inc=sTE)
                OP("vector", lambda e: e.tensor_copy(out=small[0:8, 32:160], in_=psF[0:8, 128:256]), wait=[(sTE, sTE.v)], inc=sVE)
                OP("sync", lambda e, b=b, t0=t0: e.dma_start(out=probsT_d.ap()[:, b * T + t0: b * T + t0 + 128], in_=small[0:8, 32:160]),
                   wait=[(sVE, sVE.v)], inc=sLD, incn=16)

        def shared_batch(b):
            nsub = 2
            for tt in range(16):
                tb = tt * 256
                for sub in range(nsub):
                    for dn in range(2):
                        OP("tensor", lambda e, sub=sub, dn=dn: e.matmul(
                            psF[:, 512 + sub * 1024 + dn * 512: 512 + sub * 1024 + (dn + 1) * 512],
                            onesrb_s[0:1, :], bs2_b[0:1, dn * 512:(dn + 1) * 512],
                            start=True, stop=False, skip_group_check=True),
                           wait=BAR() if (sub == 0 and dn == 0) else None)
                for hcc in range(8):
                    prevLD = sLD.v
                    for c in range(8):
                        src = wffn_d.ap()[WS1B:WS1B + D * HID].rearrange("(r h) -> r h", h=HID)
                        OP("sync", lambda e, c=c, hcc=hcc, src=src: e.dma_start(
                            out=bws1[:, c * 512:(c + 1) * 512], in_=src[c * 128:(c + 1) * 128, hcc * 512:(hcc + 1) * 512]),
                           wait=[(sTE, sTE.v)] if (c == 0) else None, inc=sLD, incn=16)
                    src2 = wffn_d.ap()[WS2B:WS2B + HID * D].rearrange("(h d) -> h d", d=D)
                    OP("sync", lambda e, hcc=hcc, src2=src2: e.dma_start(
                        out=bw2[:, :].rearrange("p (s d) -> p s d", s=4), in_=src2[hcc * 512:(hcc + 1) * 512, :].rearrange("(s p) d -> p s d", p=128)),
                       inc=sLD, incn=16)
                    ldv = sLD.v
                    for hsub in range(4):
                        for c in range(8):
                            OP("tensor", lambda e, b=b, c=c, hsub=hsub, tb=tb: e.matmul(
                                psF[:, 0:256],
                                bws1[:, c * 512 + hsub * 128: c * 512 + hsub * 128 + 128],
                                x1T[:, c * T + tb: c * T + tb + 256],
                                start=(c == 0), stop=(c == 7)),
                               wait=([(sLD, ldv), (sVE, sVE.v)] if c == 0 else None) if hsub == 0 else ([(sVE, sVE.v)] if c == 0 else None),
                               inc=sTE if c == 7 else None)
                        blk = hcc * 4 + hsub
                        OP("vector", lambda e, blk=blk: e.tensor_scalar(small[:, 640:896], psF[:, 0:256], bs1_s[:, blk:blk + 1], None, ALU.add),
                           wait=[(sTE, sTE.v)], inc=sVE)
                        VGAP()
                        OP("scalar", lambda e: e.activation(out=gscrS[:, :], in_=small[:, 640:896], func=AF.Square, scale=cst_s[:, 2:3]),
                           wait=[(sVE, sVE.v)], inc=sSC)
                        OP("vector", lambda e: e.tensor_mul(gscrS[:, :], gscrS[:, :], small[:, 640:896]),
                           wait=[(sSC, sSC.v)], inc=sVE)
                        OP("vector", lambda e: e.tensor_add(gscrS[:, :], gscrS[:, :], small[:, 640:896]), inc=sVE)
                        OP("scalar", lambda e: e.activation(out=small[:, 0:256], in_=gscrS[:, :], func=AF.Sigmoid, scale=cst_s[:, 3:4]),
                           wait=[(sVE, sVE.v)], inc=sSC)
                        OP("vector", lambda e, hsub=hsub: e.tensor_tensor(shhs[:, hsub * 256:(hsub + 1) * 256], small[:, 0:256], small[:, 640:896], ALU.mult),
                           wait=[(sSC, sSC.v)], inc=sVE)
                        for sub in range(nsub):
                            for dn in range(2):
                                OP("tensor", lambda e, sub=sub, dn=dn, hsub=hsub: e.matmul(
                                    psF[:, 512 + sub * 1024 + dn * 512: 512 + sub * 1024 + (dn + 1) * 512],
                                    shhs[:, hsub * 256 + sub * 128: hsub * 256 + sub * 128 + 128],
                                    bw2[:, hsub * D + dn * 512: hsub * D + (dn + 1) * 512],
                                    start=False, stop=(hcc == 7 and hsub == 3), skip_group_check=True),
                                   wait=[(sVE, sVE.v)] if (sub == 0 and dn == 0) else None,
                                   inc=sTE if (hcc == 7 and hsub == 3 and sub == 1 and dn == 1) else None)
                for sub in range(nsub):
                    OP("vector", lambda e, sub=sub: e.tensor_copy(out=small[:, 256:1280], in_=psF[:, 512 + sub * 1024:512 + (sub + 1) * 1024]),
                       wait=[(sTE, sTE.v), (sLD, sLD.v)], inc=sVE)
                    OP("sync", lambda e, b=b, tb=tb, sub=sub: e.dma_start(
                        out=routed_d.ap()[b * T + tb + sub * 128: b * T + tb + sub * 128 + 128, :], in_=small[:, 256:1280]),
                       wait=[(sVE, sVE.v)], inc=sLD, incn=16)

        def bisection():
            OP("sync", lambda e: e.dma_start(out=probsT_d.ap()[0:1, 0:1], in_=probsT_d.ap()[0:1, 0:1]), inc=sLD, incn=16)
            OP("gpsimd", lambda e: e.collective_compute(
                "AllGather", ALU.bypass, replica_groups=[list(range(8))],
                ins=[probsT_d.ap()], outs=[ag_d.ap()]),
               wait=BAR(), inc=sCC)
            OP("sync", lambda e: e.dma_start(out=AG[:, :], in_=ag_d.ap().rearrange("a (h x) -> (a h) x", h=2)),
               wait=[(sCC, sCC.v)], inc=sLD, incn=16)
            OP("vector", lambda e: e.memset(lo8[:, :], 0.0), wait=[(sLD, sLD.v)], inc=sVE)
            OP("vector", lambda e: e.memset(hi8[:, :], 1.0), inc=sVE)
            OP("vector", lambda e: e.memset(mid8[:, :], 0.5), inc=sVE)
            for it in range(45):
                OP("tensor", lambda e: e.matmul(psF[:, 256:257], e16T_s[:, :], mid8[:, :], start=True, stop=True),
                   wait=[(sVE, sVE.v)], inc=sTE)
                OP("vector", lambda e: e.tensor_copy(out=small[:, 0:1], in_=psF[:, 256:257]), wait=[(sTE, sTE.v)], inc=sVE)
                VGAP()
                OP("vector", lambda e: e.tensor_scalar(AGs[:, :], AG[:, :], small[:, 0:1], 0.0, ALU.is_gt, ALU.add, accum_out=small[:, 1:2]), inc=sVE)
                VGAP()
                OP("tensor", lambda e: e.matmul(psF[0:8, 257:258], e16_s[:, :], small[:, 1:2], start=True, stop=True),
                   wait=[(sVE, sVE.v)], inc=sTE)
                OP("vector", lambda e: e.tensor_copy(out=cnt8[:, :], in_=psF[0:8, 257:258]), wait=[(sTE, sTE.v)], inc=sVE)
                VGAP()
                OP("vector", lambda e: e.tensor_scalar(cge8[:, :], cnt8[:, :], 8191.5, None, ALU.is_gt), inc=sVE)
                OP("vector", lambda e: e.tensor_scalar(cgeN[:, :], cnt8[:, :], 8191.5, None, ALU.is_le), inc=sVE)
                VGAP()
                OP("vector", lambda e: e.copy_predicated(lo8[:, :], cge8[:, :], mid8[:, :]), inc=sVE)
                OP("vector", lambda e: e.copy_predicated(hi8[:, :], cgeN[:, :], mid8[:, :]), inc=sVE)
                VGAP()
                OP("vector", lambda e: e.tensor_add(mid8t[:, :], lo8[:, :], hi8[:, :]), inc=sVE)
                VGAP()
                OP("vector", lambda e: e.tensor_scalar(mid8[:, :], mid8t[:, :], 0.5, None, ALU.mult), inc=sVE)
            OP("tensor", lambda e: e.transpose(psF[0:1, 258:266], lo8[:, :], identf_s[0:8, 0:8]),
               wait=[(sVE, sVE.v)], inc=sTE)
            OP("vector", lambda e: e.tensor_copy(out=loT_s[:, :], in_=psF[0:1, 258:266]), wait=[(sTE, sTE.v)], inc=sVE)
            OP("tensor", lambda e: e.matmul(psF[:, 272:280], onesrf_s[0:1, :], loT_s[:, :], start=True, stop=True),
               wait=[(sVE, sVE.v)], inc=sTE)
            OP("vector", lambda e: e.tensor_copy(out=loB_s[:, :], in_=psF[:, 272:280]), wait=[(sTE, sTE.v)], inc=sVE)
            VGAP()
            dump("lo8", lo8[:, :])

        def compaction():
            for tl in range(NB * NT):
                pa = tl * 8
                OP("vector", lambda e, pa=pa: e.tensor_tensor(small[:, 0:8], probs_all[:, pa:pa + 8], loB_s[:, :], ALU.is_gt),
                   wait=BAR() if tl == 0 else [(sGP, sGP.v)], inc=sVE)
                VGAP()
                OP("vector", lambda e: e.tensor_copy(out=smallb[:, 0:8], in_=small[:, 0:8]), inc=sVE)
                OP("vector", lambda e: e.tensor_copy(out=smallu8[:, 0:8], in_=small[:, 0:8]), inc=sVE)
                OP("tensor", lambda e: e.matmul(psF[:, 384:392], tril_s[:, :], smallb[:, 0:8], start=True, stop=True),
                   wait=[(sVE, sVE.v)], inc=sTE)
                OP("tensor", lambda e: e.matmul(psF[:, 392:400], ones128_s[:, :], smallb[:, 0:8], start=True, stop=True), inc=sTE)
                OP("vector", lambda e: e.tensor_add(small[:, 8:16], psF[:, 384:392], baseB[:, :]), wait=[(sTE, sTE.v)], inc=sVE)
                OP("vector", lambda e: e.tensor_add(baseB[:, :], baseB[:, :], psF[:, 392:400]), inc=sVE)
                OP("vector", lambda e: e.tensor_scalar(small[:, 16:24], small[:, 8:16], float(CAP) - 0.5, None, ALU.is_gt), inc=sVE)
                VGAP()
                OP("vector", lambda e: e.tensor_sub(small[:, 0:8], small[:, 0:8], small[:, 16:24]), inc=sVE)
                VGAP()
                OP("vector", lambda e: e.tensor_scalar(smallu8[:, 0:8], small[:, 0:8], 0.5, None, ALU.is_gt), inc=sVE)
                OP("vector", lambda e: e.tensor_add(small[:, 8:16], small[:, 8:16], ecap_s[:, :]), inc=sVE)
                VGAP()
                OP("vector", lambda e: e.tensor_copy(out=smallu[:, 0:8], in_=small[:, 8:16]), inc=sVE)
                VGAP()
                OP("vector", lambda e: e.select(smallu[:, 8:16], smallu8[:, 0:8], smallu[:, 0:8], trash_s[:, 0:1].to_broadcast([P, 8])), inc=sVE)
                OP("vector", lambda e, tl=tl: e.tensor_copy(out=smallu[:, 16:32].rearrange("p (e c) -> p e c", c=2)[:, :, 0], in_=tokid_s[:, tl:tl + 1].to_broadcast([P, 8])), inc=sVE)
                OP("vector", lambda e, pa=pa: e.tensor_copy(out=smallu[:, 16:32].rearrange("p (e c) -> p e c", c=2)[:, :, 1], in_=probs_all[:, pa:pa + 8].bitcast(U32)), inc=sVE)
                vv = sVE.v
                for e_ in range(8):
                    OP("gpsimd", lambda e, e_=e_: e.indirect_dma_start(
                        out=lists_d.ap(), out_offset=IndirectOffsetOnAxis(smallu[:, 8 + e_:9 + e_], 0),
                        in_=smallu[:, 16 + 2 * e_:18 + 2 * e_], in_offset=None),
                        wait=[(sVE, vv)] if e_ == 0 else None, inc=sGP, incn=16)

        def experts():
            scatter_prev = sGP.v
            scatter_prev2 = sGP.v
            for e_ in range(E):
                OP("gpsimd", lambda e, e_=e_: e.dma_start(out=b2e_b[0:1, :], in_=b2f_in.ap()[0:1, e_ * D:(e_ + 1) * D]),
                   wait=BAR(), inc=sGP, incn=16)
                b2ld = sGP.v
                for cc in range(NCAP):
                    OP("sync", lambda e, e_=e_, cc=cc: e.dma_start(
                        out=smallu[:, 2 * cc:2 * cc + 2], in_=lists_d.ap()[e_ * CAP + cc * 128: e_ * CAP + (cc + 1) * 128, :]),
                       wait=BAR() if cc == 0 else None, inc=sLD, incn=16)
                ldl = sLD.v
                for cc in range(NCAP):
                    pb = psB if cc % 2 == 0 else psB2
                    OP("gpsimd", lambda e, cc=cc: e.indirect_dma_start(
                        out=tokrows[:, :], out_offset=None,
                        in_=x1b_d.ap(), in_offset=IndirectOffsetOnAxis(smallu[:, 2 * cc:2 * cc + 1], 0)),
                        wait=[(sLD, ldl), (sTE, sTE.v)], inc=sGP, incn=16)
                    for c in range(8):
                        OP("tensor", lambda e, c=c, pb=pb: e.transpose(pb[:, c * 128:(c + 1) * 128], tokrows[:, c * 128:(c + 1) * 128], IDB),
                           wait=[(sGP, sGP.v), (sVE, sVE.v)] if c == 0 else None, inc=sTE if c == 7 else None)
                    for c in range(8):
                        OP("vector", lambda e, c=c, cc=cc, pb=pb: e.tensor_copy(out=tokT[:, c * CAP + cc * 128: c * CAP + (cc + 1) * 128], in_=pb[:, c * 128:(c + 1) * 128]),
                           wait=[(sTE, sTE.v)] if c == 0 else None, inc=sVE if c == 7 else None)
                if e_ == 0:
                    dump("tokt0", tokT[:, 0:CAP])
                for sc in range(5):
                    sl0 = sc * 256
                    for sub in range(2):
                        for dn in range(2):
                            OP("tensor", lambda e, e_=e_, sub=sub, dn=dn: e.matmul(
                                psF[:, 512 + sub * 1024 + dn * 512: 512 + sub * 1024 + (dn + 1) * 512],
                                onesrb_s[0:1, :], b2e_b[0:1, dn * 512:(dn + 1) * 512],
                                start=True, stop=False, skip_group_check=True),
                               wait=[(sVE, sVE.v), (sGP, b2ld)] if (sub == 0 and dn == 0) else None)
                    for hcc in range(8):
                        for c in range(8):
                            src = wffn_d.ap()[e_ * NWE: e_ * NWE + D * HID].rearrange("(r h) -> r h", h=HID)
                            OP("sync", lambda e, c=c, hcc=hcc, src=src: e.dma_start(
                                out=ws1sl[:, c * 512:(c + 1) * 512], in_=src[c * 128:(c + 1) * 128, hcc * 512:(hcc + 1) * 512]),
                               wait=[(sTE, sTE.v)] if c == 0 else None, inc=sLD, incn=16)
                        src2 = wffn_d.ap()[e_ * NWE + D * HID: e_ * NWE + NWE].rearrange("(h d) -> h d", d=D)
                        OP("sync", lambda e, hcc=hcc, src2=src2: e.dma_start(
                            out=w2sl[:, :].rearrange("p (s d) -> p s d", s=4), in_=src2[hcc * 512:(hcc + 1) * 512, :].rearrange("(s p) d -> p s d", p=128)),
                           inc=sLD, incn=16)
                        ldv = sLD.v
                        for hsub in range(4):
                            for c in range(8):
                                OP("tensor", lambda e, c=c, hsub=hsub, sl0=sl0: e.matmul(
                                    psF[:, 0:256],
                                    ws1sl[:, c * 512 + hsub * 128: c * 512 + hsub * 128 + 128],
                                    tokT[:, c * CAP + sl0: c * CAP + sl0 + 256],
                                    start=(c == 0), stop=(c == 7)),
                                   wait=([(sLD, ldv), (sVE, sVE.v)] if c == 0 else None) if hsub == 0 else ([(sVE, sVE.v)] if c == 0 else None),
                                   inc=sTE if c == 7 else None)
                            blk = e_ * 32 + hcc * 4 + hsub
                            OP("vector", lambda e, blk=blk: e.tensor_scalar(small[:, 640:896], psF[:, 0:256], b1f_s[:, blk:blk + 1], None, ALU.add),
                               wait=[(sTE, sTE.v)], inc=sVE)
                            VGAP()
                            OP("scalar", lambda e: e.activation(out=gscrE[:, :], in_=small[:, 640:896], func=AF.Square, scale=cst_s[:, 2:3]),
                               wait=[(sVE, sVE.v)], inc=sSC)
                            OP("vector", lambda e: e.tensor_mul(gscrE[:, :], gscrE[:, :], small[:, 640:896]), wait=[(sSC, sSC.v)], inc=sVE)
                            OP("vector", lambda e: e.tensor_add(gscrE[:, :], gscrE[:, :], small[:, 640:896]), inc=sVE)
                            OP("scalar", lambda e: e.activation(out=small[:, 0:256], in_=gscrE[:, :], func=AF.Sigmoid, scale=cst_s[:, 3:4]),
                               wait=[(sVE, sVE.v)], inc=sSC)
                            OP("vector", lambda e, hsub=hsub: e.tensor_tensor(hsbE[:, hsub * 256:(hsub + 1) * 256], small[:, 0:256], small[:, 640:896], ALU.mult),
                               wait=[(sSC, sSC.v)], inc=sVE)
                            for sub in range(2):
                                for dn in range(2):
                                    OP("tensor", lambda e, sub=sub, dn=dn, hsub=hsub: e.matmul(
                                        psF[:, 512 + sub * 1024 + dn * 512: 512 + sub * 1024 + (dn + 1) * 512],
                                        hsbE[:, hsub * 256 + sub * 128: hsub * 256 + sub * 128 + 128],
                                        w2sl[:, hsub * D + dn * 512: hsub * D + (dn + 1) * 512],
                                        start=False, stop=(hcc == 7 and hsub == 3), skip_group_check=True),
                                       wait=[(sVE, sVE.v)] if (sub == 0 and dn == 0) else None,
                                       inc=sTE if (hcc == 7 and hsub == 3 and sub == 1 and dn == 1) else None)
                    if e_ == 0 and sc == 0:
                        dump("hsb0", hsbE[:, :])
                    for sub in range(2):
                        call = sc * 2 + sub
                        gbuf = grows if sub == 0 else grows2
                        OP("vector", lambda e, sub=sub, call=call, gbuf=gbuf: e.tensor_scalar(
                            gbuf[:, :], psF[:, 512 + sub * 1024: 512 + (sub + 1) * 1024],
                            smallu[:, 2 * call + 1: 2 * call + 2].bitcast(F32), None, ALU.mult),
                           wait=[(sTE, sTE.v), (sGP, scatter_prev2)] if sub == 0 else [(sGP, scatter_prev2)], inc=sVE)
                        OP("gpsimd", lambda e, call=call, gbuf=gbuf: e.indirect_dma_start(
                            out=routed_d.ap(), out_offset=IndirectOffsetOnAxis(smallu[:, 2 * call:2 * call + 1], 0),
                            in_=gbuf[:, :], in_offset=None, compute_op=ALU.add),
                            wait=[(sVE, sVE.v), (sGP, scatter_prev)], inc=sGP, incn=16)
                    scatter_prev2 = sGP.v
                scatter_prev = sGP.v

        def final():
            dump("ws1d", wffn_d.ap()[WS1B:WS1B + 128 * HID].rearrange("(p n) -> p n", p=128))
            dump("ws2d", wffn_d.ap()[WS2B:WS2B + 128 * D].rearrange("(p n) -> p n", p=128))
            dump("w1e0", wffn_d.ap()[0:128 * HID].rearrange("(p n) -> p n", p=128))
            dump("probsT0", probsT_d.ap()[:, :])
            dump("agd", ag_d.ap()[:, :])
            dump("loB", loB_s[:, :])
            dump("lists0", lists_d.ap()[0:10240, :])
            dump("x1ball", x1b_d.ap()[0:8192, :])
            dump("routed0", routed_d.ap()[0:256, :])
            for tl in range(NB * NT):
                t0 = tl * 128
                OP("sync", lambda e, t0=t0: e.dma_start(out=ust[:, :], in_=routed_d.ap()[t0:t0 + 128, :]),
                   wait=BAR() if tl == 0 else [(sVE, sVE.v)], inc=sLD, incn=16)
                OP("sync", lambda e, t0=t0: e.dma_start(out=x0st[:, :], in_=x1f_d.ap()[t0:t0 + 128, :]), inc=sLD, incn=16)
                OP("vector", lambda e: e.tensor_add(scrap[:, :], ust[:, :], x0st[:, :]), wait=[(sLD, sLD.v)], inc=sVE)
                VGAP()
                OP("vector", lambda e: e.tensor_reduce(small[:, 600:601], scrap[:, :], AX.X, ALU.add), inc=sVE)
                VGAP()
                OP("vector", lambda e: e.tensor_scalar(small[:, 601:602], small[:, 600:601], 1.0 / D, None, ALU.mult), inc=sVE)
                VGAP()
                OP("vector", lambda e: e.tensor_scalar(x0st[:, :], scrap[:, :], small[:, 601:602], None, ALU.subtract), inc=sVE)
                OP("scalar", lambda e: e.activation(out=scrap[:, :], in_=x0st[:, :], func=AF.Square, accum_out=small[:, 602:603]),
                   wait=[(sVE, sVE.v)], inc=sSC)
                OP("vector", lambda e: e.tensor_scalar(small[:, 603:604], small[:, 602:603], 1.0 / D, None, ALU.mult),
                   wait=[(sSC, sSC.v)], inc=sVE)
                OP("scalar", lambda e: e.activation(out=small[:, 604:605], in_=small[:, 603:604], func=AF.Sqrt, bias=cst_s[:, 1:2]),
                   wait=[(sVE, sVE.v)], inc=sSC)
                OP("vector", lambda e: e.reciprocal(out=small[:, 605:606], in_=small[:, 604:605]), wait=[(sSC, sSC.v)], inc=sVE)
                VGAP()
                OP("vector", lambda e: e.tensor_scalar(scrap[:, :], x0st[:, :], small[:, 605:606], None, ALU.mult), inc=sVE)
                VGAP()
                OP("sync", lambda e, tl=tl, t0=t0: e.dma_start(
                    out=y_out.ap()[tl // NT, (t0 % T):(t0 % T) + 128, :], in_=scrap[:, :]),
                   wait=[(sVE, sVE.v)], inc=sST, incn=16)
            OP("sync", lambda e: e.dma_start(out=probsT_d.ap()[0:1, 0:1], in_=probsT_d.ap()[0:1, 0:1]),
               wait=[(sST, sST.v)], inc=sLD, incn=16)

        import os as _os
        KS = int(_os.environ.get("KSTAGE", "99"))
        if KS >= 2:
            phaseA(0)
        if KS >= 3:
            gate_batch(0)
            shared_batch(0)
        if KS >= 4:
            phaseA(1)
            gate_batch(1)
        if KS >= 5:
            bisection()
            shared_batch(1)
        if KS >= 6:
            compaction()
        if KS >= 7:
            experts()
        if KS >= 8:
            final()

        pr.run(en(nc.Block()))

    return nc, dbg_outs


def stage_inputs(inputs, core):
    c = host_constants()
    m = {}
    m.update(c)
    x = inputs["L0_attn_x"]
    m["x"] = np.ascontiguousarray(np.asarray(x)[2 * core:2 * core + 2]).astype(np.float32, copy=False)
    m["Wi"] = np.asarray(inputs["L0_attn_Wi"], dtype=np.float32)
    m["Wp"] = np.asarray(inputs["L0_attn_Wp"], dtype=np.float32)
    m["Wo2"] = np.ascontiguousarray(np.asarray(inputs["L0_attn_Wo"], dtype=np.float32)[0])
    pal = np.asarray(inputs["L0_attn_palette"], dtype=np.float32)
    m["palT"] = np.ascontiguousarray(pal.transpose(1, 2, 0).reshape(256, D))
    w1 = np.asarray(inputs["L0_attn_W1"], dtype=np.float32)
    m["W1full"] = np.ascontiguousarray(w1)
    m["b1a"] = np.ascontiguousarray(np.asarray(inputs["L0_attn_b1"], dtype=np.float32).reshape(64, 1))
    m["W2a"] = np.asarray(inputs["L0_attn_W2"], dtype=np.float32)
    m["b2a"] = np.ascontiguousarray(np.asarray(inputs["L0_attn_b2"], dtype=np.float32).reshape(64, 1))
    m["Wc"] = np.asarray(inputs["L0_attn_Wc"], dtype=np.float32)
    m["bc_row"] = np.ascontiguousarray(np.asarray(inputs["L0_attn_bc"], dtype=np.float32).reshape(1, 2))
    m["Wm"] = np.asarray(inputs["L0_attn_Wm"], dtype=np.float32)
    m["bm_row"] = np.ascontiguousarray(np.asarray(inputs["L0_attn_bm"], dtype=np.float32).reshape(1, 1))
    m["gate"] = np.asarray(inputs["L0_ffn_gate"], dtype=np.float32)
    m["W1f"] = np.asarray(inputs["L0_ffn_W1"], dtype=np.float32)
    m["W2f"] = np.asarray(inputs["L0_ffn_W2"], dtype=np.float32)
    b1f = np.asarray(inputs["L0_ffn_b1"], dtype=np.float32)
    m["b1f_cols"] = np.ascontiguousarray(b1f.reshape(E, 32, 128).transpose(2, 0, 1).reshape(128, E * 32))
    m["b2f_row"] = np.ascontiguousarray(np.asarray(inputs["L0_ffn_b2"], dtype=np.float32).reshape(1, E * D))
    m["Ws1"] = np.asarray(inputs["L0_ffn_Ws1"], dtype=np.float32)
    m["Ws2"] = np.asarray(inputs["L0_ffn_Ws2"], dtype=np.float32)
    bs1 = np.asarray(inputs["L0_ffn_bs1"], dtype=np.float32)
    m["bs1_cols"] = np.ascontiguousarray(bs1.reshape(32, 128).T)
    m["bs2_row"] = np.ascontiguousarray(np.asarray(inputs["L0_ffn_bs2"], dtype=np.float32).reshape(1, D))
    return m


def kernel(**inputs):
    nc, _ = build_program()
    in_maps = [stage_inputs(inputs, c) for c in range(8)]
    res = run_bass_kernel_spmd(nc, in_maps, core_ids=list(range(8)))
    out = np.concatenate([res.results[c]["out"] for c in range(8)], axis=0)
    return out.astype(np.float32)


# version 04
# speedup vs baseline: 1.1383x; 1.1383x over previous
import numpy as np
import ml_dtypes
import concourse.bass as bass
import concourse.mybir as mybir
from concourse.bass_core_rust import IndirectOffsetOnAxis
from concourse.bass_utils import run_bass_kernel_spmd

F32 = mybir.dt.float32
BF16 = mybir.dt.bfloat16
U32 = mybir.dt.uint32
U8 = mybir.dt.uint8
AF = mybir.ActivationFunctionType
ALU = mybir.AluOpType
AX = mybir.AxisListType

NB = 2
T = 4096
D = 1024
NT = 32
K15 = 15
E = 8
HID = 4096
CAP = 1280
NCAP = CAP // 128
TRASH = E * CAP
LROWS = 128 * ((TRASH + 128 + 127) // 128)
ZPI = 2 * T
ZX1 = NB * T
GC = 0.044715
SQGC = GC ** 0.5
GS = 1.5957691216057308
MASKV = -1e30
NWE = D * HID + HID * D
WS1B = E * NWE
WS2B = WS1B + D * HID


class Sem:
    def __init__(self, nc, name):
        self.h = nc.alloc_semaphore(name)
        self.v = 0


class Prog:
    def __init__(self, nc):
        self.nc = nc
        self.q = {k: [] for k in ("sync", "vector", "scalar", "tensor", "gpsimd")}

    def op(self, engq, fn, wait=None, inc=None, incn=1):
        waits = []
        if wait:
            for w in wait:
                waits.append((w[0].h, w[1]) if isinstance(w, tuple) else (w.h, w.v))
        incs = None
        if inc is not None:
            inc.v += incn
            incs = (inc.h, incn)

        def emit(eng, fn=fn, waits=waits, incs=incs):
            for (h, v) in waits:
                if v > 0:
                    eng.wait_ge(h, v)
            ins = fn(eng)
            if incs is not None:
                ins.then_inc(incs[0], incs[1])
        self.q[engq].append(emit)

    def run(self, block):
        for name in ("sync", "vector", "scalar", "tensor", "gpsimd"):
            ops = self.q[name]

            def body(eng, ops=ops):
                for f in ops:
                    f(eng)
            getattr(block, name)(body)


def host_constants():
    c = {}
    half = D // 2
    inv_freq = 1.0 / 10000.0 ** (np.arange(half, dtype=np.float64) / half)
    ang = np.arange(T, dtype=np.float64)[:, None] * inv_freq[None, :]
    c["rope_cos"] = np.ascontiguousarray(np.cos(ang).T.astype(np.float32))
    c["rope_sin"] = np.ascontiguousarray(np.sin(ang).T.astype(np.float32))
    m = np.zeros((128, 4, 512), dtype=np.uint8)
    for r in range(4):
        for p in range(128):
            m[p, r, 128 * r + p + 1:] = 1
    c["cmask_u8"] = np.ascontiguousarray(m.reshape(128, 4 * 512))
    c["ident_bf"] = np.eye(128, dtype=ml_dtypes.bfloat16)
    c["ident_f"] = np.eye(128, dtype=np.float32)
    c["ones_row_f"] = np.ones((1, 128), dtype=np.float32)
    c["ones_col_f"] = np.ones((128, 1), dtype=np.float32)
    c["ones_row_bf"] = np.ones((1, 128), dtype=ml_dtypes.bfloat16)
    c["trilS_bf"] = (np.arange(128)[:, None] < np.arange(128)[None, :]).astype(ml_dtypes.bfloat16)
    c["ones128_bf"] = np.ones((128, 128), dtype=ml_dtypes.bfloat16)
    p = np.arange(128)
    c["e16_f"] = ((p[:, None] // 2) % 8 == np.arange(8)[None, :]).astype(np.float32)
    c["e16T_f"] = np.ascontiguousarray(c["e16_f"].T)
    c["iota16"] = np.tile(np.arange(16, dtype=np.float32), (128, K15))
    c["pcol_f"] = p.astype(np.float32).reshape(128, 1)
    c["gcol_u"] = (T + np.arange(NT)[None, :] * 128 + p[:, None]).astype(np.uint32)
    c["tokid_u"] = (np.arange(NB * NT)[None, :] * 128 + p[:, None]).astype(np.uint32)
    c["trash_u"] = (TRASH + p).astype(np.uint32).reshape(128, 1)
    c["ecap_f"] = np.tile((np.arange(8) * CAP).astype(np.float32), (128, 1))
    return c


def build_program(dbg=()):
    import os as _os
    nc = bass.Bass("TRN2", target_bir_lowering=False,
                   detect_race_conditions=_os.environ.get("SIM_RELAX") != "1")
    P = 128
    dbg = dict(dbg)
    dbg_outs = {}

    def din(name, shape, dt=F32):
        return nc.dram_tensor(name, list(shape), dt, kind="ExternalInput")

    x_in = din("x", [NB, T, D])
    wi_in = din("Wi", [D, D]); wp_in = din("Wp", [D, D]); wo_in = din("Wo2", [D, D])
    palT_in = din("palT", [256, D])
    w1e_in = din("W1full", [17, 64])
    b1a_in = din("b1a", [64, 1]); w2a_in = din("W2a", [64, 64]); b2a_in = din("b2a", [64, 1])
    wc_in = din("Wc", [64, 2]); bc_in = din("bc_row", [1, 2])
    wm_in = din("Wm", [64, 1]); bm_in = din("bm_row", [1, 1])
    gate_in = din("gate", [D, E])
    w1f_in = din("W1f", [E, D, HID]); w2f_in = din("W2f", [E, HID, D])
    b1f_in = din("b1f_cols", [128, E * 32])
    b2f_in = din("b2f_row", [1, E * D])
    ws1_in = din("Ws1", [D, HID]); ws2_in = din("Ws2", [HID, D])
    bs1_in = din("bs1_cols", [128, 32])
    bs2_in = din("bs2_row", [1, D])
    cos_in = din("rope_cos", [D // 2, T]); sin_in = din("rope_sin", [D // 2, T])
    cmask_in = din("cmask_u8", [128, 4 * 512], U8)
    identb_in = din("ident_bf", [128, 128], BF16)
    identf_in = din("ident_f", [128, 128])
    onesrf_in = din("ones_row_f", [1, 128]); onescf_in = din("ones_col_f", [128, 1])
    onesrb_in = din("ones_row_bf", [1, 128], BF16)
    ones128_in = din("ones128_bf", [128, 128], BF16)
    tril_in = din("trilS_bf", [128, 128], BF16)
    e16_in = din("e16_f", [128, 8]); e16T_in = din("e16T_f", [8, 128])
    iota16_in = din("iota16", [128, K15 * 16])
    pcol_in = din("pcol_f", [128, 1])
    gcol_in = din("gcol_u", [128, NT], U32)
    tokid_in = din("tokid_u", [128, NB * NT], U32)
    trash_in = din("trash_u", [128, 1], U32)
    ecap_in = din("ecap_f", [128, 8])

    y_out = nc.dram_tensor("out", [NB, T, D], F32, kind="ExternalOutput")

    PI = nc.dram_tensor("PI", [2 * T + 1, D], BF16)
    x1f_d = nc.dram_tensor("x1f", [NB * T, D], F32)
    x1b_d = nc.dram_tensor("x1b", [NB * T + 1, D], BF16)
    delta_d = nc.dram_tensor("deltab", [NB * NT, 128 * K15], F32)
    probsT_d = nc.dram_tensor("probsT", [E, NB * T], F32)
    ag_d = nc.dram_tensor("agout", [8 * E, NB * T], F32)
    lists_d = nc.dram_tensor("lists", [LROWS, 2], U32)
    routed_d = nc.dram_tensor("routed", [NB * T + 1, D], F32)
    wffn_d = nc.dram_tensor("wffnb", [E * NWE + 2 * D * HID], BF16)

    for name, (shape, dt) in dbg.items():
        dbg_outs[name] = nc.dram_tensor("dbg_" + name, list(shape), dt, kind="ExternalOutput")

    from contextlib import ExitStack
    with ExitStack() as ctx:
        en = ctx.enter_context

        def sbt(name, shape, dt):
            return en(nc.sbuf_tensor(name, list(shape), dt))

        wi_b = sbt("wi_b", [P, 8 * D], BF16)
        wp_b = sbt("wp_b", [P, 8 * D], BF16)
        wo_b = sbt("wo_b", [P, 8 * D], BF16)
        palT_b = sbt("palT_b", [P, 2 * D], BF16)
        gate_b = sbt("gate_b", [P, 8 * E], BF16)
        w1e_b = sbt("w1e_b", [17, 64], BF16)
        w2a_b = sbt("w2a_b", [64, 64], BF16)
        wc_b = sbt("wc_b", [64, 2], BF16)
        wm_b = sbt("wm_b", [64, 1], BF16)
        b1a_s = sbt("b1a_s", [64, 1], F32)
        b2a_s = sbt("b2a_s", [64, 1], F32)
        bc_r = sbt("bc_r", [1, 2], F32)
        bm_r = sbt("bm_r", [1, 1], F32)
        b2e_b = sbt("b2e_b", [1, D], BF16)
        bs2_b = sbt("bs2_b", [1, D], BF16)
        b1f_s = sbt("b1f_s", [P, E * 32], F32)
        bs1_s = sbt("bs1_s", [P, 32], F32)
        identb_s = sbt("identb_s", [P, 128], BF16)
        identf_s = sbt("identf_s", [P, 128], F32)
        onesrf_s = sbt("onesrf_s", [1, 128], F32)
        onescf_s = sbt("onescf_s", [P, 1], F32)
        onesrb_s = sbt("onesrb_s", [1, 128], BF16)
        ones128_s = sbt("ones128_s", [P, 128], BF16)
        tril_s = sbt("tril_s", [P, 128], BF16)
        e16_s = sbt("e16_s", [P, 8], F32)
        e16T_s = sbt("e16T_s", [8, 128], F32)
        iota16_s = sbt("iota16_s", [P, K15 * 16], F32)
        pcol_s = sbt("pcol_s", [P, 1], F32)
        gcol_s = sbt("gcol_s", [P, NT], U32)
        tokid_s = sbt("tokid_s", [P, NB * NT], U32)
        trash_s = sbt("trash_s", [P, 1], U32)
        ecap_s = sbt("ecap_s", [P, 8], F32)
        cmask_s = sbt("cmask_s", [P, 4 * 512], U8)
        zrow_u = sbt("zrow_u", [P, K15], U32)
        neg_c = sbt("neg_c", [P, 1], F32)
        cst_s = sbt("cst_s", [P, 4], F32)
        bcm_b = sbt("bcm_b", [1, 48], BF16)

        idxbuf = sbt("idxbuf", [P, NT * 16], U32)
        keepf = sbt("keepf", [P, NT * K15], F32)
        probs_all = sbt("probs_all", [P, NB * NT * 8], F32)
        relb = sbt("relb", [33, 128 * K15], BF16)
        h1b = sbt("h1b", [64, 128 * K15], BF16)
        h2b = sbt("h2b", [64, 128 * K15], BF16)
        lo8 = sbt("lo8", [8, 1], F32)
        hi8 = sbt("hi8", [8, 1], F32)
        mid8 = sbt("mid8", [8, 1], F32)
        mid8t = sbt("mid8t", [8, 1], F32)
        cnt8 = sbt("cnt8", [8, 1], F32)
        cge8 = sbt("cge8", [8, 1], U8)
        cgeN = sbt("cgeN", [8, 1], U8)
        loT_s = sbt("loT_s", [1, 8], F32)
        loB_s = sbt("loB_s", [P, 8], F32)
        baseB = sbt("baseB", [P, 8], F32)
        small = sbt("small", [P, 1280], F32)
        smallb = sbt("smallb", [P, 512], BF16)
        smallu = sbt("smallu", [P, 64], U32)
        smallu8 = sbt("smallu8", [P, 32], U8)

        arena = sbt("arena", [P, 122880], U8)
        A0 = nc.lookup_mloc(arena).addr

        def at(name, shape, dt, off):
            return nc.alloc_sbuf_tensor_at(name, list(shape), dt, offset=A0 + off)

        phat_T = at("phat_T", [P, 8 * T], BF16, 0)
        packT = at("packT", [P, 8 * T], BF16, 0)
        x1T = at("x1T", [P, 8 * T], BF16, 0)
        tokrows = at("tokrows", [P, D], BF16, 0)
        tokT = at("tokT", [P, 8 * CAP], BF16, 2048)
        ws1sl = at("ws1sl", [P, 8 * 512], BF16, 22528)
        w2sl = at("w2sl", [P, 4 * D], BF16, 30720)
        hsbE = at("hsbE", [P, 4 * 256], BF16, 38912)
        grows = at("grows", [P, D], F32, 40960)
        grows2 = at("grows2", [P, D], F32, 46080)
        xbE = at("xbE", [P, 256], F32, 50176)
        wstB = at("wstB", [P, HID], BF16, 81920)
        w2tB = at("w2tB", [P, HID], BF16, 73728)
        gscrE = at("gscrE", [P, 256], F32, 45056)
        logits = at("logits", [P, T], F32, 65536)
        normP_bc = at("normP_bc", [P, T], F32, 81920)
        pack = at("pack", [P, 16 * D], BF16, 65536)
        AG = at("AG", [P, 4096], F32, 65536)
        AGs = at("AGs", [P, 4096], F32, 81920)
        ST = 98304
        xstage = at("xstage", [P, D], F32, ST)
        rows_f = at("rows_f", [P, D], F32, ST)
        sqs = at("sqs", [P, D], F32, ST)
        xb_st = at("xb_st", [P, D], BF16, ST + 4096)
        xT_t = at("xT_t", [P, 8 * 128], BF16, ST + 6144)
        riT = at("riT", [P, 8 * 128], F32, ST + 8192)
        riTb = at("riTb", [P, 8 * 128], BF16, ST + 12288)
        cs_st = at("cs_st", [P, 8 * 128], F32, ST + 14336)
        rows_b = at("rows_b", [P, D], BF16, ST + 18432)
        sqs2 = at("sqs2", [P, D], F32, ST + 20480)
        x0st = at("x0st", [P, D], F32, ST)
        ust = at("ust", [P, D], F32, ST + 4096)
        scrap = at("scrap", [P, D], F32, ST + 8192)
        x1bst = at("x1bst", [P, D], BF16, ST + 12288)
        vt_b = at("vt_b", [P, 8 * 128], BF16, ST + 14336)
        ct_b = at("ct_b", [P, 256], BF16, ST + 16384)
        pwk = at("pwk", [P, 512], F32, ST + 16896)
        hxw = at("hxw", [P, 3 * 240], F32, ST + 18944)
        c256 = at("c256", [P, 256], F32, ST + 21824)
        t256 = at("t256", [P, 256], F32, ST + 22848)
        bx1row = at("bx1row", [P, D], BF16, ST)
        bws1 = at("bws1", [P, 8 * 512], BF16, ST + 2048)
        bw2 = at("bw2", [P, 4 * D], BF16, ST + 10240)
        shhs = at("shhs", [P, 4 * 256], BF16, ST + 18432)
        gscrS = at("gscrS", [P, 256], F32, ST + 20480)
        xbS = at("xbS", [P, 256], F32, ST + 21504)

        psF = en(nc.psum_tensor("psF", [P, 3072], F32))
        psB = en(nc.psum_tensor("psB", [P, 1024], BF16))
        psB2 = en(nc.psum_tensor("psB2", [P, 1024], BF16))

        sLD = Sem(nc, "sLD")
        sGP = Sem(nc, "sGP")
        sTE = Sem(nc, "sTE")
        sVE = Sem(nc, "sVE")
        sSC = Sem(nc, "sSC")
        sCC = Sem(nc, "sCC")
        sST = Sem(nc, "sST")
        ALL = [sLD, sGP, sTE, sVE, sSC, sST]

        pr = Prog(nc)
        OP = pr.op

        def VGAP():
            OP("vector", lambda e: e.tensor_copy(out=small[:, 519:520], in_=cst_s[:, 0:1]))

        def BAR():
            return [(s, s.v) for s in ALL]

        def dump(name, src_ap, wait=None):
            if name in dbg_outs:
                o = dbg_outs[name]
                OP("sync", lambda e, o=o, s=src_ap: e.dma_start(out=o.ap(), in_=s),
                   wait=wait if wait is not None else BAR(), inc=sLD, incn=16)

        def ld(dst, src):
            OP("sync", lambda e, d=dst, s=src: e.dma_start(out=d, in_=s), inc=sLD, incn=16)

        def ldc(dst, src):
            OP("gpsimd", lambda e, d=dst, s=src: e.dma_start(out=d, in_=s), inc=sGP, incn=16)

        for c in range(8):
            ldc(wi_b[:, c * D:(c + 1) * D], wi_in.ap()[c * 128:(c + 1) * 128, :])
            ldc(wp_b[:, c * D:(c + 1) * D], wp_in.ap()[c * 128:(c + 1) * 128, :])
            ldc(wo_b[:, c * D:(c + 1) * D], wo_in.ap()[c * 128:(c + 1) * 128, :])
        ldc(palT_b[:, 0:D], palT_in.ap()[0:128, :])
        ldc(palT_b[:, D:2 * D], palT_in.ap()[128:256, :])
        for c in range(8):
            ldc(gate_b[:, c * E:(c + 1) * E], gate_in.ap()[c * 128:(c + 1) * 128, :])
        ldc(w1e_b[:, :], w1e_in.ap())
        ldc(w2a_b[:, :], w2a_in.ap()); ldc(wc_b[:, :], wc_in.ap()); ldc(wm_b[:, :], wm_in.ap())
        ldc(bs2_b[:, :], bs2_in.ap())
        ld(b1a_s[:, :], b1a_in.ap()); ld(b2a_s[:, :], b2a_in.ap())
        ld(bc_r[:, :], bc_in.ap()); ld(bm_r[:, :], bm_in.ap())
        ld(b1f_s[:, :], b1f_in.ap()); ld(bs1_s[:, :], bs1_in.ap())
        ld(identb_s[:, :], identb_in.ap()); ld(identf_s[:, :], identf_in.ap())
        ld(onesrf_s[:, :], onesrf_in.ap()); ld(onescf_s[:, :], onescf_in.ap())
        ld(onesrb_s[:, :], onesrb_in.ap()); ld(ones128_s[:, :], ones128_in.ap())
        ld(tril_s[:, :], tril_in.ap())
        ld(e16_s[:, :], e16_in.ap()); ld(e16T_s[:, :], e16T_in.ap())
        ld(iota16_s[:, :], iota16_in.ap()); ld(pcol_s[:, :], pcol_in.ap())
        ld(gcol_s[:, :], gcol_in.ap()); ld(tokid_s[:, :], tokid_in.ap())
        ld(trash_s[:, :], trash_in.ap()); ld(ecap_s[:, :], ecap_in.ap())
        ld(cmask_s[:, :], cmask_in.ap())
        ld0, gp0 = sLD.v, sGP.v

        OP("vector", lambda e: e.memset(zrow_u[:, :], ZPI), wait=[(sGP, gp0), (sLD, ld0)], inc=sVE)
        OP("vector", lambda e: e.memset(neg_c[:, :], MASKV), inc=sVE)
        OP("vector", lambda e: e.memset(cst_s[:, 0:1], 1e-24), inc=sVE)
        OP("vector", lambda e: e.memset(cst_s[:, 1:2], 1e-5), inc=sVE)
        OP("vector", lambda e: e.memset(cst_s[:, 2:3], SQGC), inc=sVE)
        OP("vector", lambda e: e.memset(cst_s[:, 3:4], GS), inc=sVE)
        OP("vector", lambda e: e.memset(baseB[:, :], 0.0), inc=sVE)
        OP("vector", lambda e: e.tensor_copy(out=bcm_b[0:1, 0:15], in_=bc_r[0:1, 0:1].to_broadcast([1, 15])), inc=sVE)
        OP("vector", lambda e: e.tensor_copy(out=bcm_b[0:1, 15:30], in_=bc_r[0:1, 1:2].to_broadcast([1, 15])), inc=sVE)
        OP("vector", lambda e: e.tensor_copy(out=bcm_b[0:1, 30:45], in_=bm_r[0:1, 0:1].to_broadcast([1, 15])), inc=sVE)
        OP("vector", lambda e: e.memset(small[0:1, 0:1024], 0.0), inc=sVE)
        OP("gpsimd", lambda e: e.dma_start(out=PI.ap()[ZPI:ZPI + 1, :], in_=small[0:1, 0:1024]), wait=BAR(), inc=sGP, incn=16)
        OP("gpsimd", lambda e: e.dma_start(out=x1b_d.ap()[ZX1:ZX1 + 1, :], in_=small[0:1, 0:1024]), inc=sGP, incn=16)
        LR128 = LROWS // 128
        OP("vector", lambda e: e.memset(smallu[:, 1:2], 0), inc=sVE)
        OP("vector", lambda e: e.memset(smallu[:, 0:1], ZX1), inc=sVE)
        OP("sync", lambda e: e.dma_start(
            out=lists_d.ap().rearrange("(p r) c -> p (r c)", p=128),
            in_=smallu[:, 0:2].unsqueeze(1).to_broadcast([P, LR128, 2])),
            wait=BAR(), inc=sLD, incn=16)

        gp_p0 = sGP.v
        conv_jobs = []
        for e_ in range(E):
            conv_jobs.append((w1f_in.ap()[e_], e_ * NWE, D, HID))
            conv_jobs.append((w2f_in.ap()[e_], e_ * NWE + D * HID, HID, D))
        conv_jobs.append((ws1_in.ap(), WS1B, D, HID))
        conv_jobs.append((ws2_in.ap(), WS2B, HID, D))
        wff = wffn_d.ap()
        conv_chunks = []
        for (src, base, nr, ncol) in conv_jobs:
            rb = 512 if ncol == HID else 1024
            for r0 in range(0, nr, rb):
                def mk(src=src, base=base, r0=r0, rb=rb, ncl=ncol):
                    return lambda e: e.dma_start(
                        out=wff[base + r0 * ncl: base + (r0 + rb) * ncl].rearrange("(p n) -> p n", p=rb),
                        in_=src[r0:r0 + rb, :])
                conv_chunks.append(mk())

        def emit_conv(n):
            for _ in range(n):
                if conv_chunks:
                    OP("gpsimd", conv_chunks.pop(0), inc=sGP, incn=16)

        IDB = identb_s[:, :]
        IDF = identf_s[:, :]

        def proj_tile(b, t0, w_b):
            OP("sync", lambda e: e.dma_start(out=xstage[:, :], in_=x_in.ap()[b, t0:t0 + 128, :]),
               wait=BAR(), inc=sLD, incn=16)
            OP("sync", lambda e: e.dma_start(out=cs_st[:, 0:512].rearrange("p (m t) -> p m t", m=4), in_=cos_in.ap()[:, t0:t0 + 128].rearrange("(m p) t -> p m t", p=128)), inc=sLD, incn=16)
            OP("sync", lambda e: e.dma_start(out=cs_st[:, 512:1024].rearrange("p (m t) -> p m t", m=4), in_=sin_in.ap()[:, t0:t0 + 128].rearrange("(m p) t -> p m t", p=128)), inc=sLD, incn=16)
            OP("vector", lambda e: e.tensor_copy(out=xb_st[:, :], in_=xstage[:, :]), wait=[(sLD, sLD.v)], inc=sVE)
            for c in range(8):
                OP("tensor", lambda e, c=c: e.transpose(psB[:, c * 128:(c + 1) * 128], xb_st[:, c * 128:(c + 1) * 128], IDB),
                   wait=[(sVE, sVE.v)] if c == 0 else None, inc=sTE if c == 7 else None)
            for c in range(8):
                OP("vector", lambda e, c=c: e.tensor_copy(out=xT_t[:, c * 128:(c + 1) * 128], in_=psB[:, c * 128:(c + 1) * 128]),
                   wait=[(sTE, sTE.v)] if c == 0 else None, inc=sVE if c == 7 else None)
            for m in range(8):
                for c in range(8):
                    OP("tensor", lambda e, m=m, c=c, w_b=w_b: e.matmul(
                        psF[:, m * 128:(m + 1) * 128],
                        w_b[:, c * D + m * 128: c * D + (m + 1) * 128],
                        xT_t[:, c * 128:(c + 1) * 128],
                        start=(c == 0), stop=(c == 7)),
                       wait=[(sVE, sVE.v)] if (m == 0 and c == 0) else None,
                       inc=sTE if (m == 7 and c == 7) else None)
            teq = sTE.v
            for m in range(4):
                cosm = cs_st[:, m * 128:(m + 1) * 128]
                sinm = cs_st[:, 512 + m * 128: 512 + (m + 1) * 128]
                pm = psF[:, m * 128:(m + 1) * 128]
                pm4 = psF[:, (m + 4) * 128:(m + 5) * 128]
                t1 = small[:, 0:128]
                t2 = small[:, 128:256]
                OP("vector", lambda e, pm=pm, cosm=cosm, t1=t1: e.tensor_mul(t1, pm, cosm),
                   wait=[(sTE, teq)] if m == 0 else None)
                OP("vector", lambda e, pm4=pm4, sinm=sinm, t2=t2: e.tensor_mul(t2, pm4, sinm))
                OP("vector", lambda e, m=m, t1=t1, t2=t2: e.tensor_sub(riT[:, m * 128:(m + 1) * 128], t1, t2))
                OP("vector", lambda e, pm=pm, sinm=sinm, t1=t1: e.tensor_mul(t1, pm, sinm))
                OP("vector", lambda e, pm4=pm4, cosm=cosm, t2=t2: e.tensor_mul(t2, pm4, cosm))
                OP("vector", lambda e, m=m, t1=t1, t2=t2: e.tensor_add(riT[:, (m + 4) * 128:(m + 5) * 128], t1, t2),
                   inc=sVE if m == 3 else None)

        def phaseA(b):
            import os as _os
            KSUB = int(_os.environ.get("KSUB", "9"))
            for tl in range(NT):
                t0 = tl * 128
                proj_tile(b, t0, wp_b)
                for c in range(8):
                    OP("scalar", lambda e, c=c: e.activation(out=sqs[:, c * 128:(c + 1) * 128], in_=riT[:, c * 128:(c + 1) * 128], func=AF.Square),
                       wait=[(sVE, sVE.v)] if c == 0 else None, inc=sSC if c == 7 else None)
                for c in range(8):
                    OP("tensor", lambda e, c=c: e.matmul(psF[0:1, 1536:1664], onescf_s[:, :], sqs[:, c * 128:(c + 1) * 128],
                                                         start=(c == 0), stop=(c == 7)),
                       wait=[(sSC, sSC.v)] if c == 0 else None, inc=sTE if c == 7 else None)
                OP("scalar", lambda e: e.activation(out=small[0:1, 256:384], in_=psF[0:1, 1536:1664], func=AF.Sqrt, bias=cst_s[0:1, 0:1]),
                   wait=[(sTE, sTE.v)], inc=sSC)
                OP("tensor", lambda e: e.matmul(psF[:, 2560:2688], onesrf_s[0:1, :], small[0:1, 256:384], start=True, stop=True),
                   wait=[(sSC, sSC.v)], inc=sTE)
                OP("vector", lambda e, t0=t0: e.tensor_copy(out=normP_bc[:, t0:t0 + 128], in_=psF[:, 2560:2688]),
                   wait=[(sTE, sTE.v)], inc=sVE)
                OP("vector", lambda e: e.reciprocal(out=small[:, 384:512], in_=psF[:, 2560:2688]), inc=sVE)
                VGAP()
                for c in range(8):
                    OP("vector", lambda e, c=c, t0=t0: e.tensor_mul(
                        phat_T[:, c * T + t0: c * T + t0 + 128], riT[:, c * 128:(c + 1) * 128], small[:, 384:512]),
                       inc=sVE if c == 7 else None)
                for c in range(8):
                    OP("tensor", lambda e, c=c, t0=t0: e.transpose(psB[:, c * 128:(c + 1) * 128],
                                                                   phat_T[:, c * T + t0: c * T + t0 + 128], IDB),
                       wait=[(sVE, sVE.v)] if c == 0 else None, inc=sTE if c == 7 else None)
                for c in range(8):
                    OP("vector", lambda e, c=c: e.tensor_copy(out=rows_b[:, c * 128:(c + 1) * 128], in_=psB[:, c * 128:(c + 1) * 128]),
                       wait=[(sTE, sTE.v)] if c == 0 else None, inc=sVE if c == 7 else None)
                OP("sync", lambda e, t0=t0: e.dma_start(out=PI.ap()[t0:t0 + 128, :], in_=rows_b[:, :]),
                   wait=[(sVE, sVE.v)], inc=sLD, incn=16)
            dump("phatT", phat_T[:, :])
            dump("normPbc", normP_bc[:, :])
            if KSUB < 2:
                return
            for qt in range(NT):
                t0 = qt * 128
                nch = qt // 4 + 1
                L = nch * 512
                proj_tile(b, t0, wi_b)
                OP("vector", lambda e: e.tensor_copy(out=riTb[:, :], in_=riT[:, :]), inc=sVE)
                for c in range(8):
                    slot = 2048 if (c % 2 == 0) else 2560
                    OP("tensor", lambda e, c=c, slot=slot: e.transpose(psF[:, slot:slot + 128], riT[:, c * 128:(c + 1) * 128], IDF),
                       wait=[(sSC, sSC.v)] if c >= 2 else ([(sVE, sVE.v)] if c == 0 else None), inc=sTE)
                    OP("scalar", lambda e, c=c, slot=slot: e.copy(out=rows_f[:, c * 128:(c + 1) * 128], in_=psF[:, slot:slot + 128]),
                       wait=[(sTE, sTE.v)], inc=sSC)
                OP("scalar", lambda e: e.activation(out=sqs2[:, :], in_=rows_f[:, :], func=AF.Square, accum_out=small[:, 512:513]),
                   inc=sSC)
                OP("scalar", lambda e: e.copy(out=small[:, 516:517], in_=cst_s[:, 0:1]), inc=sSC)
                OP("scalar", lambda e: e.activation(out=small[:, 513:514], in_=small[:, 512:513], func=AF.Sqrt, bias=cst_s[:, 0:1]), inc=sSC)
                OP("vector", lambda e: e.reciprocal(out=small[:, 514:515], in_=small[:, 513:514]), wait=[(sSC, sSC.v)], inc=sVE)
                VGAP()
                OP("vector", lambda e: e.tensor_scalar(rows_b[:, :], rows_f[:, :], small[:, 514:515], None, ALU.mult), inc=sVE)
                OP("sync", lambda e, t0=t0: e.dma_start(out=PI.ap()[T + t0: T + t0 + 128, :], in_=rows_b[:, :]),
                   wait=[(sVE, sVE.v)], inc=sLD, incn=16)
                for ch in range(nch):
                    for c in range(8):
                        OP("tensor", lambda e, ch=ch, c=c: e.matmul(
                            psF[:, 1024:1536],
                            riTb[:, c * 128:(c + 1) * 128],
                            phat_T[:, c * T + ch * 512: c * T + (ch + 1) * 512],
                            start=(c == 0), stop=(c == 7)),
                           wait=[(sVE, sVE.v)] if c == 0 else None,
                           inc=sTE if c == 7 else None)
                    OP("vector", lambda e, ch=ch: e.tensor_mul(logits[:, ch * 512:(ch + 1) * 512], psF[:, 1024:1536], normP_bc[:, ch * 512:(ch + 1) * 512]),
                       wait=[(sTE, sTE.v)], inc=sVE)
                OP("vector", lambda e, qt=qt, ch=nch - 1: e.copy_predicated(
                    logits[:, ch * 512:(ch + 1) * 512], cmask_s[:, (qt % 4) * 512:(qt % 4 + 1) * 512],
                    neg_c[:, 0:1].to_broadcast([P, 512])), inc=sVE)
                if qt == 0:
                    dump("logits0", logits[:, 0:512])
                mx1 = small[:, 520:528]; mx2 = small[:, 528:536]
                mi1 = smallu[:, 0:8]; mi2 = smallu[:, 8:16]
                micat = smallu[:, 16:31]
                OP("vector", lambda e, L=L: e.max(out=mx1, in_=logits[:, 0:L]))
                OP("vector", lambda e, t0=t0: e.tensor_scalar(small[:, 536:537], pcol_s[:, :], float(t0), None, ALU.add))
                OP("vector", lambda e, L=L, mx1=mx1, mi1=mi1: e.max_index(out=mi1, in_max=mx1, in_values=logits[:, 0:L]))
                OP("vector", lambda e, L=L, mx1=mx1: e.match_replace(out=logits[:, 0:L], in_to_replace=mx1, in_values=logits[:, 0:L], imm_value=MASKV))
                OP("vector", lambda e, L=L, mx2=mx2: e.max(out=mx2, in_=logits[:, 0:L]))
                OP("vector", lambda e, qt=qt, mx1=mx1: e.tensor_scalar(keepf[:, qt * K15: qt * K15 + 8], mx1, -1e29, None, ALU.is_gt))
                OP("vector", lambda e, L=L, mx2=mx2, mi2=mi2: e.max_index(out=mi2, in_max=mx2, in_values=logits[:, 0:L]), inc=sVE)
                OP("vector", lambda e, qt=qt, mx2=mx2: e.tensor_scalar(keepf[:, qt * K15 + 8: qt * K15 + 15], mx2[:, 0:7], -1e29, None, ALU.is_gt))
                OP("vector", lambda e, mi1=mi1, micat=micat: e.tensor_copy(out=micat[:, 0:8], in_=mi1))
                OP("vector", lambda e, mi2=mi2, micat=micat: e.tensor_copy(out=micat[:, 8:15], in_=mi2[:, 0:7]), inc=sVE)
                OP("vector", lambda e, qt=qt: e.tensor_copy(out=smallu8[:, 0:15], in_=keepf[:, qt * K15: qt * K15 + 15]), inc=sVE)
                VGAP()
                OP("vector", lambda e, qt=qt, micat=micat: e.select(idxbuf[:, qt * 16: qt * 16 + 15], smallu8[:, 0:15], micat, zrow_u[:, :]), inc=sVE)
                OP("vector", lambda e, qt=qt: e.tensor_copy(out=idxbuf[:, qt * 16 + 15: qt * 16 + 16], in_=gcol_s[:, qt:qt + 1]), inc=sVE)
                OP("vector", lambda e, micat=micat: e.tensor_copy(out=small[:, 544:559], in_=micat))
                OP("vector", lambda e: e.tensor_scalar(small[:, 560:575], small[:, 544:559], small[:, 536:537], None, ALU.subtract))
                OP("vector", lambda e: e.tensor_copy(out=small[:, 518:519], in_=cst_s[:, 0:1]))
                OP("vector", lambda e: e.tensor_scalar(small[:, 592:607], small[:, 560:575], -1.0 / T, 0.0, ALU.mult, ALU.max))
                VGAP()
                OP("vector", lambda e, qt=qt: e.tensor_mul(small[:, 576:591], small[:, 592:607], keepf[:, qt * K15: qt * K15 + K15]), inc=sVE)
                OP("sync", lambda e, b=b, qt=qt: e.dma_start(
                    out=delta_d.ap()[b * NT + qt].rearrange("(p k) -> p k", p=128), in_=small[:, 576:591]),
                    wait=[(sVE, sVE.v)], inc=sLD, incn=16)
                if b == 0:
                    emit_conv(2)
                if qt == 0 and b == 0:
                    dump("dbg_tcol", small[:, 536:537])
                    dump("dbg_idxf", small[:, 544:559])
                    dump("dbg_d1", small[:, 560:575])
                    dump("dbg_d2", small[:, 592:607])
                    dump("dbg_d3", small[:, 576:591])
            if b == 0:
                emit_conv(len(conv_chunks))
            dump("idxbuf", idxbuf[:, :])
            dump("keepf", keepf[:, :])
            if KSUB < 3:
                return
            KTILES = int(_os.environ.get("KTILES", str(NT)))
            KA = int(_os.environ.get("KASTEP", "9"))
            packT_4d = packT.ap().rearrange("p (c t w) -> p c t w", c=8, t=128, w=32)
            te_tr = {}
            for qt in range(KTILES):
                t0 = qt * 128
                bar = [(sLD, sLD.v), (sVE, sVE.v), (sSC, sSC.v),
                       (sTE, te_tr[qt - 1] if qt > 0 else sTE.v), (sGP, gp_p0)]
                for g in range(16):
                    OP("gpsimd", lambda e, g=g, qt=qt: e.indirect_dma_start(
                        out=pack[:, g * D:(g + 1) * D], out_offset=None,
                        in_=PI.ap(), in_offset=IndirectOffsetOnAxis(idxbuf[:, qt * 16 + g: qt * 16 + g + 1], 0)),
                        wait=bar if g == 0 else None, inc=sGP, incn=16)
                gath = sGP.v
                if qt == 0:
                    dump("pack0", pack[:, :], wait=[(sGP, gath)])
                if KA < 2:
                    continue
                if _os.environ.get("KB_DELTA") != "0":
                    OP("gpsimd", lambda e, b=b, qt=qt: e.dma_start(out=relb[16:17, :], in_=delta_d.ap()[b * NT + qt: b * NT + qt + 1, :]),
                       inc=sGP, incn=16)
                dl = sGP.v
                evac_done = {}
                for g in range(16):
                    pb = psB if g % 2 == 0 else psB2
                    for c in range(8):
                        w = None
                        if c == 0:
                            w = [(sGP, gath)] if g == 0 else []
                            if g >= 2:
                                w.append((sVE, evac_done[g - 2]))
                        OP("tensor", lambda e, g=g, c=c, pb=pb: e.transpose(
                            pb[:, c * 128:(c + 1) * 128], pack[:, g * D + c * 128: g * D + (c + 1) * 128], IDB),
                           wait=w if w else None, inc=sTE if c == 7 else None)
                    for c in range(8):
                        OP("vector", lambda e, g=g, c=c, pb=pb: e.tensor_copy(
                            out=packT_4d[:, c, :, g], in_=pb[:, c * 128:(c + 1) * 128]),
                           wait=[(sTE, sTE.v)] if c == 0 else None, inc=sVE)
                    evac_done[g] = sVE.v
                te_tr[qt] = sTE.v
                if KA < 3:
                    continue
                for grp in range(32):
                    slot = (grp % 8) * 64
                    for c in range(8):
                        OP("tensor", lambda e, grp=grp, c=c, slot=slot: e.matmul(
                            psF[:, slot:slot + 64],
                            packT[:, c * T + grp * 128: c * T + (grp + 1) * 128],
                            packT_4d[:, c, 4 * grp:4 * grp + 4, 0:16],
                            start=(c == 0), stop=(c == 7)),
                           wait=[(sVE, sVE.v)] if c == 0 else None,
                           inc=sTE if c == 7 else None)
                    for tau in range(4):
                        t_ = grp * 4 + tau
                        OP("vector", lambda e, tau=tau, t_=t_, slot=slot: e.tensor_scalar(
                            relb[0:16, t_ * K15:(t_ + 1) * K15],
                            psF[32 * tau:32 * tau + 16, slot + tau * 16: slot + tau * 16 + 15],
                            1.0, -1.0, ALU.min, ALU.max),
                           wait=[(sTE, sTE.v)] if tau == 0 else None,
                           inc=sVE if tau == 3 else None)
                if qt == 0:
                    dump("relb0", relb[0:33, :])
                    dump("deltad", delta_d.ap()[b * NT: b * NT + 1, :])
                if KA < 4:
                    continue
                for (n0, n1) in ((0, 512), (512, 1024), (1024, 1536), (1536, 1920)):
                    OP("tensor", lambda e, n0=n0, n1=n1: e.matmul(psF[0:64, 512 + n0:512 + n1], w1e_b[:, :], relb[0:17, n0:n1], start=True, stop=True),
                       wait=[(sVE, sVE.v), (sGP, dl)] if n0 == 0 else None,
                       inc=sTE if n1 == 1920 else None)
                OP("scalar", lambda e: e.activation(out=h1b[:, :], in_=psF[0:64, 512:2432], func=AF.Gelu, bias=b1a_s[:, :]),
                   wait=[(sTE, sTE.v)], inc=sSC)
                for (n0, n1) in ((0, 512), (512, 1024), (1024, 1536), (1536, 1920)):
                    OP("tensor", lambda e, n0=n0, n1=n1: e.matmul(psF[0:64, 512 + n0:512 + n1], w2a_b[:, :], h1b[:, n0:n1], start=True, stop=True),
                       wait=[(sSC, sSC.v)] if n0 == 0 else None,
                       inc=sTE if n1 == 1920 else None)
                OP("scalar", lambda e: e.activation(out=h2b[:, :], in_=psF[0:64, 512:2432], func=AF.Gelu, bias=b2a_s[:, :]),
                   wait=[(sTE, sTE.v)], inc=sSC)
                if qt == 0:
                    dump("h1b0", h1b[:, :])
                    dump("h2b0", h2b[:, :])
                if KA < 5:
                    continue
                OP("tensor", lambda e: e.matmul(psF[:, 2560:2605], onesrb_s[0:1, :], bcm_b[0:1, 0:45], start=True, stop=False, skip_group_check=True),
                   wait=[(sSC, sSC.v)])
                hp = h2b.ap()
                pitch_h = hp.ap[0][0]
                for k in range(K15):
                    lA = bass.AP(h2b, k, [[pitch_h, 64], [K15, 128]])
                    OP("tensor", lambda e, k=k, lA=lA: e.matmul(psF[:, 2560 + k:2561 + k], lA, wc_b[0:64, 0:1], start=False, stop=True, skip_group_check=True))
                    OP("tensor", lambda e, k=k, lA=lA: e.matmul(psF[:, 2575 + k:2576 + k], lA, wc_b[0:64, 1:2], start=False, stop=True, skip_group_check=True))
                    OP("tensor", lambda e, k=k, lA=lA: e.matmul(psF[:, 2590 + k:2591 + k], lA, wm_b[0:64, 0:1], start=False, stop=True, skip_group_check=True),
                       inc=sTE if k == K15 - 1 else None)
                OP("scalar", lambda e: e.activation(out=pwk[:, 0:30], in_=psF[:, 2560:2590], func=AF.Tanh),
                   wait=[(sTE, sTE.v)], inc=sSC)
                OP("vector", lambda e: e.tensor_copy(out=pwk[:, 32:47], in_=psF[:, 2590:2605]), wait=[(sTE, sTE.v)], inc=sVE)
                VGAP()
                OP("vector", lambda e: e.tensor_reduce(pwk[:, 48:49], pwk[:, 32:47], AX.X, ALU.max), inc=sVE)
                VGAP()
                OP("vector", lambda e: e.tensor_scalar(pwk[:, 64:79], pwk[:, 32:47], pwk[:, 48:49], None, ALU.subtract), inc=sVE)
                OP("scalar", lambda e: e.activation(out=pwk[:, 80:95], in_=pwk[:, 64:79], func=AF.Exp),
                   wait=[(sVE, sVE.v)], inc=sSC)
                OP("vector", lambda e, qt=qt: e.tensor_mul(pwk[:, 96:111], pwk[:, 80:95], keepf[:, qt * K15:(qt + 1) * K15]),
                   wait=[(sSC, sSC.v)], inc=sVE)
                VGAP()
                OP("vector", lambda e: e.tensor_reduce(pwk[:, 112:113], pwk[:, 96:111], AX.X, ALU.add), inc=sVE)
                VGAP()
                OP("vector", lambda e: e.reciprocal(out=pwk[:, 113:114], in_=pwk[:, 112:113]), inc=sVE)
                VGAP()
                OP("vector", lambda e: e.tensor_scalar(pwk[:, 128:143], pwk[:, 96:111], pwk[:, 113:114], None, ALU.mult), inc=sVE)
                if qt == 0:
                    dump("pwk0", pwk[:, 0:352])
                if KA < 6:
                    continue
                OP("vector", lambda e: e.tensor_scalar(pwk[:, 144:174], pwk[:, 0:30], 7.5, 7.5, ALU.mult, ALU.add), wait=[(sSC, sSC.v)], inc=sVE)
                VGAP()
                OP("vector", lambda e: e.tensor_scalar(pwk[:, 304:334], pwk[:, 144:174], 15.0, 0.0, ALU.min, ALU.max), inc=sVE)
                VGAP()
                OP("vector", lambda e: e.tensor_scalar(pwk[:, 176:206], pwk[:, 304:334], -0.5, None, ALU.add), inc=sVE)
                VGAP()
                OP("vector", lambda e: e.tensor_copy(out=smallu[:, 32:62], in_=pwk[:, 176:206]), inc=sVE)
                VGAP()
                OP("vector", lambda e: e.tensor_copy(out=pwk[:, 176:206], in_=smallu[:, 32:62]), inc=sVE)
                VGAP()
                OP("vector", lambda e: e.tensor_sub(pwk[:, 208:238], pwk[:, 304:334], pwk[:, 176:206]), inc=sVE)
                OP("vector", lambda e: e.tensor_scalar(pwk[:, 240:270], pwk[:, 176:206], 1.0, 15.0, ALU.add, ALU.min), inc=sVE)
                VGAP()
                OP("vector", lambda e: e.tensor_scalar(pwk[:, 272:302], pwk[:, 208:238], -1.0, 1.0, ALU.mult, ALU.add), inc=sVE)
                VGAP()
                def bc15(ap_):
                    return ap_.rearrange("p k -> p k").unsqueeze(2).to_broadcast([P, 15, 16])
                OP("vector", lambda e: e.tensor_tensor(hxw[:, 0:240].rearrange("p (k i) -> p k i", k=15), iota16_s[:, :].rearrange("p (k i) -> p k i", k=15), bc15(pwk[:, 176:191]), ALU.is_equal), inc=sVE)
                OP("vector", lambda e: e.tensor_tensor(hxw[:, 0:240].rearrange("p (k i) -> p k i", k=15), hxw[:, 0:240].rearrange("p (k i) -> p k i", k=15), bc15(pwk[:, 272:287]), ALU.mult), inc=sVE)
                OP("vector", lambda e: e.tensor_tensor(t256[:, 0:240].rearrange("p (k i) -> p k i", k=15), iota16_s[:, :].rearrange("p (k i) -> p k i", k=15), bc15(pwk[:, 240:255]), ALU.is_equal), inc=sVE)
                OP("vector", lambda e: e.tensor_tensor(t256[:, 0:240].rearrange("p (k i) -> p k i", k=15), t256[:, 0:240].rearrange("p (k i) -> p k i", k=15), bc15(pwk[:, 208:223]), ALU.mult), inc=sVE)
                VGAP()
                OP("vector", lambda e: e.tensor_add(hxw[:, 0:240], hxw[:, 0:240], t256[:, 0:240]), inc=sVE)
                OP("vector", lambda e: e.tensor_tensor(hxw[:, 240:480].rearrange("p (k i) -> p k i", k=15), iota16_s[:, :].rearrange("p (k i) -> p k i", k=15), bc15(pwk[:, 191:206]), ALU.is_equal), inc=sVE)
                OP("vector", lambda e: e.tensor_tensor(hxw[:, 240:480].rearrange("p (k i) -> p k i", k=15), hxw[:, 240:480].rearrange("p (k i) -> p k i", k=15), bc15(pwk[:, 287:302]), ALU.mult), inc=sVE)
                OP("vector", lambda e: e.tensor_tensor(t256[:, 0:240].rearrange("p (k i) -> p k i", k=15), iota16_s[:, :].rearrange("p (k i) -> p k i", k=15), bc15(pwk[:, 255:270]), ALU.is_equal), inc=sVE)
                OP("vector", lambda e: e.tensor_tensor(t256[:, 0:240].rearrange("p (k i) -> p k i", k=15), t256[:, 0:240].rearrange("p (k i) -> p k i", k=15), bc15(pwk[:, 223:238]), ALU.mult), inc=sVE)
                VGAP()
                OP("vector", lambda e: e.tensor_add(hxw[:, 240:480], hxw[:, 240:480], t256[:, 0:240]), inc=sVE)
                VGAP()
                OP("vector", lambda e: e.tensor_tensor(hxw[:, 480:720].rearrange("p (k i) -> p k i", k=15), hxw[:, 240:480].rearrange("p (k i) -> p k i", k=15), bc15(pwk[:, 128:143]), ALU.mult), inc=sVE)
                bufs = [t256[:, :], small[:, 640:896]]
                for k in range(K15):
                    ay = hxw[:, 480 + k * 16:480 + (k + 1) * 16].unsqueeze(2).to_broadcast([P, 16, 16])
                    hx = hxw[:, k * 16:(k + 1) * 16].unsqueeze(1).to_broadcast([P, 16, 16])
                    dst = c256[:, :] if k == 0 else bufs[k % 2]
                    OP("vector", lambda e, ay=ay, hx=hx, dst=dst: e.tensor_tensor(dst.rearrange("p (y x) -> p y x", y=16), ay, hx, ALU.mult))
                    if k >= 2:
                        pb2 = bufs[(k - 1) % 2]
                        OP("vector", lambda e, pb2=pb2: e.tensor_add(c256[:, :], c256[:, :], pb2))
                VGAP()
                OP("vector", lambda e: e.tensor_add(c256[:, :], c256[:, :], bufs[(K15 - 1) % 2]), inc=sVE)
                if qt == 0:
                    dump("c256_0", c256[:, :])
                if KA < 7:
                    continue
                for c2 in range(2):
                    cts = 2048 if c2 == 0 else 2688
                    OP("tensor", lambda e, c2=c2, cts=cts: e.transpose(psF[:, cts:cts + 128], c256[:, c2 * 128:(c2 + 1) * 128], IDF),
                       wait=[(sVE, sVE.v)] if c2 == 0 else None, inc=sTE)
                    OP("vector", lambda e, c2=c2, cts=cts: e.tensor_copy(out=ct_b[:, c2 * 128:(c2 + 1) * 128], in_=psF[:, cts:cts + 128]),
                       wait=[(sTE, sTE.v)], inc=sVE)
                for m in range(8):
                    for c2 in range(2):
                        OP("tensor", lambda e, m=m, c2=c2: e.matmul(
                            psF[:, m * 128:(m + 1) * 128],
                            palT_b[:, c2 * D + m * 128: c2 * D + (m + 1) * 128],
                            ct_b[:, c2 * 128:(c2 + 1) * 128],
                            start=(c2 == 0), stop=(c2 == 1)),
                           wait=[(sVE, sVE.v)] if (m == 0 and c2 == 0) else None,
                           inc=sTE if (m == 7 and c2 == 1) else None)
                for m in range(8):
                    OP("vector", lambda e, m=m: e.tensor_copy(out=vt_b[:, m * 128:(m + 1) * 128], in_=psF[:, m * 128:(m + 1) * 128]),
                       wait=[(sTE, sTE.v)] if m == 0 else None, inc=sVE if m == 7 else None)
                if qt == 0:
                    dump("vtb0", vt_b[:, :])
                for n2 in range(2):
                    for c in range(8):
                        OP("tensor", lambda e, n2=n2, c=c: e.matmul(
                            psF[:, 1024 + n2 * 512:1024 + (n2 + 1) * 512],
                            vt_b[:, c * 128:(c + 1) * 128],
                            wo_b[:, c * D + n2 * 512: c * D + (n2 + 1) * 512],
                            start=(c == 0), stop=(c == 7)),
                           wait=[(sVE, sVE.v)] if (n2 == 0 and c == 0) else None,
                           inc=sTE if (n2 == 1 and c == 7) else None)
                if KA < 8:
                    continue
                OP("sync", lambda e, b=b, t0=t0: e.dma_start(out=x0st[:, :], in_=x_in.ap()[b, t0:t0 + 128, :]),
                   wait=[(sVE, sVE.v), (sSC, sSC.v)], inc=sLD, incn=16)
                OP("vector", lambda e: e.tensor_add(ust[:, :], psF[:, 1024:2048], x0st[:, :]),
                   wait=[(sTE, sTE.v), (sLD, sLD.v)], inc=sVE)
                VGAP()
                OP("vector", lambda e: e.tensor_reduce(small[:, 600:601], ust[:, :], AX.X, ALU.add), inc=sVE)
                VGAP()
                OP("vector", lambda e: e.tensor_scalar(small[:, 601:602], small[:, 600:601], 1.0 / D, None, ALU.mult), inc=sVE)
                VGAP()
                OP("vector", lambda e: e.tensor_scalar(x0st[:, :], ust[:, :], small[:, 601:602], None, ALU.subtract), inc=sVE)
                OP("scalar", lambda e: e.activation(out=scrap[:, :], in_=x0st[:, :], func=AF.Square, accum_out=small[:, 602:603]),
                   wait=[(sVE, sVE.v)], inc=sSC)
                OP("vector", lambda e: e.tensor_scalar(small[:, 603:604], small[:, 602:603], 1.0 / D, None, ALU.mult),
                   wait=[(sSC, sSC.v)], inc=sVE)
                OP("scalar", lambda e: e.activation(out=small[:, 604:605], in_=small[:, 603:604], func=AF.Sqrt, bias=cst_s[:, 1:2]),
                   wait=[(sVE, sVE.v)], inc=sSC)
                OP("vector", lambda e: e.reciprocal(out=small[:, 605:606], in_=small[:, 604:605]), wait=[(sSC, sSC.v)], inc=sVE)
                VGAP()
                OP("vector", lambda e: e.tensor_scalar(scrap[:, :], x0st[:, :], small[:, 605:606], None, ALU.mult), inc=sVE)
                VGAP()
                OP("vector", lambda e: e.tensor_copy(out=x1bst[:, :], in_=scrap[:, :]), inc=sVE)
                if qt == 0:
                    dump("x1st0", scrap[:, :])
                OP("sync", lambda e, b=b, t0=t0: e.dma_start(out=x1f_d.ap()[b * T + t0: b * T + t0 + 128, :], in_=scrap[:, :]),
                   wait=[(sVE, sVE.v)], inc=sLD, incn=16)
                OP("sync", lambda e, b=b, t0=t0: e.dma_start(out=x1b_d.ap()[b * T + t0: b * T + t0 + 128, :], in_=x1bst[:, :]),
                   inc=sLD, incn=16)
            dump("x1_b%d" % b, None)

        def gate_batch(b):
            for tl in range(NT):
                t0 = tl * 128
                OP("sync", lambda e, b=b, t0=t0: e.dma_start(out=bx1row[:, :], in_=x1b_d.ap()[b * T + t0: b * T + t0 + 128, :]),
                   wait=BAR(), inc=sLD, incn=16)
                for c in range(8):
                    OP("tensor", lambda e, c=c: e.transpose(psB[:, c * 128:(c + 1) * 128], bx1row[:, c * 128:(c + 1) * 128], IDB),
                       wait=[(sLD, sLD.v), (sVE, sVE.v)] if c == 0 else None, inc=sTE if c == 7 else None)
                for c in range(8):
                    OP("vector", lambda e, c=c, t0=t0: e.tensor_copy(out=x1T[:, c * T + t0: c * T + t0 + 128], in_=psB[:, c * 128:(c + 1) * 128]),
                       wait=[(sTE, sTE.v)] if c == 0 else None, inc=sVE if c == 7 else None)
                for c in range(8):
                    OP("tensor", lambda e, c=c, t0=t0: e.matmul(
                        psF[:, 0:8], x1T[:, c * T + t0: c * T + t0 + 128], gate_b[:, c * E:(c + 1) * E],
                        start=(c == 0), stop=(c == 7)),
                       wait=[(sVE, sVE.v)] if c == 0 else None, inc=sTE if c == 7 else None)
                pa = (b * NT + tl) * 8
                OP("vector", lambda e: e.tensor_reduce(small[:, 0:1], psF[:, 0:8], AX.X, ALU.max), wait=[(sTE, sTE.v)], inc=sVE)
                VGAP()
                OP("vector", lambda e: e.tensor_scalar(small[:, 8:16], psF[:, 0:8], small[:, 0:1], None, ALU.subtract), inc=sVE)
                OP("scalar", lambda e: e.activation(out=small[:, 16:24], in_=small[:, 8:16], func=AF.Exp),
                   wait=[(sVE, sVE.v)], inc=sSC)
                OP("vector", lambda e: e.tensor_reduce(small[:, 1:2], small[:, 16:24], AX.X, ALU.add), wait=[(sSC, sSC.v)], inc=sVE)
                VGAP()
                OP("vector", lambda e: e.reciprocal(out=small[:, 2:3], in_=small[:, 1:2]), inc=sVE)
                VGAP()
                OP("vector", lambda e, pa=pa: e.tensor_scalar(probs_all[:, pa:pa + 8], small[:, 16:24], small[:, 2:3], None, ALU.mult), inc=sVE)
                OP("tensor", lambda e, pa=pa: e.transpose(psF[0:8, 128:256], probs_all[:, pa:pa + 8], IDF),
                   wait=[(sVE, sVE.v)], inc=sTE)
                OP("vector", lambda e: e.tensor_copy(out=small[0:8, 32:160], in_=psF[0:8, 128:256]), wait=[(sTE, sTE.v)], inc=sVE)
                OP("sync", lambda e, b=b, t0=t0: e.dma_start(out=probsT_d.ap()[:, b * T + t0: b * T + t0 + 128], in_=small[0:8, 32:160]),
                   wait=[(sVE, sVE.v)], inc=sLD, incn=16)

        def shared_batch(b):
            nsub = 2
            srcW1 = wffn_d.ap()[WS1B:WS1B + D * HID].rearrange("(r h) -> r h", h=HID)
            srcW2 = wffn_d.ap()[WS2B:WS2B + HID * D].rearrange("(h d) -> h d", d=D)
            bufs = [(bws1, bw2), (wstB, w2tB)]
            ld_after = {}
            te_after = {}
            KK = 16 * 8

            def emit_load(k, first):
                hcc = k % 8
                bw_, b2_ = bufs[k % 2]
                w = [(sTE, te_after[k - 2])] if k >= 2 else (BAR() if first else None)
                for c in range(8):
                    OP("sync", lambda e, c=c, hcc=hcc, bw_=bw_: e.dma_start(
                        out=bw_[:, c * 512:(c + 1) * 512], in_=srcW1[c * 128:(c + 1) * 128, hcc * 512:(hcc + 1) * 512]),
                       wait=w if c == 0 else None, inc=sLD, incn=16)
                OP("sync", lambda e, hcc=hcc, b2_=b2_: e.dma_start(
                    out=b2_[:, :].rearrange("p (s d) -> p s d", s=4), in_=srcW2[hcc * 512:(hcc + 1) * 512, :].rearrange("(s p) d -> p s d", p=128)),
                   inc=sLD, incn=16)
                ld_after[k] = sLD.v

            emit_load(0, True)
            emit_load(1, False)
            for k in range(KK):
                tt, hcc = divmod(k, 8)
                tb = tt * 256
                bw_, b2_ = bufs[k % 2]
                if hcc == 0:
                    for sub in range(nsub):
                        for dn in range(2):
                            OP("tensor", lambda e, sub=sub, dn=dn: e.matmul(
                                psF[:, 512 + sub * 1024 + dn * 512: 512 + sub * 1024 + (dn + 1) * 512],
                                onesrb_s[0:1, :], bs2_b[0:1, dn * 512:(dn + 1) * 512],
                                start=True, stop=False, skip_group_check=True),
                               wait=[(sVE, sVE.v)] if (sub == 0 and dn == 0) else None)
                for hsub in range(4):
                    for c in range(8):
                        OP("tensor", lambda e, b=b, c=c, hsub=hsub, tb=tb, bw_=bw_: e.matmul(
                            psF[:, 0:256],
                            bw_[:, c * 512 + hsub * 128: c * 512 + hsub * 128 + 128],
                            x1T[:, c * T + tb: c * T + tb + 256],
                            start=(c == 0), stop=(c == 7)),
                           wait=([(sLD, ld_after[k]), (sVE, sVE.v)] if c == 0 else None) if hsub == 0 else ([(sVE, sVE.v)] if c == 0 else None),
                           inc=sTE if c == 7 else None)
                    blk = hcc * 4 + hsub
                    OP("vector", lambda e, blk=blk: e.tensor_scalar(xbS[:, :], psF[:, 0:256], bs1_s[:, blk:blk + 1], None, ALU.add),
                       wait=[(sTE, sTE.v)], inc=sVE)
                    VGAP()
                    OP("scalar", lambda e: e.activation(out=gscrS[:, :], in_=xbS[:, :], func=AF.Square, scale=cst_s[:, 2:3]),
                       wait=[(sVE, sVE.v)], inc=sSC)
                    OP("vector", lambda e: e.tensor_mul(gscrS[:, :], gscrS[:, :], xbS[:, :]),
                       wait=[(sSC, sSC.v)], inc=sVE)
                    OP("vector", lambda e: e.tensor_add(gscrS[:, :], gscrS[:, :], xbS[:, :]), inc=sVE)
                    OP("scalar", lambda e: e.activation(out=small[:, 0:256], in_=gscrS[:, :], func=AF.Sigmoid, scale=cst_s[:, 3:4]),
                       wait=[(sVE, sVE.v)], inc=sSC)
                    OP("vector", lambda e, hsub=hsub: e.tensor_tensor(shhs[:, hsub * 256:(hsub + 1) * 256], small[:, 0:256], xbS[:, :], ALU.mult),
                       wait=[(sSC, sSC.v)], inc=sVE)
                    for sub in range(nsub):
                        for dn in range(2):
                            OP("tensor", lambda e, sub=sub, dn=dn, hsub=hsub, hcc=hcc, b2_=b2_: e.matmul(
                                psF[:, 512 + sub * 1024 + dn * 512: 512 + sub * 1024 + (dn + 1) * 512],
                                shhs[:, hsub * 256 + sub * 128: hsub * 256 + sub * 128 + 128],
                                b2_[:, hsub * D + dn * 512: hsub * D + (dn + 1) * 512],
                                start=False, stop=(hcc == 7 and hsub == 3), skip_group_check=True),
                               wait=[(sVE, sVE.v)] if (sub == 0 and dn == 0) else None,
                               inc=sTE if (hsub == 3 and sub == 1 and dn == 1) else None)
                te_after[k] = sTE.v
                if hcc == 7:
                    for sub in range(nsub):
                        OP("vector", lambda e, sub=sub: e.tensor_copy(out=small[:, 256:1280], in_=psF[:, 512 + sub * 1024:512 + (sub + 1) * 1024]),
                           wait=[(sTE, sTE.v)] if sub == 0 else [(sLD, sLD.v)], inc=sVE)
                        OP("sync", lambda e, b=b, tb=tb, sub=sub: e.dma_start(
                            out=routed_d.ap()[b * T + tb + sub * 128: b * T + tb + sub * 128 + 128, :], in_=small[:, 256:1280]),
                           wait=[(sVE, sVE.v)], inc=sLD, incn=16)
                if k + 2 < KK:
                    emit_load(k + 2, False)

        def bisection():
            OP("sync", lambda e: e.dma_start(out=probsT_d.ap()[0:1, 0:1], in_=probsT_d.ap()[0:1, 0:1]), inc=sLD, incn=16)
            OP("gpsimd", lambda e: e.collective_compute(
                "AllGather", ALU.bypass, replica_groups=[list(range(8))],
                ins=[probsT_d.ap()], outs=[ag_d.ap()]),
               wait=BAR(), inc=sCC)
            OP("sync", lambda e: e.dma_start(out=AG[:, :], in_=ag_d.ap().rearrange("a (h x) -> (a h) x", h=2)),
               wait=[(sCC, sCC.v)], inc=sLD, incn=16)
            OP("vector", lambda e: e.memset(lo8[:, :], 0.0), wait=[(sLD, sLD.v)], inc=sVE)
            OP("vector", lambda e: e.memset(hi8[:, :], 1.0), inc=sVE)
            OP("vector", lambda e: e.memset(mid8[:, :], 0.5), inc=sVE)
            for it in range(45):
                OP("tensor", lambda e: e.matmul(psF[:, 256:257], e16T_s[:, :], mid8[:, :], start=True, stop=True),
                   wait=[(sVE, sVE.v)], inc=sTE)
                OP("vector", lambda e: e.tensor_copy(out=small[:, 0:1], in_=psF[:, 256:257]), wait=[(sTE, sTE.v)], inc=sVE)
                VGAP()
                OP("vector", lambda e: e.tensor_scalar(AGs[:, :], AG[:, :], small[:, 0:1], 0.0, ALU.is_gt, ALU.add, accum_out=small[:, 1:2]), inc=sVE)
                VGAP()
                OP("tensor", lambda e: e.matmul(psF[0:8, 257:258], e16_s[:, :], small[:, 1:2], start=True, stop=True),
                   wait=[(sVE, sVE.v)], inc=sTE)
                OP("vector", lambda e: e.tensor_copy(out=cnt8[:, :], in_=psF[0:8, 257:258]), wait=[(sTE, sTE.v)], inc=sVE)
                VGAP()
                OP("vector", lambda e: e.tensor_scalar(cge8[:, :], cnt8[:, :], 8191.5, None, ALU.is_gt), inc=sVE)
                OP("vector", lambda e: e.tensor_scalar(cgeN[:, :], cnt8[:, :], 8191.5, None, ALU.is_le), inc=sVE)
                VGAP()
                OP("vector", lambda e: e.copy_predicated(lo8[:, :], cge8[:, :], mid8[:, :]), inc=sVE)
                OP("vector", lambda e: e.copy_predicated(hi8[:, :], cgeN[:, :], mid8[:, :]), inc=sVE)
                VGAP()
                OP("vector", lambda e: e.tensor_add(mid8t[:, :], lo8[:, :], hi8[:, :]), inc=sVE)
                VGAP()
                OP("vector", lambda e: e.tensor_scalar(mid8[:, :], mid8t[:, :], 0.5, None, ALU.mult), inc=sVE)
            OP("tensor", lambda e: e.transpose(psF[0:1, 258:266], lo8[:, :], identf_s[0:8, 0:8]),
               wait=[(sVE, sVE.v)], inc=sTE)
            OP("vector", lambda e: e.tensor_copy(out=loT_s[:, :], in_=psF[0:1, 258:266]), wait=[(sTE, sTE.v)], inc=sVE)
            OP("tensor", lambda e: e.matmul(psF[:, 272:280], onesrf_s[0:1, :], loT_s[:, :], start=True, stop=True),
               wait=[(sVE, sVE.v)], inc=sTE)
            OP("vector", lambda e: e.tensor_copy(out=loB_s[:, :], in_=psF[:, 272:280]), wait=[(sTE, sTE.v)], inc=sVE)
            VGAP()
            dump("lo8", lo8[:, :])

        def compaction():
            for tl in range(NB * NT):
                pa = tl * 8
                OP("vector", lambda e, pa=pa: e.tensor_tensor(small[:, 0:8], probs_all[:, pa:pa + 8], loB_s[:, :], ALU.is_gt),
                   wait=BAR() if tl == 0 else [(sGP, sGP.v)], inc=sVE)
                VGAP()
                OP("vector", lambda e: e.tensor_copy(out=smallb[:, 0:8], in_=small[:, 0:8]), inc=sVE)
                OP("vector", lambda e: e.tensor_copy(out=smallu8[:, 0:8], in_=small[:, 0:8]), inc=sVE)
                OP("tensor", lambda e: e.matmul(psF[:, 384:392], tril_s[:, :], smallb[:, 0:8], start=True, stop=True),
                   wait=[(sVE, sVE.v)], inc=sTE)
                OP("tensor", lambda e: e.matmul(psF[:, 392:400], ones128_s[:, :], smallb[:, 0:8], start=True, stop=True), inc=sTE)
                OP("vector", lambda e: e.tensor_add(small[:, 8:16], psF[:, 384:392], baseB[:, :]), wait=[(sTE, sTE.v)], inc=sVE)
                OP("vector", lambda e: e.tensor_add(baseB[:, :], baseB[:, :], psF[:, 392:400]), inc=sVE)
                OP("vector", lambda e: e.tensor_scalar(small[:, 16:24], small[:, 8:16], float(CAP) - 0.5, None, ALU.is_gt), inc=sVE)
                VGAP()
                OP("vector", lambda e: e.tensor_sub(small[:, 0:8], small[:, 0:8], small[:, 16:24]), inc=sVE)
                VGAP()
                OP("vector", lambda e: e.tensor_scalar(smallu8[:, 0:8], small[:, 0:8], 0.5, None, ALU.is_gt), inc=sVE)
                OP("vector", lambda e: e.tensor_add(small[:, 8:16], small[:, 8:16], ecap_s[:, :]), inc=sVE)
                VGAP()
                OP("vector", lambda e: e.tensor_copy(out=smallu[:, 0:8], in_=small[:, 8:16]), inc=sVE)
                VGAP()
                OP("vector", lambda e: e.select(smallu[:, 8:16], smallu8[:, 0:8], smallu[:, 0:8], trash_s[:, 0:1].to_broadcast([P, 8])), inc=sVE)
                OP("vector", lambda e, tl=tl: e.tensor_copy(out=smallu[:, 16:32].rearrange("p (e c) -> p e c", c=2)[:, :, 0], in_=tokid_s[:, tl:tl + 1].to_broadcast([P, 8])), inc=sVE)
                OP("vector", lambda e, pa=pa: e.tensor_copy(out=smallu[:, 16:32].rearrange("p (e c) -> p e c", c=2)[:, :, 1], in_=probs_all[:, pa:pa + 8].bitcast(U32)), inc=sVE)
                vv = sVE.v
                for e_ in range(8):
                    OP("gpsimd", lambda e, e_=e_: e.indirect_dma_start(
                        out=lists_d.ap(), out_offset=IndirectOffsetOnAxis(smallu[:, 8 + e_:9 + e_], 0),
                        in_=smallu[:, 16 + 2 * e_:18 + 2 * e_], in_offset=None),
                        wait=[(sVE, vv)] if e_ == 0 else None, inc=sGP, incn=16)

        def experts():
            scatter_prev = sGP.v
            scatter_prev2 = sGP.v
            for e_ in range(E):
                OP("gpsimd", lambda e, e_=e_: e.dma_start(out=b2e_b[0:1, :], in_=b2f_in.ap()[0:1, e_ * D:(e_ + 1) * D]),
                   wait=BAR(), inc=sGP, incn=16)
                b2ld = sGP.v
                for cc in range(NCAP):
                    OP("sync", lambda e, e_=e_, cc=cc: e.dma_start(
                        out=smallu[:, 2 * cc:2 * cc + 2], in_=lists_d.ap()[e_ * CAP + cc * 128: e_ * CAP + (cc + 1) * 128, :]),
                       wait=BAR() if cc == 0 else None, inc=sLD, incn=16)
                ldl = sLD.v
                for cc in range(NCAP):
                    pb = psB if cc % 2 == 0 else psB2
                    OP("gpsimd", lambda e, cc=cc: e.indirect_dma_start(
                        out=tokrows[:, :], out_offset=None,
                        in_=x1b_d.ap(), in_offset=IndirectOffsetOnAxis(smallu[:, 2 * cc:2 * cc + 1], 0)),
                        wait=[(sLD, ldl), (sTE, sTE.v)], inc=sGP, incn=16)
                    for c in range(8):
                        OP("tensor", lambda e, c=c, pb=pb: e.transpose(pb[:, c * 128:(c + 1) * 128], tokrows[:, c * 128:(c + 1) * 128], IDB),
                           wait=[(sGP, sGP.v), (sVE, sVE.v)] if c == 0 else None, inc=sTE if c == 7 else None)
                    for c in range(8):
                        OP("vector", lambda e, c=c, cc=cc, pb=pb: e.tensor_copy(out=tokT[:, c * CAP + cc * 128: c * CAP + (cc + 1) * 128], in_=pb[:, c * 128:(c + 1) * 128]),
                           wait=[(sTE, sTE.v)] if c == 0 else None, inc=sVE if c == 7 else None)
                if e_ == 0:
                    dump("tokt0", tokT[:, 0:CAP])
                srcW1 = wffn_d.ap()[e_ * NWE: e_ * NWE + D * HID].rearrange("(r h) -> r h", h=HID)
                srcW2 = wffn_d.ap()[e_ * NWE + D * HID: e_ * NWE + NWE].rearrange("(h d) -> h d", d=D)
                bufs = [(ws1sl, w2sl), (wstB, w2tB)]
                ld_after = {}
                te_after = {}
                KK = 5 * 8

                def emit_load(k, srcW1=srcW1, srcW2=srcW2, ld_after=ld_after, te_after=te_after, bufs=bufs):
                    hcc = k % 8
                    bw_, b2_ = bufs[k % 2]
                    w = [(sTE, te_after[k - 2])] if k >= 2 else [(sTE, sTE.v)]
                    for c in range(8):
                        OP("sync", lambda e, c=c, hcc=hcc, bw_=bw_, srcW1=srcW1: e.dma_start(
                            out=bw_[:, c * 512:(c + 1) * 512], in_=srcW1[c * 128:(c + 1) * 128, hcc * 512:(hcc + 1) * 512]),
                           wait=w if c == 0 else None, inc=sLD, incn=16)
                    OP("sync", lambda e, hcc=hcc, b2_=b2_, srcW2=srcW2: e.dma_start(
                        out=b2_[:, :].rearrange("p (s d) -> p s d", s=4), in_=srcW2[hcc * 512:(hcc + 1) * 512, :].rearrange("(s p) d -> p s d", p=128)),
                       inc=sLD, incn=16)
                    ld_after[k] = sLD.v

                emit_load(0)
                emit_load(1)
                for k in range(KK):
                    sc, hcc = divmod(k, 8)
                    sl0 = sc * 256
                    bw_, b2_ = bufs[k % 2]
                    if hcc == 0:
                        for sub in range(2):
                            for dn in range(2):
                                OP("tensor", lambda e, e_=e_, sub=sub, dn=dn: e.matmul(
                                    psF[:, 512 + sub * 1024 + dn * 512: 512 + sub * 1024 + (dn + 1) * 512],
                                    onesrb_s[0:1, :], b2e_b[0:1, dn * 512:(dn + 1) * 512],
                                    start=True, stop=False, skip_group_check=True),
                                   wait=[(sVE, sVE.v), (sGP, b2ld)] if (sub == 0 and dn == 0) else None)
                    for hsub in range(4):
                        for c in range(8):
                            OP("tensor", lambda e, c=c, hsub=hsub, sl0=sl0, bw_=bw_: e.matmul(
                                psF[:, 0:256],
                                bw_[:, c * 512 + hsub * 128: c * 512 + hsub * 128 + 128],
                                tokT[:, c * CAP + sl0: c * CAP + sl0 + 256],
                                start=(c == 0), stop=(c == 7)),
                               wait=([(sLD, ld_after[k]), (sVE, sVE.v)] if c == 0 else None) if hsub == 0 else ([(sVE, sVE.v)] if c == 0 else None),
                               inc=sTE if c == 7 else None)
                        blk = e_ * 32 + hcc * 4 + hsub
                        OP("vector", lambda e, blk=blk: e.tensor_scalar(xbE[:, :], psF[:, 0:256], b1f_s[:, blk:blk + 1], None, ALU.add),
                           wait=[(sTE, sTE.v)], inc=sVE)
                        VGAP()
                        OP("scalar", lambda e: e.activation(out=gscrE[:, :], in_=xbE[:, :], func=AF.Square, scale=cst_s[:, 2:3]),
                           wait=[(sVE, sVE.v)], inc=sSC)
                        OP("vector", lambda e: e.tensor_mul(gscrE[:, :], gscrE[:, :], xbE[:, :]), wait=[(sSC, sSC.v)], inc=sVE)
                        OP("vector", lambda e: e.tensor_add(gscrE[:, :], gscrE[:, :], xbE[:, :]), inc=sVE)
                        OP("scalar", lambda e: e.activation(out=small[:, 0:256], in_=gscrE[:, :], func=AF.Sigmoid, scale=cst_s[:, 3:4]),
                           wait=[(sVE, sVE.v)], inc=sSC)
                        OP("vector", lambda e, hsub=hsub: e.tensor_tensor(hsbE[:, hsub * 256:(hsub + 1) * 256], small[:, 0:256], xbE[:, :], ALU.mult),
                           wait=[(sSC, sSC.v)], inc=sVE)
                        for sub in range(2):
                            for dn in range(2):
                                OP("tensor", lambda e, sub=sub, dn=dn, hsub=hsub, hcc=hcc, b2_=b2_: e.matmul(
                                    psF[:, 512 + sub * 1024 + dn * 512: 512 + sub * 1024 + (dn + 1) * 512],
                                    hsbE[:, hsub * 256 + sub * 128: hsub * 256 + sub * 128 + 128],
                                    b2_[:, hsub * D + dn * 512: hsub * D + (dn + 1) * 512],
                                    start=False, stop=(hcc == 7 and hsub == 3), skip_group_check=True),
                                   wait=[(sVE, sVE.v)] if (sub == 0 and dn == 0) else None,
                                   inc=sTE if (hsub == 3 and sub == 1 and dn == 1) else None)
                    te_after[k] = sTE.v
                    if hcc == 7:
                        if e_ == 0 and sc == 0:
                            dump("hsb0", hsbE[:, :])
                        for sub in range(2):
                            call = sc * 2 + sub
                            gbuf = grows if sub == 0 else grows2
                            OP("vector", lambda e, sub=sub, call=call, gbuf=gbuf: e.tensor_scalar(
                                gbuf[:, :], psF[:, 512 + sub * 1024: 512 + (sub + 1) * 1024],
                                smallu[:, 2 * call + 1: 2 * call + 2].bitcast(F32), None, ALU.mult),
                               wait=[(sTE, sTE.v), (sGP, scatter_prev2)] if sub == 0 else [(sGP, scatter_prev2)], inc=sVE)
                            OP("gpsimd", lambda e, call=call, gbuf=gbuf: e.indirect_dma_start(
                                out=routed_d.ap(), out_offset=IndirectOffsetOnAxis(smallu[:, 2 * call:2 * call + 1], 0),
                                in_=gbuf[:, :], in_offset=None, compute_op=ALU.add),
                                wait=[(sVE, sVE.v), (sGP, scatter_prev)], inc=sGP, incn=16)
                        scatter_prev2 = sGP.v
                    if k + 2 < KK:
                        emit_load(k + 2)
                scatter_prev = sGP.v

        def final():
            dump("ws1d", wffn_d.ap()[WS1B:WS1B + 128 * HID].rearrange("(p n) -> p n", p=128))
            dump("ws2d", wffn_d.ap()[WS2B:WS2B + 128 * D].rearrange("(p n) -> p n", p=128))
            dump("w1e0", wffn_d.ap()[0:128 * HID].rearrange("(p n) -> p n", p=128))
            dump("probsT0", probsT_d.ap()[:, :])
            dump("agd", ag_d.ap()[:, :])
            dump("loB", loB_s[:, :])
            dump("lists0", lists_d.ap()[0:10240, :])
            dump("x1ball", x1b_d.ap()[0:8192, :])
            dump("routed0", routed_d.ap()[0:256, :])
            for tl in range(NB * NT):
                t0 = tl * 128
                OP("sync", lambda e, t0=t0: e.dma_start(out=ust[:, :], in_=routed_d.ap()[t0:t0 + 128, :]),
                   wait=BAR() if tl == 0 else [(sVE, sVE.v)], inc=sLD, incn=16)
                OP("sync", lambda e, t0=t0: e.dma_start(out=x0st[:, :], in_=x1f_d.ap()[t0:t0 + 128, :]), inc=sLD, incn=16)
                OP("vector", lambda e: e.tensor_add(scrap[:, :], ust[:, :], x0st[:, :]), wait=[(sLD, sLD.v)], inc=sVE)
                VGAP()
                OP("vector", lambda e: e.tensor_reduce(small[:, 600:601], scrap[:, :], AX.X, ALU.add), inc=sVE)
                VGAP()
                OP("vector", lambda e: e.tensor_scalar(small[:, 601:602], small[:, 600:601], 1.0 / D, None, ALU.mult), inc=sVE)
                VGAP()
                OP("vector", lambda e: e.tensor_scalar(x0st[:, :], scrap[:, :], small[:, 601:602], None, ALU.subtract), inc=sVE)
                OP("scalar", lambda e: e.activation(out=scrap[:, :], in_=x0st[:, :], func=AF.Square, accum_out=small[:, 602:603]),
                   wait=[(sVE, sVE.v)], inc=sSC)
                OP("vector", lambda e: e.tensor_scalar(small[:, 603:604], small[:, 602:603], 1.0 / D, None, ALU.mult),
                   wait=[(sSC, sSC.v)], inc=sVE)
                OP("scalar", lambda e: e.activation(out=small[:, 604:605], in_=small[:, 603:604], func=AF.Sqrt, bias=cst_s[:, 1:2]),
                   wait=[(sVE, sVE.v)], inc=sSC)
                OP("vector", lambda e: e.reciprocal(out=small[:, 605:606], in_=small[:, 604:605]), wait=[(sSC, sSC.v)], inc=sVE)
                VGAP()
                OP("vector", lambda e: e.tensor_scalar(scrap[:, :], x0st[:, :], small[:, 605:606], None, ALU.mult), inc=sVE)
                VGAP()
                OP("sync", lambda e, tl=tl, t0=t0: e.dma_start(
                    out=y_out.ap()[tl // NT, (t0 % T):(t0 % T) + 128, :], in_=scrap[:, :]),
                   wait=[(sVE, sVE.v)], inc=sST, incn=16)
            OP("sync", lambda e: e.dma_start(out=probsT_d.ap()[0:1, 0:1], in_=probsT_d.ap()[0:1, 0:1]),
               wait=[(sST, sST.v)], inc=sLD, incn=16)

        import os as _os
        KS = int(_os.environ.get("KSTAGE", "99"))
        if KS >= 2:
            phaseA(0)
        if KS >= 3:
            gate_batch(0)
            shared_batch(0)
        if KS >= 4:
            phaseA(1)
            gate_batch(1)
        if KS >= 5:
            bisection()
            shared_batch(1)
        if KS >= 6:
            compaction()
        if KS >= 7:
            experts()
        if KS >= 8:
            final()

        pr.run(en(nc.Block()))

    return nc, dbg_outs


def stage_inputs(inputs, core):
    c = host_constants()
    m = {}
    m.update(c)
    x = inputs["L0_attn_x"]
    m["x"] = np.ascontiguousarray(np.asarray(x)[2 * core:2 * core + 2]).astype(np.float32, copy=False)
    m["Wi"] = np.asarray(inputs["L0_attn_Wi"], dtype=np.float32)
    m["Wp"] = np.asarray(inputs["L0_attn_Wp"], dtype=np.float32)
    m["Wo2"] = np.ascontiguousarray(np.asarray(inputs["L0_attn_Wo"], dtype=np.float32)[0])
    pal = np.asarray(inputs["L0_attn_palette"], dtype=np.float32)
    m["palT"] = np.ascontiguousarray(pal.transpose(1, 2, 0).reshape(256, D))
    w1 = np.asarray(inputs["L0_attn_W1"], dtype=np.float32)
    m["W1full"] = np.ascontiguousarray(w1)
    m["b1a"] = np.ascontiguousarray(np.asarray(inputs["L0_attn_b1"], dtype=np.float32).reshape(64, 1))
    m["W2a"] = np.asarray(inputs["L0_attn_W2"], dtype=np.float32)
    m["b2a"] = np.ascontiguousarray(np.asarray(inputs["L0_attn_b2"], dtype=np.float32).reshape(64, 1))
    m["Wc"] = np.asarray(inputs["L0_attn_Wc"], dtype=np.float32)
    m["bc_row"] = np.ascontiguousarray(np.asarray(inputs["L0_attn_bc"], dtype=np.float32).reshape(1, 2))
    m["Wm"] = np.asarray(inputs["L0_attn_Wm"], dtype=np.float32)
    m["bm_row"] = np.ascontiguousarray(np.asarray(inputs["L0_attn_bm"], dtype=np.float32).reshape(1, 1))
    m["gate"] = np.asarray(inputs["L0_ffn_gate"], dtype=np.float32)
    m["W1f"] = np.asarray(inputs["L0_ffn_W1"], dtype=np.float32)
    m["W2f"] = np.asarray(inputs["L0_ffn_W2"], dtype=np.float32)
    b1f = np.asarray(inputs["L0_ffn_b1"], dtype=np.float32)
    m["b1f_cols"] = np.ascontiguousarray(b1f.reshape(E, 32, 128).transpose(2, 0, 1).reshape(128, E * 32))
    m["b2f_row"] = np.ascontiguousarray(np.asarray(inputs["L0_ffn_b2"], dtype=np.float32).reshape(1, E * D))
    m["Ws1"] = np.asarray(inputs["L0_ffn_Ws1"], dtype=np.float32)
    m["Ws2"] = np.asarray(inputs["L0_ffn_Ws2"], dtype=np.float32)
    bs1 = np.asarray(inputs["L0_ffn_bs1"], dtype=np.float32)
    m["bs1_cols"] = np.ascontiguousarray(bs1.reshape(32, 128).T)
    m["bs2_row"] = np.ascontiguousarray(np.asarray(inputs["L0_ffn_bs2"], dtype=np.float32).reshape(1, D))
    return m


def kernel(**inputs):
    nc, _ = build_program()
    in_maps = [stage_inputs(inputs, c) for c in range(8)]
    res = run_bass_kernel_spmd(nc, in_maps, core_ids=list(range(8)))
    out = np.concatenate([res.results[c]["out"] for c in range(8)], axis=0)
    return out.astype(np.float32)


# version 05
# speedup vs baseline: 1.2501x; 1.0982x over previous
import numpy as np
import ml_dtypes
import concourse.bass as bass
import concourse.mybir as mybir
from concourse.bass_core_rust import IndirectOffsetOnAxis
from concourse.bass_utils import run_bass_kernel_spmd

F32 = mybir.dt.float32
BF16 = mybir.dt.bfloat16
U32 = mybir.dt.uint32
U8 = mybir.dt.uint8
AF = mybir.ActivationFunctionType
ALU = mybir.AluOpType
AX = mybir.AxisListType

NB = 2
T = 4096
D = 1024
NT = 32
K15 = 15
E = 8
HID = 4096
CAP = 1280
NCAP = CAP // 128
TRASH = E * CAP
LROWS = 128 * ((TRASH + 128 + 127) // 128)
ZPI = 2 * T
ZX1 = NB * T
GC = 0.044715
SQGC = GC ** 0.5
GS = 1.5957691216057308
MASKV = -1e30
NWE = D * HID + HID * D
WS1B = E * NWE
WS2B = WS1B + D * HID


class Sem:
    def __init__(self, nc, name):
        self.h = nc.alloc_semaphore(name)
        self.v = 0


class Prog:
    def __init__(self, nc):
        self.nc = nc
        self.q = {k: [] for k in ("sync", "vector", "scalar", "tensor", "gpsimd")}

    def op(self, engq, fn, wait=None, inc=None, incn=1):
        waits = []
        if wait:
            for w in wait:
                waits.append((w[0].h, w[1]) if isinstance(w, tuple) else (w.h, w.v))
        incs = None
        if inc is not None:
            inc.v += incn
            incs = (inc.h, incn)

        def emit(eng, fn=fn, waits=waits, incs=incs):
            for (h, v) in waits:
                if v > 0:
                    eng.wait_ge(h, v)
            ins = fn(eng)
            if incs is not None:
                ins.then_inc(incs[0], incs[1])
        self.q[engq].append(emit)

    def run(self, block):
        for name in ("sync", "vector", "scalar", "tensor", "gpsimd"):
            ops = self.q[name]

            def body(eng, ops=ops):
                for f in ops:
                    f(eng)
            getattr(block, name)(body)


def host_constants():
    c = {}
    half = D // 2
    inv_freq = 1.0 / 10000.0 ** (np.arange(half, dtype=np.float64) / half)
    ang = np.arange(T, dtype=np.float64)[:, None] * inv_freq[None, :]
    c["rope_cos"] = np.ascontiguousarray(np.cos(ang).T.astype(np.float32))
    c["rope_sin"] = np.ascontiguousarray(np.sin(ang).T.astype(np.float32))
    m = np.zeros((128, 4, 512), dtype=np.uint8)
    for r in range(4):
        for p in range(128):
            m[p, r, 128 * r + p + 1:] = 1
    c["cmask_u8"] = np.ascontiguousarray(m.reshape(128, 4 * 512))
    c["ident_bf"] = np.eye(128, dtype=ml_dtypes.bfloat16)
    c["ident_f"] = np.eye(128, dtype=np.float32)
    c["ones_row_f"] = np.ones((1, 128), dtype=np.float32)
    c["ones_col_f"] = np.ones((128, 1), dtype=np.float32)
    c["ones_row_bf"] = np.ones((1, 128), dtype=ml_dtypes.bfloat16)
    c["trilS_bf"] = (np.arange(128)[:, None] < np.arange(128)[None, :]).astype(ml_dtypes.bfloat16)
    c["ones128_bf"] = np.ones((128, 128), dtype=ml_dtypes.bfloat16)
    p = np.arange(128)
    c["e16_f"] = ((p[:, None] // 2) % 8 == np.arange(8)[None, :]).astype(np.float32)
    c["e16T_f"] = np.ascontiguousarray(c["e16_f"].T)
    c["iota16"] = np.tile(np.arange(16, dtype=np.float32), (128, K15))
    c["pcol_f"] = p.astype(np.float32).reshape(128, 1)
    c["gcol_u"] = (T + np.arange(NT)[None, :] * 128 + p[:, None]).astype(np.uint32)
    c["tokid_u"] = (np.arange(NB * NT)[None, :] * 128 + p[:, None]).astype(np.uint32)
    c["trash_u"] = (TRASH + p).astype(np.uint32).reshape(128, 1)
    c["ecap_f"] = np.tile((np.arange(8) * CAP).astype(np.float32), (128, 1))
    return c


def build_program(dbg=()):
    import os as _os
    nc = bass.Bass("TRN2", target_bir_lowering=False,
                   detect_race_conditions=_os.environ.get("SIM_RELAX") != "1")
    P = 128
    dbg = dict(dbg)
    dbg_outs = {}

    def din(name, shape, dt=F32):
        return nc.dram_tensor(name, list(shape), dt, kind="ExternalInput")

    x_in = din("x", [NB, T, D])
    wi_in = din("Wi", [D, D]); wp_in = din("Wp", [D, D]); wo_in = din("Wo2", [D, D])
    palT_in = din("palT", [256, D])
    w1e_in = din("W1full", [17, 64])
    b1a_in = din("b1a", [64, 1]); w2a_in = din("W2a", [64, 64]); b2a_in = din("b2a", [64, 1])
    wc_in = din("Wc", [64, 2]); bc_in = din("bc_row", [1, 2])
    wm_in = din("Wm", [64, 1]); bm_in = din("bm_row", [1, 1])
    gate_in = din("gate", [D, E])
    w1f_in = din("W1f", [E, D, HID]); w2f_in = din("W2f", [E, HID, D])
    b1f_in = din("b1f_cols", [128, E * 32])
    b2f_in = din("b2f_row", [1, E * D])
    ws1_in = din("Ws1", [D, HID]); ws2_in = din("Ws2", [HID, D])
    bs1_in = din("bs1_cols", [128, 32])
    bs2_in = din("bs2_row", [1, D])
    cos_in = din("rope_cos", [D // 2, T]); sin_in = din("rope_sin", [D // 2, T])
    cmask_in = din("cmask_u8", [128, 4 * 512], U8)
    identb_in = din("ident_bf", [128, 128], BF16)
    identf_in = din("ident_f", [128, 128])
    onesrf_in = din("ones_row_f", [1, 128]); onescf_in = din("ones_col_f", [128, 1])
    onesrb_in = din("ones_row_bf", [1, 128], BF16)
    ones128_in = din("ones128_bf", [128, 128], BF16)
    tril_in = din("trilS_bf", [128, 128], BF16)
    e16_in = din("e16_f", [128, 8]); e16T_in = din("e16T_f", [8, 128])
    iota16_in = din("iota16", [128, K15 * 16])
    pcol_in = din("pcol_f", [128, 1])
    gcol_in = din("gcol_u", [128, NT], U32)
    tokid_in = din("tokid_u", [128, NB * NT], U32)
    trash_in = din("trash_u", [128, 1], U32)
    ecap_in = din("ecap_f", [128, 8])

    y_out = nc.dram_tensor("out", [NB, T, D], F32, kind="ExternalOutput")

    PI = nc.dram_tensor("PI", [2 * T + 1, D], BF16)
    x1f_d = nc.dram_tensor("x1f", [NB * T, D], F32)
    x1b_d = nc.dram_tensor("x1b", [NB * T + 1, D], BF16)
    delta_d = nc.dram_tensor("deltab", [NB * NT, 128 * K15], F32)
    probsT_d = nc.dram_tensor("probsT", [E, NB * T], F32)
    ag_d = nc.dram_tensor("agout", [8 * E, NB * T], F32)
    lists_d = nc.dram_tensor("lists", [LROWS, 2], U32)
    routed_d = nc.dram_tensor("routed", [NB * T + 1, D], F32)
    wffn_d = nc.dram_tensor("wffnb", [E * NWE + 2 * D * HID], BF16)

    for name, (shape, dt) in dbg.items():
        dbg_outs[name] = nc.dram_tensor("dbg_" + name, list(shape), dt, kind="ExternalOutput")

    from contextlib import ExitStack
    with ExitStack() as ctx:
        en = ctx.enter_context

        def sbt(name, shape, dt):
            return en(nc.sbuf_tensor(name, list(shape), dt))

        wi_b = sbt("wi_b", [P, 8 * D], BF16)
        wp_b = sbt("wp_b", [P, 8 * D], BF16)
        wo_b = sbt("wo_b", [P, 8 * D], BF16)
        palT_b = sbt("palT_b", [P, 2 * D], BF16)
        gate_b = sbt("gate_b", [P, 8 * E], BF16)
        w1e_b = sbt("w1e_b", [17, 64], BF16)
        w2a_b = sbt("w2a_b", [64, 64], BF16)
        wc_b = sbt("wc_b", [64, 2], BF16)
        wm_b = sbt("wm_b", [64, 1], BF16)
        b1a_s = sbt("b1a_s", [64, 1], F32)
        b2a_s = sbt("b2a_s", [64, 1], F32)
        bc_r = sbt("bc_r", [1, 2], F32)
        bm_r = sbt("bm_r", [1, 1], F32)
        b2e_b = sbt("b2e_b", [1, D], BF16)
        bs2_b = sbt("bs2_b", [1, D], BF16)
        b1f_s = sbt("b1f_s", [P, E * 32], F32)
        bs1_s = sbt("bs1_s", [P, 32], F32)
        identb_s = sbt("identb_s", [P, 128], BF16)
        identf_s = sbt("identf_s", [P, 128], F32)
        onesrf_s = sbt("onesrf_s", [1, 128], F32)
        onescf_s = sbt("onescf_s", [P, 1], F32)
        onesrb_s = sbt("onesrb_s", [1, 128], BF16)
        ones128_s = sbt("ones128_s", [P, 128], BF16)
        tril_s = sbt("tril_s", [P, 128], BF16)
        e16_s = sbt("e16_s", [P, 8], F32)
        e16T_s = sbt("e16T_s", [8, 128], F32)
        iota16_s = sbt("iota16_s", [P, K15 * 16], F32)
        pcol_s = sbt("pcol_s", [P, 1], F32)
        gcol_s = sbt("gcol_s", [P, NT], U32)
        tokid_s = sbt("tokid_s", [P, NB * NT], U32)
        trash_s = sbt("trash_s", [P, 1], U32)
        ecap_s = sbt("ecap_s", [P, 8], F32)
        cmask_s = sbt("cmask_s", [P, 4 * 512], U8)
        zrow_u = sbt("zrow_u", [P, K15], U32)
        neg_c = sbt("neg_c", [P, 1], F32)
        cst_s = sbt("cst_s", [P, 4], F32)
        bcm_b = sbt("bcm_b", [1, 48], BF16)

        idxbuf = sbt("idxbuf", [P, NT * 16], U32)
        keepf = sbt("keepf", [P, NT * K15], F32)
        probs_all = sbt("probs_all", [P, NB * NT * 8], F32)
        relb = sbt("relb", [33, 128 * K15], BF16)
        h1b = sbt("h1b", [64, 128 * K15], BF16)
        h2b = sbt("h2b", [64, 128 * K15], BF16)
        lo8 = sbt("lo8", [8, 1], F32)
        hi8 = sbt("hi8", [8, 1], F32)
        mid8 = sbt("mid8", [8, 1], F32)
        mid8t = sbt("mid8t", [8, 1], F32)
        cnt8 = sbt("cnt8", [8, 1], F32)
        cge8 = sbt("cge8", [8, 1], U8)
        cgeN = sbt("cgeN", [8, 1], U8)
        loT_s = sbt("loT_s", [1, 8], F32)
        loB_s = sbt("loB_s", [P, 8], F32)
        baseB = sbt("baseB", [P, 8], F32)
        small = sbt("small", [P, 1280], F32)
        smallb = sbt("smallb", [P, 512], BF16)
        smallu = sbt("smallu", [P, 64], U32)
        smallu8 = sbt("smallu8", [P, 32], U8)

        arena = sbt("arena", [P, 122880], U8)
        A0 = nc.lookup_mloc(arena).addr

        def at(name, shape, dt, off):
            return nc.alloc_sbuf_tensor_at(name, list(shape), dt, offset=A0 + off)

        phat_T = at("phat_T", [P, 8 * T], BF16, 0)
        packT = at("packT", [P, 8 * T], BF16, 0)
        x1T = at("x1T", [P, 8 * T], BF16, 0)
        tokrows = at("tokrows", [P, D], BF16, 0)
        tokT = at("tokT", [P, 8 * CAP], BF16, 2048)
        ws1sl = at("ws1sl", [P, 8 * 512], BF16, 22528)
        w2sl = at("w2sl", [P, 4 * D], BF16, 30720)
        hsbE = at("hsbE", [P, 4 * 256], BF16, 38912)
        grows = at("grows", [P, D], F32, 40960)
        grows2 = at("grows2", [P, D], F32, 47104)
        xbE = at("xbE", [P, 512], F32, 51200)
        wstB = at("wstB", [P, HID], BF16, 81920)
        w2tB = at("w2tB", [P, HID], BF16, 73728)
        gscrE = at("gscrE", [P, 512], F32, 45056)
        logits = at("logits", [P, T], F32, 65536)
        normP_bc = at("normP_bc", [P, T], F32, 81920)
        pack = at("pack", [P, 16 * D], BF16, 65536)
        AG = at("AG", [P, 4096], F32, 65536)
        AGs = at("AGs", [P, 4096], F32, 81920)
        ST = 98304
        xstage = at("xstage", [P, D], F32, ST)
        rows_f = at("rows_f", [P, D], F32, ST)
        sqs = at("sqs", [P, D], F32, ST)
        xb_st = at("xb_st", [P, D], BF16, ST + 4096)
        xT_t = at("xT_t", [P, 8 * 128], BF16, ST + 6144)
        riT = at("riT", [P, 8 * 128], F32, ST + 8192)
        riTb = at("riTb", [P, 8 * 128], BF16, ST + 12288)
        cs_st = at("cs_st", [P, 8 * 128], F32, ST + 14336)
        rows_b = at("rows_b", [P, D], BF16, ST + 18432)
        sqs2 = at("sqs2", [P, D], F32, ST + 20480)
        x0st = at("x0st", [P, D], F32, ST)
        ust = at("ust", [P, D], F32, ST + 4096)
        scrap = at("scrap", [P, D], F32, ST + 8192)
        x1bst = at("x1bst", [P, D], BF16, ST + 12288)
        vt_b = at("vt_b", [P, 8 * 128], BF16, ST + 14336)
        ct_b = at("ct_b", [P, 256], BF16, ST + 16384)
        pwk = at("pwk", [P, 512], F32, ST + 16896)
        hxw = at("hxw", [P, 3 * 240], F32, ST + 18944)
        c256 = at("c256", [P, 256], F32, ST + 21824)
        t256 = at("t256", [P, 256], F32, ST + 22848)
        bx1row = at("bx1row", [P, D], BF16, ST)
        bws1 = at("bws1", [P, 8 * 512], BF16, ST + 2048)
        bw2 = at("bw2", [P, 4 * D], BF16, ST + 10240)
        shhs = at("shhs", [P, 4 * 256], BF16, ST + 18432)
        gscrS = at("gscrS", [P, 512], F32, ST + 20480)
        xbS = at("xbS", [P, 512], F32, ST + 22528)

        psF = en(nc.psum_tensor("psF", [P, 3072], F32))
        psB = en(nc.psum_tensor("psB", [P, 1024], BF16))
        psB2 = en(nc.psum_tensor("psB2", [P, 1024], BF16))

        sLD = Sem(nc, "sLD")
        sGP = Sem(nc, "sGP")
        sTE = Sem(nc, "sTE")
        sVE = Sem(nc, "sVE")
        sSC = Sem(nc, "sSC")
        sCC = Sem(nc, "sCC")
        sST = Sem(nc, "sST")
        ALL = [sLD, sGP, sTE, sVE, sSC, sST]

        pr = Prog(nc)
        OP = pr.op

        def VGAP():
            OP("vector", lambda e: e.tensor_copy(out=small[:, 519:520], in_=cst_s[:, 0:1]))

        def BAR():
            return [(s, s.v) for s in ALL]

        def dump(name, src_ap, wait=None):
            if name in dbg_outs:
                o = dbg_outs[name]
                OP("sync", lambda e, o=o, s=src_ap: e.dma_start(out=o.ap(), in_=s),
                   wait=wait if wait is not None else BAR(), inc=sLD, incn=16)

        def ld(dst, src):
            OP("sync", lambda e, d=dst, s=src: e.dma_start(out=d, in_=s), inc=sLD, incn=16)

        def ldc(dst, src):
            OP("gpsimd", lambda e, d=dst, s=src: e.dma_start(out=d, in_=s), inc=sGP, incn=16)

        for c in range(8):
            ldc(wi_b[:, c * D:(c + 1) * D], wi_in.ap()[c * 128:(c + 1) * 128, :])
            ldc(wp_b[:, c * D:(c + 1) * D], wp_in.ap()[c * 128:(c + 1) * 128, :])
            ldc(wo_b[:, c * D:(c + 1) * D], wo_in.ap()[c * 128:(c + 1) * 128, :])
        ldc(palT_b[:, 0:D], palT_in.ap()[0:128, :])
        ldc(palT_b[:, D:2 * D], palT_in.ap()[128:256, :])
        for c in range(8):
            ldc(gate_b[:, c * E:(c + 1) * E], gate_in.ap()[c * 128:(c + 1) * 128, :])
        ldc(w1e_b[:, :], w1e_in.ap())
        ldc(w2a_b[:, :], w2a_in.ap()); ldc(wc_b[:, :], wc_in.ap()); ldc(wm_b[:, :], wm_in.ap())
        ldc(bs2_b[:, :], bs2_in.ap())
        ld(b1a_s[:, :], b1a_in.ap()); ld(b2a_s[:, :], b2a_in.ap())
        ld(bc_r[:, :], bc_in.ap()); ld(bm_r[:, :], bm_in.ap())
        ld(b1f_s[:, :], b1f_in.ap()); ld(bs1_s[:, :], bs1_in.ap())
        ld(identb_s[:, :], identb_in.ap()); ld(identf_s[:, :], identf_in.ap())
        ld(onesrf_s[:, :], onesrf_in.ap()); ld(onescf_s[:, :], onescf_in.ap())
        ld(onesrb_s[:, :], onesrb_in.ap()); ld(ones128_s[:, :], ones128_in.ap())
        ld(tril_s[:, :], tril_in.ap())
        ld(e16_s[:, :], e16_in.ap()); ld(e16T_s[:, :], e16T_in.ap())
        ld(iota16_s[:, :], iota16_in.ap()); ld(pcol_s[:, :], pcol_in.ap())
        ld(gcol_s[:, :], gcol_in.ap()); ld(tokid_s[:, :], tokid_in.ap())
        ld(trash_s[:, :], trash_in.ap()); ld(ecap_s[:, :], ecap_in.ap())
        ld(cmask_s[:, :], cmask_in.ap())
        ld0, gp0 = sLD.v, sGP.v

        OP("vector", lambda e: e.memset(zrow_u[:, :], ZPI), wait=[(sGP, gp0), (sLD, ld0)], inc=sVE)
        OP("vector", lambda e: e.memset(neg_c[:, :], MASKV), inc=sVE)
        OP("vector", lambda e: e.memset(cst_s[:, 0:1], 1e-24), inc=sVE)
        OP("vector", lambda e: e.memset(cst_s[:, 1:2], 1e-5), inc=sVE)
        OP("vector", lambda e: e.memset(cst_s[:, 2:3], SQGC), inc=sVE)
        OP("vector", lambda e: e.memset(cst_s[:, 3:4], GS), inc=sVE)
        OP("vector", lambda e: e.memset(baseB[:, :], 0.0), inc=sVE)
        OP("vector", lambda e: e.tensor_copy(out=bcm_b[0:1, 0:15], in_=bc_r[0:1, 0:1].to_broadcast([1, 15])), inc=sVE)
        OP("vector", lambda e: e.tensor_copy(out=bcm_b[0:1, 15:30], in_=bc_r[0:1, 1:2].to_broadcast([1, 15])), inc=sVE)
        OP("vector", lambda e: e.tensor_copy(out=bcm_b[0:1, 30:45], in_=bm_r[0:1, 0:1].to_broadcast([1, 15])), inc=sVE)
        OP("vector", lambda e: e.memset(small[0:1, 0:1024], 0.0), inc=sVE)
        OP("gpsimd", lambda e: e.dma_start(out=PI.ap()[ZPI:ZPI + 1, :], in_=small[0:1, 0:1024]), wait=BAR(), inc=sGP, incn=16)
        OP("gpsimd", lambda e: e.dma_start(out=x1b_d.ap()[ZX1:ZX1 + 1, :], in_=small[0:1, 0:1024]), inc=sGP, incn=16)
        LR128 = LROWS // 128
        OP("vector", lambda e: e.memset(smallu[:, 1:2], 0), inc=sVE)
        OP("vector", lambda e: e.memset(smallu[:, 0:1], ZX1), inc=sVE)
        OP("sync", lambda e: e.dma_start(
            out=lists_d.ap().rearrange("(p r) c -> p (r c)", p=128),
            in_=smallu[:, 0:2].unsqueeze(1).to_broadcast([P, LR128, 2])),
            wait=BAR(), inc=sLD, incn=16)

        gp_p0 = sGP.v
        conv_jobs = []
        for e_ in range(E):
            conv_jobs.append((w1f_in.ap()[e_], e_ * NWE, D, HID))
            conv_jobs.append((w2f_in.ap()[e_], e_ * NWE + D * HID, HID, D))
        conv_jobs.append((ws1_in.ap(), WS1B, D, HID))
        conv_jobs.append((ws2_in.ap(), WS2B, HID, D))
        wff = wffn_d.ap()
        conv_chunks = []
        for (src, base, nr, ncol) in conv_jobs:
            rb = 512 if ncol == HID else 1024
            for r0 in range(0, nr, rb):
                def mk(src=src, base=base, r0=r0, rb=rb, ncl=ncol):
                    return lambda e: e.dma_start(
                        out=wff[base + r0 * ncl: base + (r0 + rb) * ncl].rearrange("(p n) -> p n", p=rb),
                        in_=src[r0:r0 + rb, :])
                conv_chunks.append(mk())

        def emit_conv(n):
            for _ in range(n):
                if conv_chunks:
                    OP("gpsimd", conv_chunks.pop(0), inc=sGP, incn=16)

        IDB = identb_s[:, :]
        IDF = identf_s[:, :]

        def proj_tile(b, t0, w_b):
            OP("sync", lambda e: e.dma_start(out=xstage[:, :], in_=x_in.ap()[b, t0:t0 + 128, :]),
               wait=BAR(), inc=sLD, incn=16)
            OP("sync", lambda e: e.dma_start(out=cs_st[:, 0:512].rearrange("p (m t) -> p m t", m=4), in_=cos_in.ap()[:, t0:t0 + 128].rearrange("(m p) t -> p m t", p=128)), inc=sLD, incn=16)
            OP("sync", lambda e: e.dma_start(out=cs_st[:, 512:1024].rearrange("p (m t) -> p m t", m=4), in_=sin_in.ap()[:, t0:t0 + 128].rearrange("(m p) t -> p m t", p=128)), inc=sLD, incn=16)
            OP("vector", lambda e: e.tensor_copy(out=xb_st[:, :], in_=xstage[:, :]), wait=[(sLD, sLD.v)], inc=sVE)
            for c in range(8):
                OP("tensor", lambda e, c=c: e.transpose(psB[:, c * 128:(c + 1) * 128], xb_st[:, c * 128:(c + 1) * 128], IDB),
                   wait=[(sVE, sVE.v)] if c == 0 else None, inc=sTE if c == 7 else None)
            for c in range(8):
                OP("vector", lambda e, c=c: e.tensor_copy(out=xT_t[:, c * 128:(c + 1) * 128], in_=psB[:, c * 128:(c + 1) * 128]),
                   wait=[(sTE, sTE.v)] if c == 0 else None, inc=sVE if c == 7 else None)
            for m in range(8):
                for c in range(8):
                    OP("tensor", lambda e, m=m, c=c, w_b=w_b: e.matmul(
                        psF[:, m * 128:(m + 1) * 128],
                        w_b[:, c * D + m * 128: c * D + (m + 1) * 128],
                        xT_t[:, c * 128:(c + 1) * 128],
                        start=(c == 0), stop=(c == 7)),
                       wait=[(sVE, sVE.v)] if (m == 0 and c == 0) else None,
                       inc=sTE if (m == 7 and c == 7) else None)
            teq = sTE.v
            for m in range(4):
                cosm = cs_st[:, m * 128:(m + 1) * 128]
                sinm = cs_st[:, 512 + m * 128: 512 + (m + 1) * 128]
                pm = psF[:, m * 128:(m + 1) * 128]
                pm4 = psF[:, (m + 4) * 128:(m + 5) * 128]
                t1 = small[:, 0:128]
                t2 = small[:, 128:256]
                OP("vector", lambda e, pm=pm, cosm=cosm, t1=t1: e.tensor_mul(t1, pm, cosm),
                   wait=[(sTE, teq)] if m == 0 else None)
                OP("vector", lambda e, pm4=pm4, sinm=sinm, t2=t2: e.tensor_mul(t2, pm4, sinm))
                OP("vector", lambda e, m=m, t1=t1, t2=t2: e.tensor_sub(riT[:, m * 128:(m + 1) * 128], t1, t2))
                OP("vector", lambda e, pm=pm, sinm=sinm, t1=t1: e.tensor_mul(t1, pm, sinm))
                OP("vector", lambda e, pm4=pm4, cosm=cosm, t2=t2: e.tensor_mul(t2, pm4, cosm))
                OP("vector", lambda e, m=m, t1=t1, t2=t2: e.tensor_add(riT[:, (m + 4) * 128:(m + 5) * 128], t1, t2),
                   inc=sVE if m == 3 else None)

        def phaseA(b):
            import os as _os
            KSUB = int(_os.environ.get("KSUB", "9"))
            for tl in range(NT):
                t0 = tl * 128
                proj_tile(b, t0, wp_b)
                for c in range(8):
                    OP("scalar", lambda e, c=c: e.activation(out=sqs[:, c * 128:(c + 1) * 128], in_=riT[:, c * 128:(c + 1) * 128], func=AF.Square),
                       wait=[(sVE, sVE.v)] if c == 0 else None, inc=sSC if c == 7 else None)
                for c in range(8):
                    OP("tensor", lambda e, c=c: e.matmul(psF[0:1, 1536:1664], onescf_s[:, :], sqs[:, c * 128:(c + 1) * 128],
                                                         start=(c == 0), stop=(c == 7)),
                       wait=[(sSC, sSC.v)] if c == 0 else None, inc=sTE if c == 7 else None)
                OP("scalar", lambda e: e.activation(out=small[0:1, 256:384], in_=psF[0:1, 1536:1664], func=AF.Sqrt, bias=cst_s[0:1, 0:1]),
                   wait=[(sTE, sTE.v)], inc=sSC)
                OP("tensor", lambda e: e.matmul(psF[:, 2560:2688], onesrf_s[0:1, :], small[0:1, 256:384], start=True, stop=True),
                   wait=[(sSC, sSC.v)], inc=sTE)
                OP("vector", lambda e, t0=t0: e.tensor_copy(out=normP_bc[:, t0:t0 + 128], in_=psF[:, 2560:2688]),
                   wait=[(sTE, sTE.v)], inc=sVE)
                OP("vector", lambda e: e.reciprocal(out=small[:, 384:512], in_=psF[:, 2560:2688]), inc=sVE)
                VGAP()
                for c in range(8):
                    OP("vector", lambda e, c=c, t0=t0: e.tensor_mul(
                        phat_T[:, c * T + t0: c * T + t0 + 128], riT[:, c * 128:(c + 1) * 128], small[:, 384:512]),
                       inc=sVE if c == 7 else None)
                for c in range(8):
                    OP("tensor", lambda e, c=c, t0=t0: e.transpose(psB[:, c * 128:(c + 1) * 128],
                                                                   phat_T[:, c * T + t0: c * T + t0 + 128], IDB),
                       wait=[(sVE, sVE.v)] if c == 0 else None, inc=sTE if c == 7 else None)
                for c in range(8):
                    OP("vector", lambda e, c=c: e.tensor_copy(out=rows_b[:, c * 128:(c + 1) * 128], in_=psB[:, c * 128:(c + 1) * 128]),
                       wait=[(sTE, sTE.v)] if c == 0 else None, inc=sVE if c == 7 else None)
                OP("sync", lambda e, t0=t0: e.dma_start(out=PI.ap()[t0:t0 + 128, :], in_=rows_b[:, :]),
                   wait=[(sVE, sVE.v)], inc=sLD, incn=16)
            dump("phatT", phat_T[:, :])
            dump("normPbc", normP_bc[:, :])
            if KSUB < 2:
                return
            for qt in range(NT):
                t0 = qt * 128
                nch = qt // 4 + 1
                L = nch * 512
                proj_tile(b, t0, wi_b)
                OP("vector", lambda e: e.tensor_copy(out=riTb[:, :], in_=riT[:, :]), inc=sVE)
                for c in range(8):
                    slot = 2048 if (c % 2 == 0) else 2560
                    OP("tensor", lambda e, c=c, slot=slot: e.transpose(psF[:, slot:slot + 128], riT[:, c * 128:(c + 1) * 128], IDF),
                       wait=[(sSC, sSC.v)] if c >= 2 else ([(sVE, sVE.v)] if c == 0 else None), inc=sTE)
                    OP("scalar", lambda e, c=c, slot=slot: e.copy(out=rows_f[:, c * 128:(c + 1) * 128], in_=psF[:, slot:slot + 128]),
                       wait=[(sTE, sTE.v)], inc=sSC)
                OP("scalar", lambda e: e.activation(out=sqs2[:, :], in_=rows_f[:, :], func=AF.Square, accum_out=small[:, 512:513]),
                   inc=sSC)
                OP("scalar", lambda e: e.copy(out=small[:, 516:517], in_=cst_s[:, 0:1]), inc=sSC)
                OP("scalar", lambda e: e.activation(out=small[:, 513:514], in_=small[:, 512:513], func=AF.Sqrt, bias=cst_s[:, 0:1]), inc=sSC)
                OP("vector", lambda e: e.reciprocal(out=small[:, 514:515], in_=small[:, 513:514]), wait=[(sSC, sSC.v)], inc=sVE)
                VGAP()
                OP("vector", lambda e: e.tensor_scalar(rows_b[:, :], rows_f[:, :], small[:, 514:515], None, ALU.mult), inc=sVE)
                OP("sync", lambda e, t0=t0: e.dma_start(out=PI.ap()[T + t0: T + t0 + 128, :], in_=rows_b[:, :]),
                   wait=[(sVE, sVE.v)], inc=sLD, incn=16)
                for ch in range(nch):
                    for c in range(8):
                        OP("tensor", lambda e, ch=ch, c=c: e.matmul(
                            psF[:, 1024:1536],
                            riTb[:, c * 128:(c + 1) * 128],
                            phat_T[:, c * T + ch * 512: c * T + (ch + 1) * 512],
                            start=(c == 0), stop=(c == 7)),
                           wait=[(sVE, sVE.v)] if c == 0 else None,
                           inc=sTE if c == 7 else None)
                    OP("vector", lambda e, ch=ch: e.tensor_mul(logits[:, ch * 512:(ch + 1) * 512], psF[:, 1024:1536], normP_bc[:, ch * 512:(ch + 1) * 512]),
                       wait=[(sTE, sTE.v)], inc=sVE)
                OP("vector", lambda e, qt=qt, ch=nch - 1: e.copy_predicated(
                    logits[:, ch * 512:(ch + 1) * 512], cmask_s[:, (qt % 4) * 512:(qt % 4 + 1) * 512],
                    neg_c[:, 0:1].to_broadcast([P, 512])), inc=sVE)
                if qt == 0:
                    dump("logits0", logits[:, 0:512])
                mx1 = small[:, 520:528]; mx2 = small[:, 528:536]
                mi1 = smallu[:, 0:8]; mi2 = smallu[:, 8:16]
                micat = smallu[:, 16:31]
                OP("vector", lambda e, L=L: e.max(out=mx1, in_=logits[:, 0:L]))
                OP("vector", lambda e, t0=t0: e.tensor_scalar(small[:, 536:537], pcol_s[:, :], float(t0), None, ALU.add))
                OP("vector", lambda e, L=L, mx1=mx1, mi1=mi1: e.max_index(out=mi1, in_max=mx1, in_values=logits[:, 0:L]))
                OP("vector", lambda e, L=L, mx1=mx1: e.match_replace(out=logits[:, 0:L], in_to_replace=mx1, in_values=logits[:, 0:L], imm_value=MASKV))
                OP("vector", lambda e, L=L, mx2=mx2: e.max(out=mx2, in_=logits[:, 0:L]))
                OP("vector", lambda e, qt=qt, mx1=mx1: e.tensor_scalar(keepf[:, qt * K15: qt * K15 + 8], mx1, -1e29, None, ALU.is_gt))
                OP("vector", lambda e, L=L, mx2=mx2, mi2=mi2: e.max_index(out=mi2, in_max=mx2, in_values=logits[:, 0:L]), inc=sVE)
                OP("vector", lambda e, qt=qt, mx2=mx2: e.tensor_scalar(keepf[:, qt * K15 + 8: qt * K15 + 15], mx2[:, 0:7], -1e29, None, ALU.is_gt))
                OP("vector", lambda e, mi1=mi1, micat=micat: e.tensor_copy(out=micat[:, 0:8], in_=mi1))
                OP("vector", lambda e, mi2=mi2, micat=micat: e.tensor_copy(out=micat[:, 8:15], in_=mi2[:, 0:7]), inc=sVE)
                OP("vector", lambda e, qt=qt: e.tensor_copy(out=smallu8[:, 0:15], in_=keepf[:, qt * K15: qt * K15 + 15]), inc=sVE)
                VGAP()
                OP("vector", lambda e, qt=qt, micat=micat: e.select(idxbuf[:, qt * 16: qt * 16 + 15], smallu8[:, 0:15], micat, zrow_u[:, :]), inc=sVE)
                OP("vector", lambda e, qt=qt: e.tensor_copy(out=idxbuf[:, qt * 16 + 15: qt * 16 + 16], in_=gcol_s[:, qt:qt + 1]), inc=sVE)
                OP("vector", lambda e, micat=micat: e.tensor_copy(out=small[:, 544:559], in_=micat))
                OP("vector", lambda e: e.tensor_scalar(small[:, 560:575], small[:, 544:559], small[:, 536:537], None, ALU.subtract))
                OP("vector", lambda e: e.tensor_copy(out=small[:, 518:519], in_=cst_s[:, 0:1]))
                OP("vector", lambda e: e.tensor_scalar(small[:, 592:607], small[:, 560:575], -1.0 / T, 0.0, ALU.mult, ALU.max))
                VGAP()
                OP("vector", lambda e, qt=qt: e.tensor_mul(small[:, 576:591], small[:, 592:607], keepf[:, qt * K15: qt * K15 + K15]), inc=sVE)
                OP("sync", lambda e, b=b, qt=qt: e.dma_start(
                    out=delta_d.ap()[b * NT + qt].rearrange("(p k) -> p k", p=128), in_=small[:, 576:591]),
                    wait=[(sVE, sVE.v)], inc=sLD, incn=16)
                if b == 0:
                    emit_conv(2)
                if qt == 0 and b == 0:
                    dump("dbg_tcol", small[:, 536:537])
                    dump("dbg_idxf", small[:, 544:559])
                    dump("dbg_d1", small[:, 560:575])
                    dump("dbg_d2", small[:, 592:607])
                    dump("dbg_d3", small[:, 576:591])
            if b == 0:
                emit_conv(len(conv_chunks))
            dump("idxbuf", idxbuf[:, :])
            dump("keepf", keepf[:, :])
            if KSUB < 3:
                return
            KTILES = int(_os.environ.get("KTILES", str(NT)))
            KA = int(_os.environ.get("KASTEP", "9"))
            packT_4d = packT.ap().rearrange("p (c t w) -> p c t w", c=8, t=128, w=32)
            te_tr = {}
            for qt in range(KTILES):
                t0 = qt * 128
                bar = [(sLD, sLD.v), (sVE, sVE.v), (sSC, sSC.v),
                       (sTE, te_tr[qt - 1] if qt > 0 else sTE.v), (sGP, gp_p0)]
                for g in range(16):
                    OP("gpsimd", lambda e, g=g, qt=qt: e.indirect_dma_start(
                        out=pack[:, g * D:(g + 1) * D], out_offset=None,
                        in_=PI.ap(), in_offset=IndirectOffsetOnAxis(idxbuf[:, qt * 16 + g: qt * 16 + g + 1], 0)),
                        wait=bar if g == 0 else None, inc=sGP, incn=16)
                gath = sGP.v
                if qt == 0:
                    dump("pack0", pack[:, :], wait=[(sGP, gath)])
                if KA < 2:
                    continue
                if _os.environ.get("KB_DELTA") != "0":
                    OP("gpsimd", lambda e, b=b, qt=qt: e.dma_start(out=relb[16:17, :], in_=delta_d.ap()[b * NT + qt: b * NT + qt + 1, :]),
                       inc=sGP, incn=16)
                dl = sGP.v
                evac_done = {}
                for g in range(16):
                    pb = psB if g % 2 == 0 else psB2
                    for c in range(8):
                        w = None
                        if c == 0:
                            w = [(sGP, gath)] if g == 0 else []
                            if g >= 2:
                                w.append((sVE, evac_done[g - 2]))
                        OP("tensor", lambda e, g=g, c=c, pb=pb: e.transpose(
                            pb[:, c * 128:(c + 1) * 128], pack[:, g * D + c * 128: g * D + (c + 1) * 128], IDB),
                           wait=w if w else None, inc=sTE if c == 7 else None)
                    for c in range(8):
                        OP("vector", lambda e, g=g, c=c, pb=pb: e.tensor_copy(
                            out=packT_4d[:, c, :, g], in_=pb[:, c * 128:(c + 1) * 128]),
                           wait=[(sTE, sTE.v)] if c == 0 else None, inc=sVE)
                    evac_done[g] = sVE.v
                te_tr[qt] = sTE.v
                if KA < 3:
                    continue
                for grp in range(32):
                    slot = (grp % 8) * 64
                    for c in range(8):
                        OP("tensor", lambda e, grp=grp, c=c, slot=slot: e.matmul(
                            psF[:, slot:slot + 64],
                            packT[:, c * T + grp * 128: c * T + (grp + 1) * 128],
                            packT_4d[:, c, 4 * grp:4 * grp + 4, 0:16],
                            start=(c == 0), stop=(c == 7)),
                           wait=[(sVE, sVE.v)] if c == 0 else None,
                           inc=sTE if c == 7 else None)
                    for tau in range(4):
                        t_ = grp * 4 + tau
                        OP("vector", lambda e, tau=tau, t_=t_, slot=slot: e.tensor_scalar(
                            relb[0:16, t_ * K15:(t_ + 1) * K15],
                            psF[32 * tau:32 * tau + 16, slot + tau * 16: slot + tau * 16 + 15],
                            1.0, -1.0, ALU.min, ALU.max),
                           wait=[(sTE, sTE.v)] if tau == 0 else None,
                           inc=sVE if tau == 3 else None)
                if qt == 0:
                    dump("relb0", relb[0:33, :])
                    dump("deltad", delta_d.ap()[b * NT: b * NT + 1, :])
                if KA < 4:
                    continue
                for (n0, n1) in ((0, 512), (512, 1024), (1024, 1536), (1536, 1920)):
                    OP("tensor", lambda e, n0=n0, n1=n1: e.matmul(psF[0:64, 512 + n0:512 + n1], w1e_b[:, :], relb[0:17, n0:n1], start=True, stop=True),
                       wait=[(sVE, sVE.v), (sGP, dl)] if n0 == 0 else None,
                       inc=sTE if n1 == 1920 else None)
                OP("scalar", lambda e: e.activation(out=h1b[:, :], in_=psF[0:64, 512:2432], func=AF.Gelu, bias=b1a_s[:, :]),
                   wait=[(sTE, sTE.v)], inc=sSC)
                for (n0, n1) in ((0, 512), (512, 1024), (1024, 1536), (1536, 1920)):
                    OP("tensor", lambda e, n0=n0, n1=n1: e.matmul(psF[0:64, 512 + n0:512 + n1], w2a_b[:, :], h1b[:, n0:n1], start=True, stop=True),
                       wait=[(sSC, sSC.v)] if n0 == 0 else None,
                       inc=sTE if n1 == 1920 else None)
                OP("scalar", lambda e: e.activation(out=h2b[:, :], in_=psF[0:64, 512:2432], func=AF.Gelu, bias=b2a_s[:, :]),
                   wait=[(sTE, sTE.v)], inc=sSC)
                if qt == 0:
                    dump("h1b0", h1b[:, :])
                    dump("h2b0", h2b[:, :])
                if KA < 5:
                    continue
                OP("tensor", lambda e: e.matmul(psF[:, 2560:2605], onesrb_s[0:1, :], bcm_b[0:1, 0:45], start=True, stop=False, skip_group_check=True),
                   wait=[(sSC, sSC.v)])
                hp = h2b.ap()
                pitch_h = hp.ap[0][0]
                for k in range(K15):
                    lA = bass.AP(h2b, k, [[pitch_h, 64], [K15, 128]])
                    OP("tensor", lambda e, k=k, lA=lA: e.matmul(psF[:, 2560 + k:2561 + k], lA, wc_b[0:64, 0:1], start=False, stop=True, skip_group_check=True))
                    OP("tensor", lambda e, k=k, lA=lA: e.matmul(psF[:, 2575 + k:2576 + k], lA, wc_b[0:64, 1:2], start=False, stop=True, skip_group_check=True))
                    OP("tensor", lambda e, k=k, lA=lA: e.matmul(psF[:, 2590 + k:2591 + k], lA, wm_b[0:64, 0:1], start=False, stop=True, skip_group_check=True),
                       inc=sTE if k == K15 - 1 else None)
                OP("scalar", lambda e: e.activation(out=pwk[:, 0:30], in_=psF[:, 2560:2590], func=AF.Tanh),
                   wait=[(sTE, sTE.v)], inc=sSC)
                OP("vector", lambda e: e.tensor_copy(out=pwk[:, 32:47], in_=psF[:, 2590:2605]), wait=[(sTE, sTE.v)], inc=sVE)
                VGAP()
                OP("vector", lambda e: e.tensor_reduce(pwk[:, 48:49], pwk[:, 32:47], AX.X, ALU.max), inc=sVE)
                VGAP()
                OP("vector", lambda e: e.tensor_scalar(pwk[:, 64:79], pwk[:, 32:47], pwk[:, 48:49], None, ALU.subtract), inc=sVE)
                OP("scalar", lambda e: e.activation(out=pwk[:, 80:95], in_=pwk[:, 64:79], func=AF.Exp),
                   wait=[(sVE, sVE.v)], inc=sSC)
                OP("vector", lambda e, qt=qt: e.tensor_mul(pwk[:, 96:111], pwk[:, 80:95], keepf[:, qt * K15:(qt + 1) * K15]),
                   wait=[(sSC, sSC.v)], inc=sVE)
                VGAP()
                OP("vector", lambda e: e.tensor_reduce(pwk[:, 112:113], pwk[:, 96:111], AX.X, ALU.add), inc=sVE)
                VGAP()
                OP("vector", lambda e: e.reciprocal(out=pwk[:, 113:114], in_=pwk[:, 112:113]), inc=sVE)
                VGAP()
                OP("vector", lambda e: e.tensor_scalar(pwk[:, 128:143], pwk[:, 96:111], pwk[:, 113:114], None, ALU.mult), inc=sVE)
                if qt == 0:
                    dump("pwk0", pwk[:, 0:352])
                if KA < 6:
                    continue
                OP("vector", lambda e: e.tensor_scalar(pwk[:, 144:174], pwk[:, 0:30], 7.5, 7.5, ALU.mult, ALU.add), wait=[(sSC, sSC.v)], inc=sVE)
                VGAP()
                OP("vector", lambda e: e.tensor_scalar(pwk[:, 304:334], pwk[:, 144:174], 15.0, 0.0, ALU.min, ALU.max), inc=sVE)
                VGAP()
                OP("vector", lambda e: e.tensor_scalar(pwk[:, 176:206], pwk[:, 304:334], -0.5, None, ALU.add), inc=sVE)
                VGAP()
                OP("vector", lambda e: e.tensor_copy(out=smallu[:, 32:62], in_=pwk[:, 176:206]), inc=sVE)
                VGAP()
                OP("vector", lambda e: e.tensor_copy(out=pwk[:, 176:206], in_=smallu[:, 32:62]), inc=sVE)
                VGAP()
                OP("vector", lambda e: e.tensor_sub(pwk[:, 208:238], pwk[:, 304:334], pwk[:, 176:206]), inc=sVE)
                OP("vector", lambda e: e.tensor_scalar(pwk[:, 240:270], pwk[:, 176:206], 1.0, 15.0, ALU.add, ALU.min), inc=sVE)
                VGAP()
                OP("vector", lambda e: e.tensor_scalar(pwk[:, 272:302], pwk[:, 208:238], -1.0, 1.0, ALU.mult, ALU.add), inc=sVE)
                VGAP()
                def bc15(ap_):
                    return ap_.rearrange("p k -> p k").unsqueeze(2).to_broadcast([P, 15, 16])
                OP("vector", lambda e: e.tensor_tensor(hxw[:, 0:240].rearrange("p (k i) -> p k i", k=15), iota16_s[:, :].rearrange("p (k i) -> p k i", k=15), bc15(pwk[:, 176:191]), ALU.is_equal), inc=sVE)
                OP("vector", lambda e: e.tensor_tensor(hxw[:, 0:240].rearrange("p (k i) -> p k i", k=15), hxw[:, 0:240].rearrange("p (k i) -> p k i", k=15), bc15(pwk[:, 272:287]), ALU.mult), inc=sVE)
                OP("vector", lambda e: e.tensor_tensor(t256[:, 0:240].rearrange("p (k i) -> p k i", k=15), iota16_s[:, :].rearrange("p (k i) -> p k i", k=15), bc15(pwk[:, 240:255]), ALU.is_equal), inc=sVE)
                OP("vector", lambda e: e.tensor_tensor(t256[:, 0:240].rearrange("p (k i) -> p k i", k=15), t256[:, 0:240].rearrange("p (k i) -> p k i", k=15), bc15(pwk[:, 208:223]), ALU.mult), inc=sVE)
                VGAP()
                OP("vector", lambda e: e.tensor_add(hxw[:, 0:240], hxw[:, 0:240], t256[:, 0:240]), inc=sVE)
                OP("vector", lambda e: e.tensor_tensor(hxw[:, 240:480].rearrange("p (k i) -> p k i", k=15), iota16_s[:, :].rearrange("p (k i) -> p k i", k=15), bc15(pwk[:, 191:206]), ALU.is_equal), inc=sVE)
                OP("vector", lambda e: e.tensor_tensor(hxw[:, 240:480].rearrange("p (k i) -> p k i", k=15), hxw[:, 240:480].rearrange("p (k i) -> p k i", k=15), bc15(pwk[:, 287:302]), ALU.mult), inc=sVE)
                OP("vector", lambda e: e.tensor_tensor(t256[:, 0:240].rearrange("p (k i) -> p k i", k=15), iota16_s[:, :].rearrange("p (k i) -> p k i", k=15), bc15(pwk[:, 255:270]), ALU.is_equal), inc=sVE)
                OP("vector", lambda e: e.tensor_tensor(t256[:, 0:240].rearrange("p (k i) -> p k i", k=15), t256[:, 0:240].rearrange("p (k i) -> p k i", k=15), bc15(pwk[:, 223:238]), ALU.mult), inc=sVE)
                VGAP()
                OP("vector", lambda e: e.tensor_add(hxw[:, 240:480], hxw[:, 240:480], t256[:, 0:240]), inc=sVE)
                VGAP()
                OP("vector", lambda e: e.tensor_tensor(hxw[:, 480:720].rearrange("p (k i) -> p k i", k=15), hxw[:, 240:480].rearrange("p (k i) -> p k i", k=15), bc15(pwk[:, 128:143]), ALU.mult), inc=sVE)
                bufs = [t256[:, :], small[:, 640:896]]
                for k in range(K15):
                    ay = hxw[:, 480 + k * 16:480 + (k + 1) * 16].unsqueeze(2).to_broadcast([P, 16, 16])
                    hx = hxw[:, k * 16:(k + 1) * 16].unsqueeze(1).to_broadcast([P, 16, 16])
                    dst = c256[:, :] if k == 0 else bufs[k % 2]
                    OP("vector", lambda e, ay=ay, hx=hx, dst=dst: e.tensor_tensor(dst.rearrange("p (y x) -> p y x", y=16), ay, hx, ALU.mult))
                    if k >= 2:
                        pb2 = bufs[(k - 1) % 2]
                        OP("vector", lambda e, pb2=pb2: e.tensor_add(c256[:, :], c256[:, :], pb2))
                VGAP()
                OP("vector", lambda e: e.tensor_add(c256[:, :], c256[:, :], bufs[(K15 - 1) % 2]), inc=sVE)
                if qt == 0:
                    dump("c256_0", c256[:, :])
                if KA < 7:
                    continue
                for c2 in range(2):
                    cts = 2048 if c2 == 0 else 2688
                    OP("tensor", lambda e, c2=c2, cts=cts: e.transpose(psF[:, cts:cts + 128], c256[:, c2 * 128:(c2 + 1) * 128], IDF),
                       wait=[(sVE, sVE.v)] if c2 == 0 else None, inc=sTE)
                    OP("vector", lambda e, c2=c2, cts=cts: e.tensor_copy(out=ct_b[:, c2 * 128:(c2 + 1) * 128], in_=psF[:, cts:cts + 128]),
                       wait=[(sTE, sTE.v)], inc=sVE)
                for m in range(8):
                    for c2 in range(2):
                        OP("tensor", lambda e, m=m, c2=c2: e.matmul(
                            psF[:, m * 128:(m + 1) * 128],
                            palT_b[:, c2 * D + m * 128: c2 * D + (m + 1) * 128],
                            ct_b[:, c2 * 128:(c2 + 1) * 128],
                            start=(c2 == 0), stop=(c2 == 1)),
                           wait=[(sVE, sVE.v)] if (m == 0 and c2 == 0) else None,
                           inc=sTE if (m == 7 and c2 == 1) else None)
                for m in range(8):
                    OP("vector", lambda e, m=m: e.tensor_copy(out=vt_b[:, m * 128:(m + 1) * 128], in_=psF[:, m * 128:(m + 1) * 128]),
                       wait=[(sTE, sTE.v)] if m == 0 else None, inc=sVE if m == 7 else None)
                if qt == 0:
                    dump("vtb0", vt_b[:, :])
                for n2 in range(2):
                    for c in range(8):
                        OP("tensor", lambda e, n2=n2, c=c: e.matmul(
                            psF[:, 1024 + n2 * 512:1024 + (n2 + 1) * 512],
                            vt_b[:, c * 128:(c + 1) * 128],
                            wo_b[:, c * D + n2 * 512: c * D + (n2 + 1) * 512],
                            start=(c == 0), stop=(c == 7)),
                           wait=[(sVE, sVE.v)] if (n2 == 0 and c == 0) else None,
                           inc=sTE if (n2 == 1 and c == 7) else None)
                if KA < 8:
                    continue
                OP("sync", lambda e, b=b, t0=t0: e.dma_start(out=x0st[:, :], in_=x_in.ap()[b, t0:t0 + 128, :]),
                   wait=[(sVE, sVE.v), (sSC, sSC.v)], inc=sLD, incn=16)
                OP("vector", lambda e: e.tensor_add(ust[:, :], psF[:, 1024:2048], x0st[:, :]),
                   wait=[(sTE, sTE.v), (sLD, sLD.v)], inc=sVE)
                VGAP()
                OP("vector", lambda e: e.tensor_reduce(small[:, 600:601], ust[:, :], AX.X, ALU.add), inc=sVE)
                VGAP()
                OP("vector", lambda e: e.tensor_scalar(small[:, 601:602], small[:, 600:601], 1.0 / D, None, ALU.mult), inc=sVE)
                VGAP()
                OP("vector", lambda e: e.tensor_scalar(x0st[:, :], ust[:, :], small[:, 601:602], None, ALU.subtract), inc=sVE)
                OP("scalar", lambda e: e.activation(out=scrap[:, :], in_=x0st[:, :], func=AF.Square, accum_out=small[:, 602:603]),
                   wait=[(sVE, sVE.v)], inc=sSC)
                OP("vector", lambda e: e.tensor_scalar(small[:, 603:604], small[:, 602:603], 1.0 / D, None, ALU.mult),
                   wait=[(sSC, sSC.v)], inc=sVE)
                OP("scalar", lambda e: e.activation(out=small[:, 604:605], in_=small[:, 603:604], func=AF.Sqrt, bias=cst_s[:, 1:2]),
                   wait=[(sVE, sVE.v)], inc=sSC)
                OP("vector", lambda e: e.reciprocal(out=small[:, 605:606], in_=small[:, 604:605]), wait=[(sSC, sSC.v)], inc=sVE)
                VGAP()
                OP("vector", lambda e: e.tensor_scalar(scrap[:, :], x0st[:, :], small[:, 605:606], None, ALU.mult), inc=sVE)
                VGAP()
                OP("vector", lambda e: e.tensor_copy(out=x1bst[:, :], in_=scrap[:, :]), inc=sVE)
                if qt == 0:
                    dump("x1st0", scrap[:, :])
                OP("sync", lambda e, b=b, t0=t0: e.dma_start(out=x1f_d.ap()[b * T + t0: b * T + t0 + 128, :], in_=scrap[:, :]),
                   wait=[(sVE, sVE.v)], inc=sLD, incn=16)
                OP("sync", lambda e, b=b, t0=t0: e.dma_start(out=x1b_d.ap()[b * T + t0: b * T + t0 + 128, :], in_=x1bst[:, :]),
                   inc=sLD, incn=16)
            dump("x1_b%d" % b, None)

        def gate_batch(b):
            for tl in range(NT):
                t0 = tl * 128
                OP("sync", lambda e, b=b, t0=t0: e.dma_start(out=bx1row[:, :], in_=x1b_d.ap()[b * T + t0: b * T + t0 + 128, :]),
                   wait=BAR(), inc=sLD, incn=16)
                for c in range(8):
                    OP("tensor", lambda e, c=c: e.transpose(psB[:, c * 128:(c + 1) * 128], bx1row[:, c * 128:(c + 1) * 128], IDB),
                       wait=[(sLD, sLD.v), (sVE, sVE.v)] if c == 0 else None, inc=sTE if c == 7 else None)
                for c in range(8):
                    OP("vector", lambda e, c=c, t0=t0: e.tensor_copy(out=x1T[:, c * T + t0: c * T + t0 + 128], in_=psB[:, c * 128:(c + 1) * 128]),
                       wait=[(sTE, sTE.v)] if c == 0 else None, inc=sVE if c == 7 else None)
                for c in range(8):
                    OP("tensor", lambda e, c=c, t0=t0: e.matmul(
                        psF[:, 0:8], x1T[:, c * T + t0: c * T + t0 + 128], gate_b[:, c * E:(c + 1) * E],
                        start=(c == 0), stop=(c == 7)),
                       wait=[(sVE, sVE.v)] if c == 0 else None, inc=sTE if c == 7 else None)
                pa = (b * NT + tl) * 8
                OP("vector", lambda e: e.tensor_reduce(small[:, 0:1], psF[:, 0:8], AX.X, ALU.max), wait=[(sTE, sTE.v)], inc=sVE)
                VGAP()
                OP("vector", lambda e: e.tensor_scalar(small[:, 8:16], psF[:, 0:8], small[:, 0:1], None, ALU.subtract), inc=sVE)
                OP("scalar", lambda e: e.activation(out=small[:, 16:24], in_=small[:, 8:16], func=AF.Exp),
                   wait=[(sVE, sVE.v)], inc=sSC)
                OP("vector", lambda e: e.tensor_reduce(small[:, 1:2], small[:, 16:24], AX.X, ALU.add), wait=[(sSC, sSC.v)], inc=sVE)
                VGAP()
                OP("vector", lambda e: e.reciprocal(out=small[:, 2:3], in_=small[:, 1:2]), inc=sVE)
                VGAP()
                OP("vector", lambda e, pa=pa: e.tensor_scalar(probs_all[:, pa:pa + 8], small[:, 16:24], small[:, 2:3], None, ALU.mult), inc=sVE)
                OP("tensor", lambda e, pa=pa: e.transpose(psF[0:8, 128:256], probs_all[:, pa:pa + 8], IDF),
                   wait=[(sVE, sVE.v)], inc=sTE)
                OP("vector", lambda e: e.tensor_copy(out=small[0:8, 32:160], in_=psF[0:8, 128:256]), wait=[(sTE, sTE.v)], inc=sVE)
                OP("sync", lambda e, b=b, t0=t0: e.dma_start(out=probsT_d.ap()[:, b * T + t0: b * T + t0 + 128], in_=small[0:8, 32:160]),
                   wait=[(sVE, sVE.v)], inc=sLD, incn=16)

        def shared_batch(b):
            nsub = 2
            srcW1 = wffn_d.ap()[WS1B:WS1B + D * HID].rearrange("(r h) -> r h", h=HID)
            srcW2 = wffn_d.ap()[WS2B:WS2B + HID * D].rearrange("(h d) -> h d", d=D)
            bufs = [(bws1, bw2), (wstB, w2tB)]
            ld_after = {}
            te_after = {}
            KK = 16 * 8

            def emit_load(k, first):
                hcc = k % 8
                bw_, b2_ = bufs[k % 2]
                w = [(sTE, te_after[k - 2])] if k >= 2 else (BAR() if first else None)
                for c in range(8):
                    OP("sync", lambda e, c=c, hcc=hcc, bw_=bw_: e.dma_start(
                        out=bw_[:, c * 512:(c + 1) * 512], in_=srcW1[c * 128:(c + 1) * 128, hcc * 512:(hcc + 1) * 512]),
                       wait=w if c == 0 else None, inc=sLD, incn=16)
                OP("sync", lambda e, hcc=hcc, b2_=b2_: e.dma_start(
                    out=b2_[:, :].rearrange("p (s d) -> p s d", s=4), in_=srcW2[hcc * 512:(hcc + 1) * 512, :].rearrange("(s p) d -> p s d", p=128)),
                   inc=sLD, incn=16)
                ld_after[k] = sLD.v

            emit_load(0, True)
            emit_load(1, False)
            ev_ld = sLD.v
            for k in range(KK):
                tt, hcc = divmod(k, 8)
                tb = tt * 256
                bw_, b2_ = bufs[k % 2]
                if hcc == 0:
                    for sub in range(nsub):
                        for dn in range(2):
                            OP("tensor", lambda e, sub=sub, dn=dn: e.matmul(
                                psF[:, 512 + sub * 1024 + dn * 512: 512 + sub * 1024 + (dn + 1) * 512],
                                onesrb_s[0:1, :], bs2_b[0:1, dn * 512:(dn + 1) * 512],
                                start=True, stop=False, skip_group_check=True),
                               wait=[(sVE, sVE.v), (sLD, ev_ld)] if (sub == 0 and dn == 0) else None)
                for hp in range(2):
                    for hsub in (2 * hp, 2 * hp + 1):
                        po = (hsub % 2) * 256
                        for c in range(8):
                            OP("tensor", lambda e, b=b, c=c, hsub=hsub, tb=tb, bw_=bw_, po=po: e.matmul(
                                psF[:, po:po + 256],
                                bw_[:, c * 512 + hsub * 128: c * 512 + hsub * 128 + 128],
                                x1T[:, c * T + tb: c * T + tb + 256],
                                start=(c == 0), stop=(c == 7)),
                               wait=([(sLD, ld_after[k]), (sVE, sVE.v)] if c == 0 else None) if hsub == 0 else ([(sVE, sVE.v)] if c == 0 else None),
                               inc=sTE if c == 7 else None)
                    blk = hcc * 4 + 2 * hp
                    OP("vector", lambda e, blk=blk: e.tensor_tensor(
                        xbS[:, :].rearrange("p (u t) -> p u t", u=2), psF[:, 0:512].rearrange("p (u t) -> p u t", u=2),
                        bs1_s[:, blk:blk + 2].unsqueeze(2).to_broadcast([P, 2, 256]), ALU.add),
                       wait=[(sTE, sTE.v)], inc=sVE)
                    VGAP()
                    OP("scalar", lambda e: e.activation(out=gscrS[:, :], in_=xbS[:, :], func=AF.Square, scale=cst_s[:, 2:3]),
                       wait=[(sVE, sVE.v)], inc=sSC)
                    OP("vector", lambda e: e.scalar_tensor_tensor(
                        out=gscrS[:, :], in0=gscrS[:, :], scalar=1.0, in1=xbS[:, :], op0=ALU.add, op1=ALU.mult),
                       wait=[(sSC, sSC.v)], inc=sVE)
                    OP("scalar", lambda e, w_=(hcc == 0 and hp == 0): e.activation(out=small[:, 0:512], in_=gscrS[:, :], func=AF.Sigmoid, scale=cst_s[:, 3:4]),
                       wait=[(sVE, sVE.v), (sLD, ev_ld)] if (hcc == 0 and hp == 0) else [(sVE, sVE.v)], inc=sSC)
                    OP("vector", lambda e, hp=hp: e.tensor_tensor(shhs[:, hp * 512:(hp + 1) * 512], small[:, 0:512], xbS[:, :], ALU.mult),
                       wait=[(sSC, sSC.v)], inc=sVE)
                    for hsub in (2 * hp, 2 * hp + 1):
                        for sub in range(nsub):
                            for dn in range(2):
                                OP("tensor", lambda e, sub=sub, dn=dn, hsub=hsub, hcc=hcc, b2_=b2_: e.matmul(
                                    psF[:, 512 + sub * 1024 + dn * 512: 512 + sub * 1024 + (dn + 1) * 512],
                                    shhs[:, hsub * 256 + sub * 128: hsub * 256 + sub * 128 + 128],
                                    b2_[:, hsub * D + dn * 512: hsub * D + (dn + 1) * 512],
                                    start=False, stop=(hcc == 7 and hsub == 3), skip_group_check=True),
                                   wait=[(sVE, sVE.v)] if (hsub % 2 == 0 and sub == 0 and dn == 0) else None,
                                   inc=sTE if (hsub == 3 and sub == 1 and dn == 1) else None)
                te_after[k] = sTE.v
                if hcc == 7:
                    for sub in range(nsub):
                        OP("vector", lambda e, sub=sub: e.tensor_copy(out=small[:, 256:1280], in_=psF[:, 512 + sub * 1024:512 + (sub + 1) * 1024]),
                           wait=[(sTE, sTE.v)] if sub == 0 else [(sLD, sLD.v)], inc=sVE)
                        OP("sync", lambda e, b=b, tb=tb, sub=sub: e.dma_start(
                            out=routed_d.ap()[b * T + tb + sub * 128: b * T + tb + sub * 128 + 128, :], in_=small[:, 256:1280]),
                           wait=[(sVE, sVE.v)], inc=sLD, incn=16)
                    ev_ld = sLD.v
                if k + 2 < KK:
                    emit_load(k + 2, False)

        def bisection():
            OP("sync", lambda e: e.dma_start(out=probsT_d.ap()[0:1, 0:1], in_=probsT_d.ap()[0:1, 0:1]), inc=sLD, incn=16)
            OP("gpsimd", lambda e: e.collective_compute(
                "AllGather", ALU.bypass, replica_groups=[list(range(8))],
                ins=[probsT_d.ap()], outs=[ag_d.ap()]),
               wait=BAR(), inc=sCC)
            OP("sync", lambda e: e.dma_start(out=AG[:, :], in_=ag_d.ap().rearrange("a (h x) -> (a h) x", h=2)),
               wait=[(sCC, sCC.v)], inc=sLD, incn=16)
            OP("vector", lambda e: e.memset(lo8[:, :], 0.0), wait=[(sLD, sLD.v)], inc=sVE)
            OP("vector", lambda e: e.memset(hi8[:, :], 1.0), inc=sVE)
            OP("vector", lambda e: e.memset(mid8[:, :], 0.5), inc=sVE)
            for it in range(45):
                OP("tensor", lambda e: e.matmul(psF[:, 256:257], e16T_s[:, :], mid8[:, :], start=True, stop=True),
                   wait=[(sVE, sVE.v)], inc=sTE)
                OP("vector", lambda e: e.tensor_copy(out=small[:, 0:1], in_=psF[:, 256:257]), wait=[(sTE, sTE.v)], inc=sVE)
                VGAP()
                OP("vector", lambda e: e.tensor_scalar(AGs[:, :], AG[:, :], small[:, 0:1], 0.0, ALU.is_gt, ALU.add, accum_out=small[:, 1:2]), inc=sVE)
                VGAP()
                OP("tensor", lambda e: e.matmul(psF[0:8, 257:258], e16_s[:, :], small[:, 1:2], start=True, stop=True),
                   wait=[(sVE, sVE.v)], inc=sTE)
                OP("vector", lambda e: e.tensor_copy(out=cnt8[:, :], in_=psF[0:8, 257:258]), wait=[(sTE, sTE.v)], inc=sVE)
                VGAP()
                OP("vector", lambda e: e.tensor_scalar(cge8[:, :], cnt8[:, :], 8191.5, None, ALU.is_gt), inc=sVE)
                OP("vector", lambda e: e.tensor_scalar(cgeN[:, :], cnt8[:, :], 8191.5, None, ALU.is_le), inc=sVE)
                VGAP()
                OP("vector", lambda e: e.copy_predicated(lo8[:, :], cge8[:, :], mid8[:, :]), inc=sVE)
                OP("vector", lambda e: e.copy_predicated(hi8[:, :], cgeN[:, :], mid8[:, :]), inc=sVE)
                VGAP()
                OP("vector", lambda e: e.tensor_add(mid8t[:, :], lo8[:, :], hi8[:, :]), inc=sVE)
                VGAP()
                OP("vector", lambda e: e.tensor_scalar(mid8[:, :], mid8t[:, :], 0.5, None, ALU.mult), inc=sVE)
            OP("tensor", lambda e: e.transpose(psF[0:1, 258:266], lo8[:, :], identf_s[0:8, 0:8]),
               wait=[(sVE, sVE.v)], inc=sTE)
            OP("vector", lambda e: e.tensor_copy(out=loT_s[:, :], in_=psF[0:1, 258:266]), wait=[(sTE, sTE.v)], inc=sVE)
            OP("tensor", lambda e: e.matmul(psF[:, 272:280], onesrf_s[0:1, :], loT_s[:, :], start=True, stop=True),
               wait=[(sVE, sVE.v)], inc=sTE)
            OP("vector", lambda e: e.tensor_copy(out=loB_s[:, :], in_=psF[:, 272:280]), wait=[(sTE, sTE.v)], inc=sVE)
            VGAP()
            dump("lo8", lo8[:, :])

        def compaction():
            for tl in range(NB * NT):
                pa = tl * 8
                OP("vector", lambda e, pa=pa: e.tensor_tensor(small[:, 0:8], probs_all[:, pa:pa + 8], loB_s[:, :], ALU.is_gt),
                   wait=BAR() if tl == 0 else [(sGP, sGP.v)], inc=sVE)
                VGAP()
                OP("vector", lambda e: e.tensor_copy(out=smallb[:, 0:8], in_=small[:, 0:8]), inc=sVE)
                OP("vector", lambda e: e.tensor_copy(out=smallu8[:, 0:8], in_=small[:, 0:8]), inc=sVE)
                OP("tensor", lambda e: e.matmul(psF[:, 384:392], tril_s[:, :], smallb[:, 0:8], start=True, stop=True),
                   wait=[(sVE, sVE.v)], inc=sTE)
                OP("tensor", lambda e: e.matmul(psF[:, 392:400], ones128_s[:, :], smallb[:, 0:8], start=True, stop=True), inc=sTE)
                OP("vector", lambda e: e.tensor_add(small[:, 8:16], psF[:, 384:392], baseB[:, :]), wait=[(sTE, sTE.v)], inc=sVE)
                OP("vector", lambda e: e.tensor_add(baseB[:, :], baseB[:, :], psF[:, 392:400]), inc=sVE)
                OP("vector", lambda e: e.tensor_scalar(small[:, 16:24], small[:, 8:16], float(CAP) - 0.5, None, ALU.is_gt), inc=sVE)
                VGAP()
                OP("vector", lambda e: e.tensor_sub(small[:, 0:8], small[:, 0:8], small[:, 16:24]), inc=sVE)
                VGAP()
                OP("vector", lambda e: e.tensor_scalar(smallu8[:, 0:8], small[:, 0:8], 0.5, None, ALU.is_gt), inc=sVE)
                OP("vector", lambda e: e.tensor_add(small[:, 8:16], small[:, 8:16], ecap_s[:, :]), inc=sVE)
                VGAP()
                OP("vector", lambda e: e.tensor_copy(out=smallu[:, 0:8], in_=small[:, 8:16]), inc=sVE)
                VGAP()
                OP("vector", lambda e: e.select(smallu[:, 8:16], smallu8[:, 0:8], smallu[:, 0:8], trash_s[:, 0:1].to_broadcast([P, 8])), inc=sVE)
                OP("vector", lambda e, tl=tl: e.tensor_copy(out=smallu[:, 16:32].rearrange("p (e c) -> p e c", c=2)[:, :, 0], in_=tokid_s[:, tl:tl + 1].to_broadcast([P, 8])), inc=sVE)
                OP("vector", lambda e, pa=pa: e.tensor_copy(out=smallu[:, 16:32].rearrange("p (e c) -> p e c", c=2)[:, :, 1], in_=probs_all[:, pa:pa + 8].bitcast(U32)), inc=sVE)
                vv = sVE.v
                for e_ in range(8):
                    OP("gpsimd", lambda e, e_=e_: e.indirect_dma_start(
                        out=lists_d.ap(), out_offset=IndirectOffsetOnAxis(smallu[:, 8 + e_:9 + e_], 0),
                        in_=smallu[:, 16 + 2 * e_:18 + 2 * e_], in_offset=None),
                        wait=[(sVE, vv)] if e_ == 0 else None, inc=sGP, incn=16)

        def experts():
            scatter_prev = sGP.v
            scatter_prev2 = sGP.v
            for e_ in range(E):
                OP("gpsimd", lambda e, e_=e_: e.dma_start(out=b2e_b[0:1, :], in_=b2f_in.ap()[0:1, e_ * D:(e_ + 1) * D]),
                   wait=BAR(), inc=sGP, incn=16)
                b2ld = sGP.v
                for cc in range(NCAP):
                    OP("sync", lambda e, e_=e_, cc=cc: e.dma_start(
                        out=smallu[:, 2 * cc:2 * cc + 2], in_=lists_d.ap()[e_ * CAP + cc * 128: e_ * CAP + (cc + 1) * 128, :]),
                       wait=BAR() if cc == 0 else None, inc=sLD, incn=16)
                ldl = sLD.v
                for cc in range(NCAP):
                    pb = psB if cc % 2 == 0 else psB2
                    OP("gpsimd", lambda e, cc=cc: e.indirect_dma_start(
                        out=tokrows[:, :], out_offset=None,
                        in_=x1b_d.ap(), in_offset=IndirectOffsetOnAxis(smallu[:, 2 * cc:2 * cc + 1], 0)),
                        wait=[(sLD, ldl), (sTE, sTE.v)], inc=sGP, incn=16)
                    for c in range(8):
                        OP("tensor", lambda e, c=c, pb=pb: e.transpose(pb[:, c * 128:(c + 1) * 128], tokrows[:, c * 128:(c + 1) * 128], IDB),
                           wait=[(sGP, sGP.v), (sVE, sVE.v)] if c == 0 else None, inc=sTE if c == 7 else None)
                    for c in range(8):
                        OP("vector", lambda e, c=c, cc=cc, pb=pb: e.tensor_copy(out=tokT[:, c * CAP + cc * 128: c * CAP + (cc + 1) * 128], in_=pb[:, c * 128:(c + 1) * 128]),
                           wait=[(sTE, sTE.v)] if c == 0 else None, inc=sVE if c == 7 else None)
                if e_ == 0:
                    dump("tokt0", tokT[:, 0:CAP])
                srcW1 = wffn_d.ap()[e_ * NWE: e_ * NWE + D * HID].rearrange("(r h) -> r h", h=HID)
                srcW2 = wffn_d.ap()[e_ * NWE + D * HID: e_ * NWE + NWE].rearrange("(h d) -> h d", d=D)
                bufs = [(ws1sl, w2sl), (wstB, w2tB)]
                ld_after = {}
                te_after = {}
                KK = 5 * 8

                def emit_load(k, srcW1=srcW1, srcW2=srcW2, ld_after=ld_after, te_after=te_after, bufs=bufs):
                    hcc = k % 8
                    bw_, b2_ = bufs[k % 2]
                    w = [(sTE, te_after[k - 2])] if k >= 2 else [(sTE, sTE.v)]
                    for c in range(8):
                        OP("sync", lambda e, c=c, hcc=hcc, bw_=bw_, srcW1=srcW1: e.dma_start(
                            out=bw_[:, c * 512:(c + 1) * 512], in_=srcW1[c * 128:(c + 1) * 128, hcc * 512:(hcc + 1) * 512]),
                           wait=w if c == 0 else None, inc=sLD, incn=16)
                    OP("sync", lambda e, hcc=hcc, b2_=b2_, srcW2=srcW2: e.dma_start(
                        out=b2_[:, :].rearrange("p (s d) -> p s d", s=4), in_=srcW2[hcc * 512:(hcc + 1) * 512, :].rearrange("(s p) d -> p s d", p=128)),
                       inc=sLD, incn=16)
                    ld_after[k] = sLD.v

                emit_load(0)
                emit_load(1)
                for k in range(KK):
                    sc, hcc = divmod(k, 8)
                    sl0 = sc * 256
                    bw_, b2_ = bufs[k % 2]
                    if hcc == 0:
                        for sub in range(2):
                            for dn in range(2):
                                OP("tensor", lambda e, e_=e_, sub=sub, dn=dn: e.matmul(
                                    psF[:, 512 + sub * 1024 + dn * 512: 512 + sub * 1024 + (dn + 1) * 512],
                                    onesrb_s[0:1, :], b2e_b[0:1, dn * 512:(dn + 1) * 512],
                                    start=True, stop=False, skip_group_check=True),
                                   wait=[(sVE, sVE.v), (sGP, b2ld)] if (sub == 0 and dn == 0) else None)
                    for hp in range(2):
                        for hsub in (2 * hp, 2 * hp + 1):
                            po = (hsub % 2) * 256
                            for c in range(8):
                                OP("tensor", lambda e, c=c, hsub=hsub, sl0=sl0, bw_=bw_, po=po: e.matmul(
                                    psF[:, po:po + 256],
                                    bw_[:, c * 512 + hsub * 128: c * 512 + hsub * 128 + 128],
                                    tokT[:, c * CAP + sl0: c * CAP + sl0 + 256],
                                    start=(c == 0), stop=(c == 7)),
                                   wait=([(sLD, ld_after[k]), (sVE, sVE.v)] if c == 0 else None) if hsub == 0 else ([(sVE, sVE.v)] if c == 0 else None),
                                   inc=sTE if c == 7 else None)
                        blk = e_ * 32 + hcc * 4 + 2 * hp
                        OP("vector", lambda e, blk=blk: e.tensor_tensor(
                            xbE[:, :].rearrange("p (u t) -> p u t", u=2), psF[:, 0:512].rearrange("p (u t) -> p u t", u=2),
                            b1f_s[:, blk:blk + 2].unsqueeze(2).to_broadcast([P, 2, 256]), ALU.add),
                           wait=[(sTE, sTE.v)], inc=sVE)
                        VGAP()
                        OP("scalar", lambda e: e.activation(out=gscrE[:, :], in_=xbE[:, :], func=AF.Square, scale=cst_s[:, 2:3]),
                           wait=[(sVE, sVE.v)], inc=sSC)
                        OP("vector", lambda e: e.scalar_tensor_tensor(
                            out=gscrE[:, :], in0=gscrE[:, :], scalar=1.0, in1=xbE[:, :], op0=ALU.add, op1=ALU.mult),
                           wait=[(sSC, sSC.v)], inc=sVE)
                        OP("scalar", lambda e: e.activation(out=small[:, 0:512], in_=gscrE[:, :], func=AF.Sigmoid, scale=cst_s[:, 3:4]),
                           wait=[(sVE, sVE.v)], inc=sSC)
                        OP("vector", lambda e, hp=hp: e.tensor_tensor(hsbE[:, hp * 512:(hp + 1) * 512], small[:, 0:512], xbE[:, :], ALU.mult),
                           wait=[(sSC, sSC.v)], inc=sVE)
                        for hsub in (2 * hp, 2 * hp + 1):
                            for sub in range(2):
                                for dn in range(2):
                                    OP("tensor", lambda e, sub=sub, dn=dn, hsub=hsub, hcc=hcc, b2_=b2_: e.matmul(
                                        psF[:, 512 + sub * 1024 + dn * 512: 512 + sub * 1024 + (dn + 1) * 512],
                                        hsbE[:, hsub * 256 + sub * 128: hsub * 256 + sub * 128 + 128],
                                        b2_[:, hsub * D + dn * 512: hsub * D + (dn + 1) * 512],
                                        start=False, stop=(hcc == 7 and hsub == 3), skip_group_check=True),
                                       wait=[(sVE, sVE.v)] if (hsub % 2 == 0 and sub == 0 and dn == 0) else None,
                                       inc=sTE if (hsub == 3 and sub == 1 and dn == 1) else None)
                    te_after[k] = sTE.v
                    if hcc == 7:
                        if e_ == 0 and sc == 0:
                            dump("hsb0", hsbE[:, :])
                        for sub in range(2):
                            call = sc * 2 + sub
                            gbuf = grows if sub == 0 else grows2
                            OP("vector", lambda e, sub=sub, call=call, gbuf=gbuf: e.tensor_scalar(
                                gbuf[:, :], psF[:, 512 + sub * 1024: 512 + (sub + 1) * 1024],
                                smallu[:, 2 * call + 1: 2 * call + 2].bitcast(F32), None, ALU.mult),
                               wait=[(sTE, sTE.v), (sGP, scatter_prev2)] if sub == 0 else [(sGP, scatter_prev2)], inc=sVE)
                            OP("gpsimd", lambda e, call=call, gbuf=gbuf: e.indirect_dma_start(
                                out=routed_d.ap(), out_offset=IndirectOffsetOnAxis(smallu[:, 2 * call:2 * call + 1], 0),
                                in_=gbuf[:, :], in_offset=None, compute_op=ALU.add),
                                wait=[(sVE, sVE.v), (sGP, scatter_prev)], inc=sGP, incn=16)
                        scatter_prev2 = sGP.v
                    if k + 2 < KK:
                        emit_load(k + 2)
                scatter_prev = sGP.v

        def final():
            dump("ws1d", wffn_d.ap()[WS1B:WS1B + 128 * HID].rearrange("(p n) -> p n", p=128))
            dump("ws2d", wffn_d.ap()[WS2B:WS2B + 128 * D].rearrange("(p n) -> p n", p=128))
            dump("w1e0", wffn_d.ap()[0:128 * HID].rearrange("(p n) -> p n", p=128))
            dump("probsT0", probsT_d.ap()[:, :])
            dump("agd", ag_d.ap()[:, :])
            dump("loB", loB_s[:, :])
            dump("lists0", lists_d.ap()[0:10240, :])
            dump("x1ball", x1b_d.ap()[0:8192, :])
            dump("routed0", routed_d.ap()[0:256, :])
            for tl in range(NB * NT):
                t0 = tl * 128
                OP("sync", lambda e, t0=t0: e.dma_start(out=ust[:, :], in_=routed_d.ap()[t0:t0 + 128, :]),
                   wait=BAR() if tl == 0 else [(sVE, sVE.v)], inc=sLD, incn=16)
                OP("sync", lambda e, t0=t0: e.dma_start(out=x0st[:, :], in_=x1f_d.ap()[t0:t0 + 128, :]), inc=sLD, incn=16)
                OP("vector", lambda e: e.tensor_add(scrap[:, :], ust[:, :], x0st[:, :]), wait=[(sLD, sLD.v)], inc=sVE)
                VGAP()
                OP("vector", lambda e: e.tensor_reduce(small[:, 600:601], scrap[:, :], AX.X, ALU.add), inc=sVE)
                VGAP()
                OP("vector", lambda e: e.tensor_scalar(small[:, 601:602], small[:, 600:601], 1.0 / D, None, ALU.mult), inc=sVE)
                VGAP()
                OP("vector", lambda e: e.tensor_scalar(x0st[:, :], scrap[:, :], small[:, 601:602], None, ALU.subtract), inc=sVE)
                OP("scalar", lambda e: e.activation(out=scrap[:, :], in_=x0st[:, :], func=AF.Square, accum_out=small[:, 602:603]),
                   wait=[(sVE, sVE.v)], inc=sSC)
                OP("vector", lambda e: e.tensor_scalar(small[:, 603:604], small[:, 602:603], 1.0 / D, None, ALU.mult),
                   wait=[(sSC, sSC.v)], inc=sVE)
                OP("scalar", lambda e: e.activation(out=small[:, 604:605], in_=small[:, 603:604], func=AF.Sqrt, bias=cst_s[:, 1:2]),
                   wait=[(sVE, sVE.v)], inc=sSC)
                OP("vector", lambda e: e.reciprocal(out=small[:, 605:606], in_=small[:, 604:605]), wait=[(sSC, sSC.v)], inc=sVE)
                VGAP()
                OP("vector", lambda e: e.tensor_scalar(scrap[:, :], x0st[:, :], small[:, 605:606], None, ALU.mult), inc=sVE)
                VGAP()
                OP("sync", lambda e, tl=tl, t0=t0: e.dma_start(
                    out=y_out.ap()[tl // NT, (t0 % T):(t0 % T) + 128, :], in_=scrap[:, :]),
                   wait=[(sVE, sVE.v)], inc=sST, incn=16)
            OP("sync", lambda e: e.dma_start(out=probsT_d.ap()[0:1, 0:1], in_=probsT_d.ap()[0:1, 0:1]),
               wait=[(sST, sST.v)], inc=sLD, incn=16)

        import os as _os
        KS = int(_os.environ.get("KSTAGE", "99"))
        if KS >= 2:
            phaseA(0)
        if KS >= 3:
            gate_batch(0)
            shared_batch(0)
        if KS >= 4:
            phaseA(1)
            gate_batch(1)
        if KS >= 5:
            bisection()
            shared_batch(1)
        if KS >= 6:
            compaction()
        if KS >= 7:
            experts()
        if KS >= 8:
            final()

        pr.run(en(nc.Block()))

    return nc, dbg_outs


def stage_inputs(inputs, core):
    c = host_constants()
    m = {}
    m.update(c)
    x = inputs["L0_attn_x"]
    m["x"] = np.ascontiguousarray(np.asarray(x)[2 * core:2 * core + 2]).astype(np.float32, copy=False)
    m["Wi"] = np.asarray(inputs["L0_attn_Wi"], dtype=np.float32)
    m["Wp"] = np.asarray(inputs["L0_attn_Wp"], dtype=np.float32)
    m["Wo2"] = np.ascontiguousarray(np.asarray(inputs["L0_attn_Wo"], dtype=np.float32)[0])
    pal = np.asarray(inputs["L0_attn_palette"], dtype=np.float32)
    m["palT"] = np.ascontiguousarray(pal.transpose(1, 2, 0).reshape(256, D))
    w1 = np.asarray(inputs["L0_attn_W1"], dtype=np.float32)
    m["W1full"] = np.ascontiguousarray(w1)
    m["b1a"] = np.ascontiguousarray(np.asarray(inputs["L0_attn_b1"], dtype=np.float32).reshape(64, 1))
    m["W2a"] = np.asarray(inputs["L0_attn_W2"], dtype=np.float32)
    m["b2a"] = np.ascontiguousarray(np.asarray(inputs["L0_attn_b2"], dtype=np.float32).reshape(64, 1))
    m["Wc"] = np.asarray(inputs["L0_attn_Wc"], dtype=np.float32)
    m["bc_row"] = np.ascontiguousarray(np.asarray(inputs["L0_attn_bc"], dtype=np.float32).reshape(1, 2))
    m["Wm"] = np.asarray(inputs["L0_attn_Wm"], dtype=np.float32)
    m["bm_row"] = np.ascontiguousarray(np.asarray(inputs["L0_attn_bm"], dtype=np.float32).reshape(1, 1))
    m["gate"] = np.asarray(inputs["L0_ffn_gate"], dtype=np.float32)
    m["W1f"] = np.asarray(inputs["L0_ffn_W1"], dtype=np.float32)
    m["W2f"] = np.asarray(inputs["L0_ffn_W2"], dtype=np.float32)
    b1f = np.asarray(inputs["L0_ffn_b1"], dtype=np.float32)
    m["b1f_cols"] = np.ascontiguousarray(b1f.reshape(E, 32, 128).transpose(2, 0, 1).reshape(128, E * 32))
    m["b2f_row"] = np.ascontiguousarray(np.asarray(inputs["L0_ffn_b2"], dtype=np.float32).reshape(1, E * D))
    m["Ws1"] = np.asarray(inputs["L0_ffn_Ws1"], dtype=np.float32)
    m["Ws2"] = np.asarray(inputs["L0_ffn_Ws2"], dtype=np.float32)
    bs1 = np.asarray(inputs["L0_ffn_bs1"], dtype=np.float32)
    m["bs1_cols"] = np.ascontiguousarray(bs1.reshape(32, 128).T)
    m["bs2_row"] = np.ascontiguousarray(np.asarray(inputs["L0_ffn_bs2"], dtype=np.float32).reshape(1, D))
    return m


def kernel(**inputs):
    nc, _ = build_program()
    in_maps = [stage_inputs(inputs, c) for c in range(8)]
    res = run_bass_kernel_spmd(nc, in_maps, core_ids=list(range(8)))
    out = np.concatenate([res.results[c]["out"] for c in range(8)], axis=0)
    return out.astype(np.float32)
